# version 08
# baseline (speedup 1.0000x reference)
import numpy as np
import ml_dtypes
import concourse.bass as bass
import concourse.mybir as mybir
from concourse.bass_utils import run_bass_kernel_spmd

F32 = mybir.dt.float32
BF16 = mybir.dt.bfloat16
AF = mybir.ActivationFunctionType
ALU = mybir.AluOpType
AX = mybir.AxisListType

D = 1024
NDK = 8
H_RAW = 2730
HPAD = 2816
NHK = 22
NEXP = 8
LN_EPS = 1e-5
RMS_EPS = float(np.finfo(np.float32).eps)
LAMBDA_INIT = 0.8 - 0.6 * float(np.exp(-0.3 * 1))
NHEAD = 8
HD = 64
VD = 128
VDE = 129
PBF_COLS = 65536
PF_COLS = 18944

ENGINES = ("sync", "tensor", "scalar", "vector", "gpsimd")


class Op:
    __slots__ = ("fn", "waits", "inc", "tag")

    def __init__(self, fn, waits, inc, tag):
        self.fn = fn
        self.waits = waits
        self.inc = inc
        self.tag = tag


class MK:

    def __init__(self, T=2048, n_sub=4):
        assert T % 256 == 0
        self.T = T
        self.NLC = T // 128
        self.LT = min(512, T)
        self.NLT = T // self.LT
        self.n_sub = n_sub
        self.ops = {e: [] for e in ENGINES}
        self.nbar = 0

    def op(self, eng, fn, waits=(), inc=None, tag=""):
        w = list(waits)
        pend = getattr(self, "_pending_first_wait", None)
        if pend and eng in pend and fn is not None:
            w = pend.pop(eng) + w
        self.ops[eng].append(Op(fn, w, inc, tag))

    def barrier(self):
        i = self.nbar
        self.nbar += 1
        target = 5 * (i + 1)
        extra = list(self.xspill_keys.get(0, ())) + list(getattr(self, "nspill_keys", []))
        for e in ENGINES:
            self.op(e, None, inc=("bar", f"bar{i}_{e}", 1), tag=f"barrier{i}i")
            self.op(e, None, waits=[("bar", ("BARTARGET", target))] + extra,
                    tag=f"barrier{i}w")

    def build(self):
        T, NLC, LT, NLT = self.T, self.NLC, self.LT, self.NLT
        nc = bass.Bass("TRN2", target_bir_lowering=False)
        self.nc = nc

        dt = {}
        dt["x"] = nc.dram_tensor("x", [T, D], F32, kind="ExternalInput").ap()
        for w in ("aWq", "aWk", "aWv", "aWo"):
            dt[w] = nc.dram_tensor(w, [D, D], F32, kind="ExternalInput").ap()
        for v in ("bo_b", "gamma_b", "beta_b"):
            dt[v] = nc.dram_tensor(v, [128, D], F32, kind="ExternalInput").ap()
        for w in ("bWq", "bWk", "bWv", "bWp"):
            dt[w] = nc.dram_tensor(w, [D, D], F32, kind="ExternalInput").ap()
        dt["lam"] = nc.dram_tensor("lam", [4, 32], F32, kind="ExternalInput").ap()
        dt["Wr"] = nc.dram_tensor("Wr", [D, NEXP], F32, kind="ExternalInput").ap()
        dt["br_b"] = nc.dram_tensor("br_b", [128, NEXP], F32, kind="ExternalInput").ap()
        dt["W1"] = nc.dram_tensor("W1", [NEXP, D, HPAD], F32, kind="ExternalInput").ap()
        dt["W2"] = nc.dram_tensor("W2", [NEXP, D, HPAD], F32, kind="ExternalInput").ap()
        dt["W3"] = nc.dram_tensor("W3", [NEXP, HPAD, D], F32, kind="ExternalInput").ap()
        dt["b12"] = nc.dram_tensor("b12", [128, 2 * NEXP * NHK], F32, kind="ExternalInput").ap()
        dt["b3"] = nc.dram_tensor("b3", [NEXP, D], F32, kind="ExternalInput").ap()
        dt["cs_tab"] = nc.dram_tensor("cs_tab", [T, 1024], F32, kind="ExternalInput").ap()
        dt["masks"] = nc.dram_tensor("masks", [4 * 128, 512], BF16, kind="ExternalInput").ap()
        dt["y"] = nc.dram_tensor("y", [T, D], F32, kind="ExternalOutput").ap()
        dt["xspill"] = nc.dram_tensor("xspill", [T, D], F32, kind="Internal").ap()
        dt["nspill"] = nc.dram_tensor("nspill", [T, D], BF16, kind="Internal").ap()
        dt["qspill"] = nc.dram_tensor("qspill", [T, D], BF16, kind="Internal").ap()
        dt["kspill"] = nc.dram_tensor("kspill", [T, D], BF16, kind="Internal").ap()
        dt["yspill"] = nc.dram_tensor("yspill", [T, D], BF16, kind="Internal").ap()
        dt["gspill"] = nc.dram_tensor("gspill", [T, 128], BF16, kind="Internal").ap()
        for w in ("bWq", "bWk", "bWv", "bWp"):
            dt[w + "s"] = nc.dram_tensor(w + "s", [D, D], BF16, kind="Internal").ap()
        self.dt = dt

        self.pbf = nc.alloc_sbuf_tensor("pbf", [128, PBF_COLS], BF16)
        self.pf = nc.alloc_sbuf_tensor("pf", [128, PF_COLS], F32)
        self.ones_bf = nc.alloc_sbuf_tensor("ones_bf", [128, 8], BF16)
        self.ones_f = nc.alloc_sbuf_tensor("ones_f", [1, 128], F32)
        self.means = nc.alloc_sbuf_tensor("means", [128, self.NLC], F32)
        self.rstds = nc.alloc_sbuf_tensor("rstds", [128, self.NLC], F32)
        self.stat = nc.alloc_sbuf_tensor("stat", [128, 64], F32)
        self.b12_sb = nc.alloc_sbuf_tensor("b12_sb", [128, 2 * NEXP * NHK], F32)
        self.br_sb = nc.alloc_sbuf_tensor("br_sb", [128, NEXP], F32)
        self.lam_sb = [nc.alloc_sbuf_tensor(f"lam{i}", [1, 32], F32) for i in range(4)]
        self.lam_t = nc.alloc_sbuf_tensor("lam_t", [1, 32], F32)
        self.lam_col = nc.alloc_sbuf_tensor("lam_col", [128, 1], F32)
        self.epst = nc.alloc_sbuf_tensor("epst", [128, 4], F32)

        self.P = [nc.alloc_psum_tensor(f"P{i}", [128, 1024], F32) for i in range(4)]

        self.sems = {}

        for w in ("bWq", "bWk", "bWv", "bWp"):
            self.op("gpsimd", lambda e, w=w: e.dma_start(
                out=dt[w + "s"][:], in_=dt[w][:]),
                inc=("wbcpre" + w, ("precast", w), 16))
        self.precast_keys = {w: ("wbcpre" + w, ("precast", w))
                             for w in ("bWq", "bWk", "bWv", "bWp")}

        def init_eps(e):
            e.memset(self.epst[:, 0:1], LN_EPS)
            e.memset(self.epst[:, 1:2], RMS_EPS)
            e.memset(self.epst[:, 2:3], float(np.log(0.125)))
            return e.memset(self.epst[:, 3:4], 0.0)
        self.op("vector", init_eps, inc=("dve", ("init",), 1))
        self._pending_first_wait = {"scalar": [("dve", ("init",))]}

        self.xspill_keys = {}
        self.nspill_keys = []
        self.prev_fused = False
        src = dt["x"]
        for sub in range(self.n_sub):
            kind = ("attn_a", "ffn", "attn_b", "ffn")[sub]
            last = (sub == self.n_sub - 1)
            dst = dt["y"] if last else dt["xspill"]
            if kind == "attn_a":
                self.phase_attn_a(src, dst)
            elif kind == "ffn":
                self.phase_ffn(src, dst)
            else:
                self.phase_attn_b(src, dst)
            src = dt["xspill"]
            self.barrier()

        self.op("sync", None, waits=[("xw0", ("FINALSEM", "xw0")), ("xw1", ("FINALSEM", "xw1"))],
                tag="final_store_wait")

        self.finalize()
        return nc

    def finalize(self):
        counters = {}
        keyval = {}
        for e in ENGINES:
            for o in self.ops[e]:
                if o.inc is not None:
                    s, key, n = o.inc
                    counters[s] = counters.get(s, 0) + n
                    assert key not in keyval, f"dup key {key}"
                    keyval[key] = (s, counters[s])

        def resolve(w):
            s, key = w
            if isinstance(key, tuple) and key and key[0] == "BARTARGET":
                return s, key[1]
            if isinstance(key, tuple) and key and key[0] == "FINALSEM":
                return s, counters.get(s, 0)
            assert key in keyval, f"unresolved wait key {key}"
            ks, v = keyval[key]
            assert ks == s, f"wait {key} on sem {s} but inc was on {ks}"
            return s, v

        nc = self.nc
        CHAINED = ("vector", "scalar", "gpsimd")
        for e in CHAINED:
            counters.setdefault("C" + e, 0)
        for s in counters:
            self.sems[s] = nc.semaphore(s).__enter__()

        _EMIT = {
            "tensor_scalar", "tensor_tensor", "tensor_copy", "tensor_add",
            "tensor_sub", "tensor_mul", "tensor_max", "reduce_sum",
            "reduce_max", "tensor_reduce", "reciprocal", "memset", "memzero",
            "scalar_tensor_tensor", "tensor_single_scalar", "select",
            "copy_predicated", "bn_stats", "bn_aggr", "activation", "copy",
            "mul", "add", "square", "sqrt", "sign", "iota", "affine_select",
            "tensor_scalar_mul", "tensor_scalar_add", "tensor_relu",
        }

        class _Chained:
            def __init__(self, eng, sem, holder):
                self._e = eng
                self._sem = sem
                self._h = holder

            def __getattr__(self, name):
                f = getattr(self._e, name)
                if name == "dma_start":
                    def g(*a, **k):
                        if self._h[0]:
                            self._e.wait_ge(self._sem, self._h[0])
                        return f(*a, **k)
                    return g
                if name in _EMIT:
                    def g(*a, **k):
                        if self._h[0]:
                            self._e.wait_ge(self._sem, self._h[0])
                        inst = f(*a, **k)
                        inst.then_inc(self._sem, 1)
                        self._h[0] += 1
                        return inst
                    return g
                return f

        with nc.Block() as block:
            for e in ENGINES:
                ops = self.ops[e]
                chained = e in CHAINED

                def body(eng, ops=ops, chained=chained, e=e):
                    holder = [0]
                    proxy = _Chained(eng, self.sems["C" + e], holder) if chained else eng
                    for o in ops:
                        for w in o.waits:
                            s, v = resolve(w)
                            eng.wait_ge(self.sems[s], v)
                        h0 = holder[0]
                        inst = o.fn(proxy) if o.fn is not None else None
                        if o.inc is not None:
                            s, key, n = o.inc
                            if chained and inst is not None and holder[0] > h0:
                                eng.wait_ge(self.sems["C" + e], holder[0])
                                eng.sem_inc(self.sems[s], n)
                            else:
                                if inst is None:
                                    inst = eng.nop()
                                inst.then_inc(self.sems[s], n)

                getattr(block, e)(body)

    def snt(self, src, ph, xnt, xc, sq, xnb, xc_views=None, pre_normed=False):
        T, NLC = self.T, self.NLC
        dt, st = self.dt, self.stat
        if pre_normed:
            for dk in range(NDK):
                self.op("sync", lambda e, dk=dk: e.dma_start(
                    out=xnt[:, dk * T:(dk + 1) * T],
                    in_=self.dt["nspill"][:, dk * 128:(dk + 1) * 128], transpose=True),
                    waits=list(self.nspill_keys) if dk == 0 else [],
                    inc=("tl", (ph, "xnt", dk), 16), tag=f"snt tl{dk}")
            return
        if xc_views is not None:
            xcv = xc_views
        else:
            xcv = [xc[lc % 2] for lc in range(NLC)]
        for lc in range(NLC):
            b = lc % 2
            if xc_views is None:
                w = [("dve", (ph, "xnb", lc - 2)), ("act", (ph, "ssq", lc - 2))] if lc >= 2 else []
            else:
                w = [(f"xc{b}", (ph, "xc", lc - 2))] if lc >= 2 else []
            if src is self.dt["xspill"]:
                w.extend(self.xspill_keys.get(lc, ()))
            self.op("sync", lambda e, lc=lc, src=src: e.dma_start(
                out=xcv[lc][:], in_=src[lc * 128:(lc + 1) * 128, :]),
                waits=w, inc=(f"xc{b}", (ph, "xc", lc), 16), tag=f"snt xc{lc}")
            self.op("vector", lambda e, lc=lc: e.reduce_sum(
                st[:, 0:1], xcv[lc][:], axis=AX.X),
                waits=[(f"xc{b}", (ph, "xc", lc))], inc=("dve", (ph, "s", lc), 1))
            self.op("scalar", lambda e, lc=lc: e.activation(
                sq[:], xcv[lc][:], AF.Square, accum_out=st[:, 1:2]),
                waits=[(f"xc{b}", (ph, "xc", lc))] + ([("dve", (ph, "var", lc - 1))] if lc >= 1 else []),
                inc=("act", (ph, "ssq", lc), 1))
            self.op("vector", lambda e, lc=lc: e.tensor_scalar(
                self.means[:, lc:lc + 1], st[:, 0:1], 1.0 / D, None, ALU.mult),
                waits=[("dve", (ph, "s", lc))])
            self.op("vector", lambda e: e.tensor_scalar(
                st[:, 2:3], st[:, 1:2], 1.0 / D, None, ALU.mult),
                waits=[("act", (ph, "ssq", lc))])
            self.op("vector", lambda e, lc=lc: e.tensor_tensor(
                st[:, 3:4], self.means[:, lc:lc + 1], self.means[:, lc:lc + 1], ALU.mult))
            self.op("vector", lambda e: e.tensor_tensor(
                st[:, 4:5], st[:, 2:3], st[:, 3:4], ALU.subtract),
                inc=("dve", (ph, "var", lc), 1))
            self.op("scalar", lambda e: e.activation(
                st[:, 5:6], st[:, 4:5], AF.Ln, bias=self.epst[:, 0:1]),
                waits=[("dve", (ph, "var", lc))])
            self.op("scalar", lambda e, lc=lc: e.activation(
                self.rstds[:, lc:lc + 1], st[:, 5:6], AF.Exp, scale=-0.5),
                inc=("act", (ph, "rstd", lc), 1))
            self.op("vector", lambda e, lc=lc, b=b: e.tensor_scalar(
                xnb[b][:], xcv[lc][:], self.means[:, lc:lc + 1],
                self.rstds[:, lc:lc + 1], ALU.subtract, ALU.mult),
                waits=[("act", (ph, "rstd", lc))] + ([(f"nsp{b}", (ph, "nsp", lc - 2))] if lc >= 2 else []),
                inc=("dve", (ph, "xnb", lc), 1))
            self.op("gpsimd", lambda e, lc=lc, b=b: e.dma_start(
                out=self.dt["nspill"][lc * 128:(lc + 1) * 128, :], in_=xnb[b][:]),
                waits=[("dve", (ph, "xnb", lc))], inc=(f"nsp{b}", (ph, "nsp", lc), 16))
        for dk in range(NDK):
            self.op("sync", lambda e, dk=dk: e.dma_start(
                out=xnt[:, dk * T:(dk + 1) * T],
                in_=self.dt["nspill"][:, dk * 128:(dk + 1) * 128], transpose=True),
                waits=[("nsp0", (ph, "nsp", NLC - 2)), ("nsp1", (ph, "nsp", NLC - 1))],
                inc=("tl", (ph, "xnt", dk), 16), tag=f"snt tl{dk}")

    def w_xnt(self, ph):
        return [("tl", (ph, "xnt", NDK - 1))]

    def ep_ln(self, ph, lc, xv, xnb, sq, ready_key, dst):
        if dst is not self.dt["xspill"]:
            return
        st = self.stat
        b = lc % 2
        self.op("vector", lambda e, xv=xv: e.reduce_sum(st[:, 32:33], xv[:], axis=AX.X))
        wsq = [ready_key]
        if lc >= 1:
            wsq.append(("dve", (ph, "fvar", lc - 1)))
        self.op("scalar", lambda e, xv=xv, sq=sq: e.activation(
            sq[:], xv[:], AF.Square, accum_out=st[:, 33:34]),
            waits=wsq, inc=("act", (ph, "fssq", lc), 1))
        def fstats(e, lc=lc):
            e.tensor_scalar(self.means[:, lc:lc + 1], st[:, 32:33], 1.0 / D, None, ALU.mult)
            e.tensor_scalar(st[:, 34:35], st[:, 33:34], 1.0 / D, None, ALU.mult)
            e.tensor_tensor(st[:, 35:36], self.means[:, lc:lc + 1],
                            self.means[:, lc:lc + 1], ALU.mult)
            return e.tensor_tensor(st[:, 36:37], st[:, 34:35], st[:, 35:36], ALU.subtract)
        self.op("vector", fstats, waits=[("act", (ph, "fssq", lc))],
                inc=("dve", (ph, "fvar", lc), 1))
        def frstd(e, lc=lc):
            e.activation(st[:, 37:38], st[:, 36:37], AF.Ln, bias=self.epst[:, 0:1])
            return e.activation(self.rstds[:, lc:lc + 1], st[:, 37:38], AF.Exp, scale=-0.5)
        wr = [("dve", (ph, "fvar", lc))]
        if lc >= 1:
            wr.append(("dve", (ph, "fxn", lc - 1)))
        self.op("scalar", frstd, waits=wr, inc=("act", (ph, "frstd", lc), 1))
        wx = [("act", (ph, "frstd", lc))]
        if lc >= 2:
            wx.append((f"nsp{b}", (ph, "fnsp", lc - 2)))
        self.op("vector", lambda e, lc=lc, b=b, xv=xv: e.tensor_scalar(
            xnb[b][:], xv[:], self.means[:, lc:lc + 1],
            self.rstds[:, lc:lc + 1], ALU.subtract, ALU.mult),
            waits=wx, inc=("dve", (ph, "fxn", lc), 1))
        self.op("gpsimd", lambda e, lc=lc, b=b: e.dma_start(
            out=self.dt["nspill"][lc * 128:(lc + 1) * 128, :], in_=xnb[b][:]),
            waits=[("dve", (ph, "fxn", lc))], inc=(f"nsp{b}", (ph, "fnsp", lc), 16))
        if lc == self.NLC - 1:
            self.nspill_keys = [("nsp0", (ph, "fnsp", self.NLC - 2)),
                                ("nsp1", (ph, "fnsp", self.NLC - 1))]

    def set_xspill_totals(self, ph, dst):
        if dst is self.dt["xspill"]:
            NLC = self.NLC
            tot = [(f"xw{(NLC - 2) % 2}", (ph, "xnew", NLC - 2)),
                   (f"xw{(NLC - 1) % 2}", (ph, "xnew", NLC - 1))]
            self.xspill_keys = {lc: tot for lc in range(NLC)}

    def phase_attn_a(self, src, dst):
        T, NLC, LT, NLT = self.T, self.NLC, self.LT, self.NLT
        ph = f"A{self.nbar}"
        nc, dt, P = self.nc, self.dt, self.P
        st = self.stat

        o = 0
        def carve(n):
            nonlocal o
            v = self.pbf[:, o:o + n]
            o += n
            return v
        xnt = carve(NDK * T)
        v_sb = carve(NDK * T)
        eq_sb = v_sb
        ek_sb = carve(NDK * T)
        at_sb = ek_sb
        wW = carve(NDK * 1024)
        ctx_sb = carve(NDK * 1024)
        xnb = [ctx_sb[:, 0:1024], ctx_sb[:, 1024:2048]]
        assert o <= PBF_COLS, o

        of = 0
        def carvef(n):
            nonlocal of
            v = self.pf[:, of:of + n]
            of += n
            return v
        xc = [carvef(D), carvef(D)]
        sq = carvef(D)
        rzk = carvef(NEXP)
        zql = carvef(NLC)
        t2 = [carvef(D), carvef(D)]
        zbuf = [carvef(D), carvef(D)]
        ubuf = carvef(D)
        wbuf = carvef(D)
        v2buf = carvef(D)
        xc2 = [carvef(D), carvef(D)]
        xnew = [carvef(D), carvef(D)]
        bo_b = carvef(D)
        gamma_b = carvef(D)
        beta_b = carvef(D)
        assert of <= PF_COLS, of

        self.op("vector", lambda e: e.memset(self.ones_bf[:], 1.0))
        self.op("vector", lambda e: e.memset(self.ones_f[:], 1.0),
                inc=("dve", (ph, "ones"), 1))
        for name, buf in (("bo_b", bo_b), ("gamma_b", gamma_b), ("beta_b", beta_b)):
            self.op("sync", lambda e, name=name, buf=buf: e.dma_start(
                out=buf[:], in_=dt[name][:]), inc=("vec", (ph, name), 16))

        self.snt(src, ph, xnt, xc, sq, xnb)

        def wcast(wname, key, waits=()):
            self.op("gpsimd", lambda e, wname=wname: e.dma_start(
                out=wW.rearrange("p (c e) -> p c e", c=NDK),
                in_=dt[wname].rearrange("(c p) e -> p c e", p=128)),
                waits=list(waits), inc=("wcast", (ph, key), 16), tag=f"wcast {key}")
        wcast("aWv", "wv")

        def bank4(i):
            t = P[(i % 4) // 2]
            return t[:, (i % 2) * 512:(i % 2) * 512 + 512]
        def bank2(i):
            return P[2][:, (i % 2) * 512:(i % 2) * 512 + 512]
        zk_ps = P[3][:, 512:513]
        zq_ps = P[3][:, 0:1]

        nvt = NLC * 2
        for i in range(nvt):
            lc, et = divmod(i, 2)
            def mmv(e, lc=lc, et=et, i=i):
                ps = bank4(i)
                for dk in range(NDK):
                    inst = e.matmul(ps[:], xnt[:, dk * T + lc * 128: dk * T + lc * 128 + 128],
                                    wW[:, dk * 1024 + et * 512: dk * 1024 + et * 512 + 512],
                                    start=(dk == 0), stop=(dk == NDK - 1))
                return inst
            w = []
            if i == 0:
                w = self.w_xnt(ph) + [("wcast", (ph, "wv"))]
            if i >= 4:
                w.append(("act", (ph, "vev", i - 4)))
            self.op("tensor", mmv, waits=w, inc=("pe", (ph, "vmm", i), 1))
            self.op("scalar", lambda e, lc=lc, et=et, i=i: e.activation(
                v_sb[:, lc * 1024 + et * 512: lc * 1024 + et * 512 + 512],
                bank4(i)[:], AF.Copy),
                waits=[("pe", (ph, "vmm", i))], inc=("act", (ph, "vev", i), 1))
        wcast("aWk", "wk", waits=[("pe", (ph, "vmm", nvt - 1))])

        for i in range(nvt):
            lc, et = divmod(i, 2)
            def mmk(e, lc=lc, et=et, i=i):
                ps = bank4(i)
                for dk in range(NDK):
                    inst = e.matmul(ps[:], xnt[:, dk * T + lc * 128: dk * T + lc * 128 + 128],
                                    wW[:, dk * 1024 + et * 512: dk * 1024 + et * 512 + 512],
                                    start=(dk == 0), stop=(dk == NDK - 1))
                return inst
            w = [("wcast", (ph, "wk"))] if i == 0 else []
            if i >= 4:
                w.append(("act", (ph, "kev", i - 4)))
            elif i < 4:
                w.append(("act", (ph, "vev", nvt - 1)) if i == 0 else None)
            w = [x for x in w if x]
            self.op("tensor", mmk, waits=w, inc=("pe", (ph, "kmm", i), 1))
            self.op("scalar", lambda e, lc=lc, et=et, i=i: e.activation(
                ek_sb[:, lc * 1024 + et * 512: lc * 1024 + et * 512 + 512],
                bank4(i)[:], AF.Exp),
                waits=[("pe", (ph, "kmm", i))], inc=("act", (ph, "kev", i), 1))
        wcast("aWq", "wq", waits=[("pe", (ph, "kmm", nvt - 1))])

        for ek in range(NEXP):
            def mmz(e, ek=ek):
                for lc in range(NLC):
                    inst = e.matmul(zk_ps, ek_sb[:, lc * 1024 + ek * 128: lc * 1024 + ek * 128 + 128],
                                    self.ones_bf[:, 0:1], start=(lc == 0), stop=(lc == NLC - 1))
                return inst
            w = [("act", (ph, "kev", nvt - 1)), ("dve", (ph, "ones"))] if ek == 0 else [("dve", (ph, "zkev", ek - 1))]
            self.op("tensor", mmz, waits=w, inc=("pe", (ph, "zk", ek), 1))
            self.op("vector", lambda e, ek=ek: e.tensor_copy(rzk[:, ek:ek + 1], zk_ps),
                    waits=[("pe", (ph, "zk", ek))], inc=("dve", (ph, "zkev", ek), 1))
        self.op("vector", lambda e: e.reciprocal(rzk[:], rzk[:]),
                waits=[("dve", (ph, "zkev", NEXP - 1))], inc=("dve", (ph, "rzk"), 1))

        nct = NDK * 2
        for i in range(nct):
            dk, evt = divmod(i, 2)
            def mmc(e, dk=dk, evt=evt, i=i):
                ps = bank2(i)
                for lc in range(NLC):
                    inst = e.matmul(ps[:], ek_sb[:, lc * 1024 + dk * 128: lc * 1024 + dk * 128 + 128],
                                    v_sb[:, lc * 1024 + evt * 512: lc * 1024 + evt * 512 + 512],
                                    start=(lc == 0), stop=(lc == NLC - 1))
                return inst
            w = [("dve", (ph, "rzk"))] if i == 0 else []
            if i >= 2:
                w.append(("dve", (ph, "cev", i - 2)))
            self.op("tensor", mmc, waits=w, inc=("pe", (ph, "cmm", i), 1))
            self.op("vector", lambda e, dk=dk, evt=evt, i=i: e.tensor_scalar(
                ctx_sb[:, dk * 1024 + evt * 512: dk * 1024 + evt * 512 + 512],
                bank2(i)[:], rzk[:, dk:dk + 1], None, ALU.mult),
                waits=[("pe", (ph, "cmm", i))], inc=("dve", (ph, "cev", i), 1))

        nqt = NDK * NLT
        for i in range(nqt):
            ek, lt = divmod(i, NLT)
            def mmq(e, ek=ek, lt=lt, i=i):
                ps = bank4(i)
                for dk in range(NDK):
                    inst = e.matmul(ps[:, :LT], wW[:, dk * 1024 + ek * 128: dk * 1024 + ek * 128 + 128],
                                    xnt[:, dk * T + lt * LT: dk * T + lt * LT + LT],
                                    start=(dk == 0), stop=(dk == NDK - 1))
                return inst
            w = []
            if i == 0:
                w = [("wcast", (ph, "wq"))]
            if i >= 4:
                w.append(("act", (ph, "qev", i - 4)))
            self.op("tensor", mmq, waits=w, inc=("pe", (ph, "qmm", i), 1))
            wq_ = [("pe", (ph, "qmm", i))]
            if i == 0:
                wq_.append(("pe", (ph, "cmm", nct - 1)))
            self.op("scalar", lambda e, ek=ek, lt=lt, i=i: e.activation(
                eq_sb[:, ek * T + lt * LT: ek * T + lt * LT + LT],
                bank4(i)[:, :LT], AF.Exp),
                waits=wq_, inc=("act", (ph, "qev", i), 1))
        wcast("aWo", "wo", waits=[("pe", (ph, "qmm", nqt - 1))])

        for j in range(NLC):
            def mmzq(e, j=j):
                for ek in range(NEXP):
                    inst = e.matmul(zq_ps, eq_sb[:, ek * T + j * 128: ek * T + j * 128 + 128],
                                    self.ones_bf[:, 0:1], start=(ek == 0), stop=(ek == NEXP - 1))
                return inst
            w = [("dve", (ph, "zqev", j - 1))] if j > 0 else [("act", (ph, "qev", nqt - 1))]
            self.op("tensor", mmzq, waits=w, inc=("pe", (ph, "zq", j), 1))
            self.op("vector", lambda e, j=j: e.tensor_copy(zql[:, j:j + 1], zq_ps),
                    waits=[("pe", (ph, "zq", j))], inc=("dve", (ph, "zqev", j), 1))
        def zq_fin(e):
            e.reciprocal(zql[:], zql[:])
            return e.tensor_scalar(zql[:], zql[:], 1.0 / 32.0, None, ALU.mult)
        self.op("vector", zq_fin, waits=[("dve", (ph, "zqev", NLC - 1))],
                inc=("dve", (ph, "zql"), 1))

        nat = NDK * NLT
        for i in range(nat):
            evk, lt = divmod(i, NLT)
            def mma(e, evk=evk, lt=lt, i=i):
                ps = bank2(i)
                for dk in range(NDK):
                    inst = e.matmul(ps[:, :LT], ctx_sb[:, dk * 1024 + evk * 128: dk * 1024 + evk * 128 + 128],
                                    eq_sb[:, dk * T + lt * LT: dk * T + lt * LT + LT],
                                    start=(dk == 0), stop=(dk == NDK - 1))
                return inst
            w = []
            if i == 0:
                w = [("act", (ph, "qev", nqt - 1)), ("dve", (ph, "cev", nct - 1))]
            if i >= 2:
                w.append(("act", (ph, "aev", i - 2)))
            self.op("tensor", mma, waits=w, inc=("pe", (ph, "amm", i), 1))
            wa_ = [("pe", (ph, "amm", i))]
            self.op("scalar", lambda e, evk=evk, lt=lt, i=i: e.activation(
                at_sb[:, evk * T + lt * LT: evk * T + lt * LT + LT],
                bank2(i)[:, :LT], AF.Copy),
                waits=wa_, inc=("act", (ph, "aev", i), 1))

        for lc in range(NLC):
            par = lc % 2
            pb = P[par]
            eb = par
            woxc = [("dve", (ph, "eplg", lc - 2))] if lc >= 2 else []
            if src is self.dt["xspill"]:
                woxc.extend(self.xspill_keys.get(lc, ()))
            self.op("sync", lambda e, lc=lc, eb=eb: e.dma_start(
                out=xc2[eb][:], in_=src[lc * 128:(lc + 1) * 128, :]),
                waits=woxc, inc=(f"oxc{eb}", (ph, "oxc", lc), 16))
            def mmo(e, lc=lc, pb=pb):
                for dt_ in range(2):
                    for evk in range(NEXP):
                        inst = e.matmul(pb[:, dt_ * 512:dt_ * 512 + 512],
                                        at_sb[:, evk * T + lc * 128: evk * T + lc * 128 + 128],
                                        wW[:, evk * 1024 + dt_ * 512: evk * 1024 + dt_ * 512 + 512],
                                        start=(evk == 0), stop=(evk == NEXP - 1))
                return inst
            w = []
            if lc == 0:
                w.append(("act", (ph, "aev", nat - 1)))
                w.append(("wcast", (ph, "wo")))
            if lc >= 2:
                w.append(("dve", (ph, "eplg_t2", lc - 2)))
            self.op("tensor", mmo, waits=w, inc=("pe", (ph, "omm", lc), 1))
            def ep_t2(e, lc=lc, eb=eb, pb=pb):
                e.tensor_scalar(t2[eb][:], xc2[eb][:], self.means[:, lc:lc + 1],
                                self.rstds[:, lc:lc + 1], ALU.subtract, ALU.mult)
                return e.scalar_tensor_tensor(t2[eb][:], pb[:], zql[:, lc:lc + 1],
                                              t2[eb][:], ALU.mult, ALU.add)
            self.op("vector", ep_t2,
                    waits=[("pe", (ph, "omm", lc)), (f"oxc{eb}", (ph, "oxc", lc))] +
                          ([("dve", (ph, "zql"))] if lc == 0 else []),
                    inc=("dve", (ph, "eplg_t2", lc), 1))
            def ep_z(e, lc=lc, eb=eb):
                return e.tensor_tensor(zbuf[eb][:], t2[eb][:], bo_b[:], ALU.add)
            self.op("vector", ep_z, waits=[("vec", (ph, "beta_b"))] if lc == 0 else [],
                    inc=("dve", (ph, "z", lc), 1))
            self.op("vector", lambda e, eb=eb: e.reduce_sum(
                st[:, 8:9], zbuf[eb][:], axis=AX.X))
            wsq_ = [("dve", (ph, "z", lc))]
            if lc >= 1:
                wsq_.append(("dve", (ph, "var2", lc - 1)))
            self.op("scalar", lambda e, eb=eb: e.activation(
                sq[:], zbuf[eb][:], AF.Square, accum_out=st[:, 9:10]),
                waits=wsq_, inc=("act", (ph, "zsq", lc), 1))
            def ep_stats(e):
                e.tensor_scalar(st[:, 10:11], st[:, 8:9], 1.0 / D, None, ALU.mult)
                e.tensor_scalar(st[:, 11:12], st[:, 9:10], 1.0 / D, None, ALU.mult)
                e.tensor_tensor(st[:, 12:13], st[:, 10:11], st[:, 10:11], ALU.mult)
                return e.tensor_tensor(st[:, 13:14], st[:, 11:12], st[:, 12:13], ALU.subtract)
            self.op("vector", ep_stats, waits=[("act", (ph, "zsq", lc))],
                    inc=("dve", (ph, "var2", lc), 1))
            def ep_rstd(e):
                e.activation(st[:, 14:15], st[:, 13:14], AF.Ln, bias=self.epst[:, 0:1])
                return e.activation(st[:, 15:16], st[:, 14:15], AF.Exp, scale=-0.5)
            wr_ = [("dve", (ph, "var2", lc))]
            if lc >= 1:
                wr_.append(("dve", (ph, "eplg", lc - 1)))
            self.op("scalar", ep_rstd, waits=wr_, inc=("act", (ph, "rstd2", lc), 1))
            def ep_fin(e, lc=lc, eb=eb):
                e.tensor_scalar(ubuf[:], zbuf[eb][:], st[:, 10:11], st[:, 15:16],
                                ALU.subtract, ALU.mult)
                e.tensor_tensor(wbuf[:], ubuf[:], gamma_b[:], ALU.mult)
                e.tensor_tensor(v2buf[:], xc2[eb][:], beta_b[:], ALU.add)
                return e.tensor_tensor(xnew[eb][:], wbuf[:], v2buf[:], ALU.add)
            w2_ = [("act", (ph, "rstd2", lc))]

            if lc >= 2:
                w2_.append((f"xw{eb}", (ph, "xnew", lc - 2)))
                if dst is self.dt["xspill"]:
                    w2_.append(("act", (ph, "fssq", lc - 2)))
            self.op("vector", ep_fin, waits=w2_, inc=("dve", (ph, "eplg", lc), 1))
            self.op("gpsimd", lambda e, lc=lc, eb=eb, dst=dst: e.dma_start(
                out=dst[lc * 128:(lc + 1) * 128, :], in_=xnew[eb][:]),
                waits=[("dve", (ph, "eplg", lc))], inc=(f"xw{eb}", (ph, "xnew", lc), 16))
            self.ep_ln(ph, lc, xnew[eb], xnb, sq, ("dve", (ph, "eplg", lc)), dst)
        self.set_xspill_totals(ph, dst)
        self.prev_fused = dst is self.dt["xspill"]

    def phase_ffn(self, src, dst):
        T, NLC, LT, NLT = self.T, self.NLC, self.LT, self.NLT
        ph = f"F{self.nbar}"
        nc, dt, P = self.nc, self.dt, self.P
        st = self.stat

        slices = [(0, 4), (4, 8), (8, 12), (12, 16), (16, 20), (20, 22)]
        NS = len(slices)
        NSI = NEXP * NS

        o = 0
        def carve(n):
            nonlocal o
            v = self.pbf[:, o:o + n]
            o += n
            return v
        xnt = carve(NDK * T)
        wf = [carve(3 * 4 * 1024) for _ in range(2)]
        hbuf = [carve(4 * T) for _ in range(2)]
        gT = carve(T)
        b3bf = carve(D)
        wr_bf = carve(NDK * 8)
        gbf = carve(128)
        xnb_cols = carve(2 * 1024)
        xnb = [xnb_cols[:, 0:1024], xnb_cols[:, 1024:2048]]
        assert o <= PBF_COLS, o

        of = 0
        def carvef(n):
            nonlocal of
            v = self.pf[:, of:of + n]
            of += n
            return v
        acc = carvef(NLC * 1024)
        s_sc = [carvef(512), carvef(512)]
        sq = self.pf[:, of - 1024:of]
        logits = carvef(NLC * NEXP)
        gates = carvef(NLC * NEXP)
        mask1 = carvef(NLC * NEXP)
        l2 = carvef(NLC * NEXP)
        gm = carvef(8 * NLC)
        assert of <= PF_COLS, of
        m1 = gm[:, 0 * NLC:1 * NLC]
        m2 = gm[:, 1 * NLC:2 * NLC]
        ddt = gm[:, 2 * NLC:3 * NLC]
        e12 = gm[:, 3 * NLC:4 * NLC]
        w1 = gm[:, 4 * NLC:5 * NLC]
        w2 = gm[:, 5 * NLC:6 * NLC]
        w1d = gm[:, 6 * NLC:7 * NLC]

        self.op("sync", lambda e: e.dma_start(out=self.b12_sb[:], in_=dt["b12"][:]),
                inc=("vec", (ph, "b12"), 16))
        self.op("sync", lambda e: e.dma_start(out=self.br_sb[:], in_=dt["br_b"][:]),
                inc=("vec", (ph, "br"), 16))
        if self.prev_fused:
            self.snt(src, ph, xnt, None, sq, xnb, pre_normed=True)
            for lc in range(NLC):
                b = lc % 2
                wacc = [(f"xc{b}", (ph, "xc", lc - 2))] if lc >= 2 else []
                wacc.extend(self.xspill_keys.get(lc, ()))
                self.op("sync", lambda e, lc=lc: e.dma_start(
                    out=acc[:, lc * 1024:(lc + 1) * 1024],
                    in_=src[lc * 128:(lc + 1) * 128, :]),
                    waits=wacc, inc=(f"xc{b}", (ph, "xc", lc), 16))
        else:
            self.snt(src, ph, xnt, None, sq, xnb,
                     xc_views=[acc[:, lc * 1024:(lc + 1) * 1024] for lc in range(NLC)])

        self.op("gpsimd", lambda e: e.dma_start(
            out=wr_bf.rearrange("p (c e) -> p c e", c=NDK),
            in_=dt["Wr"].rearrange("(c p) e -> p c e", p=128)),
            inc=("wcx", (ph, "wr"), 16))
        self.op("gpsimd", lambda e: e.dma_start(out=b3bf[0:NEXP, :], in_=dt["b3"][:]),
                inc=("wcx", (ph, "b3"), 16))

        for lc in range(NLC):
            zps = P[3][:, (lc % 2) * 512:(lc % 2) * 512 + 8]
            def mml(e, lc=lc, zps=zps):
                for dk in range(NDK):
                    inst = e.matmul(zps, xnt[:, dk * T + lc * 128: dk * T + lc * 128 + 128],
                                    wr_bf[:, dk * 8:dk * 8 + 8],
                                    start=(dk == 0), stop=(dk == NDK - 1))
                return inst
            w = []
            if lc == 0:
                w = self.w_xnt(ph) + [("wcx", (ph, "b3"))]
            if lc >= 2:
                w.append(("dve", (ph, "lgev", lc - 2)))
            self.op("tensor", mml, waits=w, inc=("pe", (ph, "lgmm", lc), 1))
            self.op("vector", lambda e, lc=lc, zps=zps: e.tensor_tensor(
                logits[:, lc * 8:lc * 8 + 8], zps, self.br_sb[:], ALU.add),
                waits=[("pe", (ph, "lgmm", lc))] + ([("vec", (ph, "br"))] if lc == 0 else []),
                inc=("dve", (ph, "lgev", lc), 1))
        lg3 = logits.rearrange("p (lc e) -> p lc e", lc=NLC)
        def g_m1(e):
            return e.reduce_max(m1[:], lg3, axis=AX.X)
        self.op("vector", g_m1, waits=[("dve", (ph, "lgev", NLC - 1))])
        def g_mask1(e):
            m1b = m1[:].unsqueeze(2).broadcast_to([128, NLC, 8])
            return e.tensor_tensor(mask1.rearrange("p (lc e) -> p lc e", lc=NLC),
                                   logits.rearrange("p (lc e) -> p lc e", lc=NLC),
                                   m1b, ALU.is_equal)
        self.op("vector", g_mask1)
        def g_l2(e):
            return e.scalar_tensor_tensor(l2[:], mask1[:], -1e30, logits[:], ALU.mult, ALU.add)
        self.op("vector", g_l2)
        def g_m2(e):
            return e.reduce_max(m2[:], l2.rearrange("p (lc e) -> p lc e", lc=NLC), axis=AX.X)
        self.op("vector", g_m2)
        def g_dd(e):
            return e.tensor_tensor(ddt[:], m1[:], m2[:], ALU.subtract)
        self.op("vector", g_dd, inc=("dve", (ph, "gdd"), 1))
        def g_exp(e):
            return e.activation(e12[:], ddt[:], AF.Exp, scale=-1.0)
        self.op("scalar", g_exp, waits=[("dve", (ph, "gdd"))], inc=("act", (ph, "ge"), 1))
        def g_w(e):
            e.tensor_scalar(w1d[:], e12[:], 1.0, None, ALU.add)
            e.reciprocal(w1[:], w1d[:])
            return e.tensor_tensor(w2[:], e12[:], w1[:], ALU.mult)
        self.op("vector", g_w, waits=[("act", (ph, "ge"))])
        def g_gates(e):
            l23 = l2.rearrange("p (lc e) -> p lc e", lc=NLC)
            m13 = mask1.rearrange("p (lc e) -> p lc e", lc=NLC)
            g3 = gates.rearrange("p (lc e) -> p lc e", lc=NLC)
            gt3 = logits.rearrange("p (lc e) -> p lc e", lc=NLC)
            m2b = m2[:].unsqueeze(2).broadcast_to([128, NLC, 8])
            w1b = w1[:].unsqueeze(2).broadcast_to([128, NLC, 8])
            w2b = w2[:].unsqueeze(2).broadcast_to([128, NLC, 8])
            e.tensor_tensor(gt3, l23, m2b, ALU.is_equal)
            e.tensor_tensor(l23, gt3, w2b, ALU.mult)
            e.tensor_tensor(gt3, m13, w1b, ALU.mult)
            return e.tensor_tensor(g3, l23, gt3, ALU.add)
        self.op("vector", g_gates, inc=("dve", (ph, "gates"), 1))
        self.op("vector", lambda e: e.memset(gbf[:, 8:128], 0.0),
                inc=("dve", (ph, "gbf0"), 1))
        for lc in range(NLC):
            w = [("gsp", (ph, "gsp", lc - 1))] if lc >= 1 else [("dve", (ph, "gbf0"))]
            self.op("vector", lambda e, lc=lc: e.tensor_copy(
                gbf[:, 0:8], gates[:, lc * 8:lc * 8 + 8]),
                waits=w, inc=("dve", (ph, "gbf", lc), 1))
            self.op("gpsimd", lambda e, lc=lc: e.dma_start(
                out=dt["gspill"][lc * 128:(lc + 1) * 128, :], in_=gbf[:]),
                waits=[("dve", (ph, "gbf", lc))], inc=("gsp", (ph, "gsp", lc), 16))
        self.op("sync", lambda e: e.dma_start(
            out=gT[:, 0:T], in_=dt["gspill"][:], transpose=True),
            waits=[("gsp", (ph, "gsp", NLC - 1))], inc=("tl", (ph, "gT"), 16))

        def wslice(si):
            e_, s_ = divmod(si, NS)
            return e_, slices[s_]

        def emit_casts(si):
            ex, (h0, h1) = wslice(si)
            nch = h1 - h0
            par = si % 2
            wfb = wf[par]
            w1v = wfb[:, 0:nch * 1024]
            w2v = wfb[:, 4 * 1024:4 * 1024 + nch * 1024]
            w3v = wfb[:, 8 * 1024:8 * 1024 + nch * 1024]
            prev_ntile = ((slices[(si - 2) % NS][1] - slices[(si - 2) % NS][0]) * NLT
                          if si >= 2 else 0)
            w12_w = [("pe", (ph, "hmm", si - 2, prev_ntile - 1))] if si >= 2 else []
            w3_w = [("pe", (ph, "bmm", si - 2, NLC * 2 - 1))] if si >= 2 else []
            self.op("gpsimd", lambda e, ex=ex, h0=h0, nch=nch, w1v=w1v: e.dma_start(
                out=w1v.rearrange("p (c h) -> p c h", c=NDK),
                in_=dt["W1"][ex, :, h0 * 128:(h0 + nch) * 128]
                .rearrange("(c p) h -> p c h", p=128)),
                waits=w12_w, inc=(f"wa{par}", (ph, "wc1", si), 16))
            self.op("gpsimd", lambda e, ex=ex, h0=h0, nch=nch, w2v=w2v: e.dma_start(
                out=w2v.rearrange("p (c h) -> p c h", c=NDK),
                in_=dt["W2"][ex, :, h0 * 128:(h0 + nch) * 128]
                .rearrange("(c p) h -> p c h", p=128)),
                inc=(f"wa{par}", (ph, "wc12", si), 16))
            self.op("gpsimd", lambda e, ex=ex, h0=h0, nch=nch, w3v=w3v: e.dma_start(
                out=w3v.rearrange("p (c dd) -> p c dd", c=nch),
                in_=dt["W3"][ex, h0 * 128:(h0 + nch) * 128, :]
                .rearrange("(c p) dd -> p c dd", p=128)),
                waits=w3_w, inc=(f"wb{par}", (ph, "wc", si), 16))

        emit_casts(0)
        emit_casts(1)
        for si in range(NSI):
            ex, (h0, h1) = wslice(si)
            nch = h1 - h0
            par = si % 2
            wfb = wf[par]
            hb = hbuf[par]
            w1v = wfb[:, 0:nch * 1024]
            w2v = wfb[:, 4 * 1024:4 * 1024 + nch * 1024]
            w3v = wfb[:, 8 * 1024:8 * 1024 + nch * 1024]
            if si >= 2:
                emit_casts(si)

            ntile = nch * NLT
            for t in range(ntile):
                hk, lt = divmod(t, NLT)
                pa = P[t % 2][:, 0:LT]
                pb_ = P[t % 2][:, 512:512 + LT]
                b1col = self.b12_sb[:, (ex * 2 + 0) * NHK + h0 + hk:(ex * 2 + 0) * NHK + h0 + hk + 1]
                b2col = self.b12_sb[:, (ex * 2 + 1) * NHK + h0 + hk:(ex * 2 + 1) * NHK + h0 + hk + 1]
                def mmh(e, hk=hk, lt=lt, pa=pa, pb_=pb_, w1v=w1v, w2v=w2v, nch=nch):
                    cw = nch * 128
                    for dk in range(NDK):
                        e.matmul(pa, w1v[:, dk * cw + hk * 128: dk * cw + hk * 128 + 128],
                                 xnt[:, dk * T + lt * LT: dk * T + lt * LT + LT],
                                 start=(dk == 0), stop=(dk == NDK - 1))
                    for dk in range(NDK):
                        inst = e.matmul(pb_, w2v[:, dk * cw + hk * 128: dk * cw + hk * 128 + 128],
                                        xnt[:, dk * T + lt * LT: dk * T + lt * LT + LT],
                                        start=(dk == 0), stop=(dk == NDK - 1))
                    return inst
                w = []
                if t == 0:
                    w.append((f"wa{par}", (ph, "wc12", si)))
                    if si == 0:
                        w.append(("vec", (ph, "br")))
                    if si >= 2:
                        pass
                if t >= 2:
                    w.append(("dve", (ph, "hev", si, t - 2)))
                    w.append(("act", (ph, "sev", si, t - 2)))
                elif si >= 1:
                    pt = (si, t)
                    prev_nt = (slices[(si - 1) % NS][1] - slices[(si - 1) % NS][0]) * NLT
                    if prev_nt + t - 2 >= 0 and t < 2:
                        w.append(("dve", (ph, "hev", si - 1, prev_nt + t - 2)))
                        w.append(("act", (ph, "sev", si - 1, prev_nt + t - 2)))
                self.op("tensor", mmh, waits=w, inc=("pe", (ph, "hmm", si, t), 1))
                sb = s_sc[t % 2]
                def silu(e, pb_=pb_, sb=sb, b2col=b2col):
                    return e.activation(sb[:, 0:LT], pb_, AF.Silu, bias=b2col)
                self.op("scalar", silu, waits=[("pe", (ph, "hmm", si, t))],
                        inc=("act", (ph, "sev", si, t), 1))
                def hstt(e, hk=hk, lt=lt, pa=pa, sb=sb, b1col=b1col, hb=hb):
                    return e.scalar_tensor_tensor(
                        hb[:, hk * T + lt * LT: hk * T + lt * LT + LT],
                        pa, b1col, sb[:, 0:LT], ALU.add, ALU.mult)
                wh = [("act", (ph, "sev", si, t))]
                if t == 0 and si >= 2:
                    wh.append(("pe", (ph, "bmm", si - 2, NLC * 2 - 1)))
                self.op("vector", hstt, waits=wh, inc=("dve", (ph, "hev", si, t), 1))

            if si >= 1:
                self.emit_ffn_B(ph, si - 1, wslice, wf, hbuf, gates, acc, dst)
        self.emit_ffn_B(ph, NSI - 1, wslice, wf, hbuf, gates, acc, dst)

        for lc in range(NLC):
            for dtile in range(2):
                j = lc * 2 + dtile
                ps = P[2 + (j % 4) // 2][:, (j % 2) * 512:(j % 2) * 512 + 512]
                def mmb3(e, lc=lc, dtile=dtile, ps=ps):
                    return e.matmul(ps, gT[0:NEXP, lc * 128:(lc + 1) * 128],
                                    b3bf[0:NEXP, dtile * 512:dtile * 512 + 512],
                                    start=True, stop=True)
                w = []
                if j == 0:
                    w = [("tl", (ph, "gT")),
                         ("dve", (ph, "bev", NSI - 1, NLC * 2 - 1))]
                if j >= 4:
                    w.append(("dve", (ph, "b3ev", j - 4)))
                self.op("tensor", mmb3, waits=w, inc=("pe", (ph, "b3mm", j), 1))
                def b3ev(e, lc=lc, dtile=dtile, ps=ps):
                    a = acc[:, lc * 1024 + dtile * 512: lc * 1024 + dtile * 512 + 512]
                    return e.tensor_tensor(a, a, ps, ALU.add)
                self.op("vector", b3ev, waits=[("pe", (ph, "b3mm", j))],
                        inc=("dve", (ph, "b3ev", j), 1))
            self.op("gpsimd", lambda e, lc=lc, dst=dst: e.dma_start(
                out=dst[lc * 128:(lc + 1) * 128, :], in_=acc[:, lc * 1024:(lc + 1) * 1024]),
                waits=[("dve", (ph, "b3ev", lc * 2 + 1))],
                inc=(f"xw{lc % 2}", (ph, "xnew", lc), 16))
            self.ep_ln(ph, lc, acc[:, lc * 1024:(lc + 1) * 1024], xnb, sq,
                       ("dve", (ph, "b3ev", lc * 2 + 1)), dst)
        self.set_xspill_totals(ph, dst)
        self.prev_fused = dst is self.dt["xspill"]

    def emit_ffn_B(self, ph, si, wslice, wf, hbuf, gates, acc, dst):
        T, NLC = self.T, self.NLC
        P = self.P
        ex, (h0, h1) = wslice(si)
        nch = h1 - h0
        par = si % 2
        w3v = wf[par][:, 8 * 1024:8 * 1024 + nch * 1024]
        hb = hbuf[par]
        ntile = nch * self.NLT
        for j in range(NLC * 2):
            lc, dtile = divmod(j, 2)
            ps = P[2 + (j % 4) // 2][:, (j % 2) * 512:(j % 2) * 512 + 512]
            def mmb(e, lc=lc, dtile=dtile, ps=ps, hb=hb, w3v=w3v, nch=nch):
                for hk in range(nch):
                    inst = e.matmul(ps, hb[:, hk * T + lc * 128: hk * T + lc * 128 + 128],
                                    w3v[:, hk * 1024 + dtile * 512: hk * 1024 + dtile * 512 + 512],
                                    start=(hk == 0), stop=(hk == nch - 1))
                return inst
            w = []
            if j == 0:
                w.append(("dve", (ph, "hev", si, ntile - 1)))
                w.append((f"wb{par}", (ph, "wc", si)))
                if si == 0:
                    w.append(("dve", (ph, "lgev", NLC - 1)))
            if j >= 4:
                w.append(("dve", (ph, "bev", si, j - 4)))
            elif si >= 1:
                w.append(("dve", (ph, "bev", si - 1, NLC * 2 - 4 + j)))
            self.op("tensor", mmb, waits=w, inc=("pe", (ph, "bmm", si, j), 1))
            gcol = gates[:, lc * 8 + ex: lc * 8 + ex + 1]
            def bev(e, lc=lc, dtile=dtile, ps=ps, gcol=gcol):
                a = acc[:, lc * 1024 + dtile * 512: lc * 1024 + dtile * 512 + 512]
                return e.scalar_tensor_tensor(a, ps, gcol, a, ALU.mult, ALU.add)
            wv = [("pe", (ph, "bmm", si, j))]
            if si == 0 and j == 0:
                wv.append((f"xc{(NLC - 1) % 2}", (ph, "xc", NLC - 1)))
                wv.append((f"xc{(NLC - 2) % 2}", (ph, "xc", NLC - 2)))
                wv.append(("dve", (ph, "gates")))
            self.op("vector", bev, waits=wv, inc=("dve", (ph, "bev", si, j), 1))

    def phase_attn_b(self, src, dst):
        T, NLC, LT, NLT = self.T, self.NLC, self.LT, self.NLT
        ph = f"B{self.nbar}"
        nc, dt, P = self.nc, self.dt, self.P
        st = self.stat
        QT = LT
        NQTILE = T // QT
        NQC = QT // 128

        o = 0
        def carve(n):
            nonlocal o
            v = self.pbf[:, o:o + n]
            o += n
            return v
        reg0 = carve(max(NDK * T, NDK * 1024))
        reg1 = carve(max(NDK * T, NDK * 1024))
        vx = carve(NLC * 8 * VDE)
        E_sb = carve(16 * 512)
        masks = carve(4 * 512)
        ycol = carve(NLC * 128)
        qsb = carve(2 * 1024)
        xnb = [E_sb[:, 0:1024], E_sb[:, 1024:2048]]
        assert o <= PBF_COLS, o
        xnt = reg0
        qT = reg0
        wWb = reg1[:, 0:NDK * 1024]
        kT = reg1
        yT = reg1
        wpT = reg0[:, 0:NDK * 1024]

        of = 0
        def carvef(n):
            nonlocal of
            v = self.pf[:, of:of + n]
            of += n
            return v
        xc = [carvef(D), carvef(D)]
        sq = carvef(D)
        qr = [carvef(D), carvef(D)]
        rot = carvef(D)
        tmp = carvef(512)
        tmp2 = carvef(512)
        cs = [carvef(D), carvef(D)]
        y1 = carvef(NLC * 128)
        sqb = carvef(2 * D)
        grp = carvef(32)
        rstdg = carvef(32)
        rz = carvef(2)
        xc2 = [carvef(D), carvef(D)]
        xnew = [carvef(D), carvef(D)]
        assert of <= PF_COLS, of

        self.op("sync", lambda e: e.dma_start(
            out=masks.rearrange("p (j f) -> p j f", j=4),
            in_=dt["masks"].rearrange("(j p) f -> p j f", p=128)),
            inc=("mks", (ph, "mks"), 16))
        for i in range(4):
            self.op("sync", lambda e, i=i: e.dma_start(
                out=self.lam_sb[i][:], in_=dt["lam"][i:i + 1, :]),
                inc=("lml", (ph, "lam", i), 16))
        self.op("vector", lambda e: e.memset(self.ones_f[:], 1.0),
                inc=("dve", (ph, "onesf"), 1))
        self.op("vector", lambda e: e.memset(
            vx.rearrange("p (lc h c) -> p lc h c", lc=NLC, h=8)[:, :, :, 128:129], 1.0),
            inc=("dve", (ph, "vones"), 1))
        def lam_comp(e):
            e.tensor_tensor(self.lam_t[:], self.lam_sb[0][:], self.lam_sb[1][:], ALU.mult)
            e.reduce_sum(st[0:1, 20:21], self.lam_t[:], axis=AX.X)
            e.tensor_tensor(self.lam_t[:], self.lam_sb[2][:], self.lam_sb[3][:], ALU.mult)
            return e.reduce_sum(st[0:1, 21:22], self.lam_t[:], axis=AX.X)
        self.op("vector", lam_comp, waits=[("lml", (ph, "lam", 3))],
                inc=("dve", (ph, "lamdot"), 1))
        def lam_exp(e):
            e.activation(st[0:1, 22:23], st[0:1, 20:21], AF.Exp)
            return e.activation(st[0:1, 23:24], st[0:1, 21:22], AF.Exp)
        self.op("scalar", lam_exp, waits=[("dve", (ph, "lamdot"))],
                inc=("act", (ph, "lamexp"), 1))
        def lam_full(e):
            e.tensor_tensor(st[0:1, 24:25], st[0:1, 22:23], st[0:1, 23:24], ALU.subtract)
            return e.tensor_scalar(st[0:1, 25:26], st[0:1, 24:25], LAMBDA_INIT, None, ALU.add)
        self.op("vector", lam_full, waits=[("act", (ph, "lamexp"))],
                inc=("dve", (ph, "lamf"), 1))
        lam_ps = P[3][:, 0:1]
        self.op("tensor", lambda e: e.matmul(lam_ps, self.ones_f[:], st[0:1, 25:26],
                                             start=True, stop=True),
                waits=[("dve", (ph, "lamf")), ("dve", (ph, "onesf"))],
                inc=("pe", (ph, "lammm"), 1))
        self.op("vector", lambda e: e.tensor_copy(self.lam_col[:], lam_ps),
                waits=[("pe", (ph, "lammm"))], inc=("dve", (ph, "lamcol"), 1))

        if self.prev_fused:
            self.snt(src, ph, xnt, xc, sq, xnb, pre_normed=True)
        else:
            self.snt(src, ph, xnt, xc, sq, xnb)

        def wprep(wname, dest, key, waits=()):
            for ik in range(NDK):
                self.op("sync", lambda e, wname=wname, ik=ik, dest=dest: e.dma_start(
                    out=dest[:, ik * 1024:(ik + 1) * 1024],
                    in_=dt[wname + "s"][:, ik * 128:(ik + 1) * 128], transpose=True),
                    waits=([self.precast_keys[wname]] + list(waits)) if ik == 0 else [],
                    inc=("tlw" + key, (ph, "tl" + key, ik), 16))
        wprep("bWv", wWb, "wv")

        def bank4(i):
            t = P[(i % 4) // 2]
            return t[:, (i % 2) * 512:(i % 2) * 512 + 512]

        nvt = NLC * 2
        for i in range(nvt):
            lc, it = divmod(i, 2)
            def mmv(e, lc=lc, it=it, i=i):
                ps = bank4(i)
                for dk in range(NDK):
                    inst = e.matmul(ps, xnt[:, dk * T + lc * 128: dk * T + lc * 128 + 128],
                                    wWb[:, dk * 1024 + it * 512: dk * 1024 + it * 512 + 512],
                                    start=(dk == 0), stop=(dk == NDK - 1))
                return inst
            w = []
            if i == 0:
                w = self.w_xnt(ph) + [("tlwwv", (ph, "tlwv", NDK - 1))]
            if i >= 4:
                w.append(("act", (ph, "vev", i - 4)))
            self.op("tensor", mmv, waits=w, inc=("pe", (ph, "vmm", i), 1))
            def vev(e, lc=lc, it=it, i=i):
                dst_v = vx.rearrange("p (lc h c) -> p lc h c", lc=NLC, h=8)[
                    :, lc, it * 4:(it + 1) * 4, 0:128]
                return e.activation(dst_v, bank4(i).rearrange("p (h c) -> p h c", h=4), AF.Copy)
            self.op("scalar", vev, waits=[("pe", (ph, "vmm", i))],
                    inc=("act", (ph, "vev", i), 1))
        wprep("bWk", wWb, "wk", waits=[("pe", (ph, "vmm", nvt - 1))])

        def qk_pipeline(which, spillname, scale_q):
            wkey = "wk" if which == "k" else "wq"
            pend_rstd = []

            def emit_rstd(lc):
                b = lc % 2
                def rmsr(e, b=b, lc=lc):
                    e.activation(rstdg[:, b * 16:b * 16 + 16], grp[:, b * 16:b * 16 + 16],
                                 AF.Ln, bias=self.epst[:, 1:2], scale=1.0 / HD)
                    bias2 = self.epst[:, 2:3] if scale_q else self.epst[:, 3:4]
                    return e.activation(rstdg[:, b * 16:b * 16 + 16],
                                        rstdg[:, b * 16:b * 16 + 16],
                                        AF.Exp, scale=-0.5, bias=bias2)
                w = [("dve", (ph, which, "grp", lc))]
                if lc >= 2:
                    w.append(("pool", (ph, which, "qsb", lc - 2)))
                self.op("scalar", rmsr, waits=w, inc=("act", (ph, which, "rstd", lc), 1))

            for lc in range(NLC):
                b = lc % 2
                for it in range(2):
                    i = lc * 2 + it
                    def mmk(e, lc=lc, it=it, i=i):
                        ps = bank4(i)
                        for dk in range(NDK):
                            inst = e.matmul(ps, xnt[:, dk * T + lc * 128: dk * T + lc * 128 + 128],
                                            wWb[:, dk * 1024 + it * 512: dk * 1024 + it * 512 + 512],
                                            start=(dk == 0), stop=(dk == NDK - 1))
                        return inst
                    w = []
                    if i == 0:
                        w.append(("tlw" + wkey, (ph, "tl" + wkey, NDK - 1)))
                        if which == "k":
                            w.append(("act", (ph, "vev", nvt - 1)))
                        else:
                            w.append(("dve", (ph, "kdone")))
                    if i >= 4:
                        w.append(("dve", (ph, which, "ev", i - 4)))
                    self.op("tensor", mmk, waits=w, inc=("pe", (ph, which, "mm", i), 1))
                    def kev(e, it=it, i=i, b=b):
                        return e.tensor_copy(qr[b][:, it * 512:it * 512 + 512], bank4(i))
                    wkev = [("pe", (ph, which, "mm", i))]
                    if it == 0:
                        if lc >= 2:
                            wkev.append(("pool", (ph, which, "rot", lc - 2)))
                        elif which == "q" and NLC - 2 + lc >= 0:
                            wkev.append(("pool", (ph, "k", "rot", NLC - 2 + lc)))
                    self.op("vector", kev, waits=wkev, inc=("dve", (ph, which, "ev", i), 1))
                self.op("sync", lambda e, lc=lc, b=b: e.dma_start(
                    out=cs[b][:], in_=dt["cs_tab"][lc * 128:(lc + 1) * 128, :]),
                    waits=[("pool", (ph, which, "rot", lc - 2))] if lc >= 2 else
                          ([("pool", (ph, "k", "rot", NLC - 2)), ("pool", (ph, "k", "rot", NLC - 1))] if which == "q" and NLC >= 2 else []),
                    inc=(f"cs{b}", (ph, which, "cs", lc), 16))
                def rmsq(e, b=b):
                    return e.activation(sqb[:, b * 1024:(b + 1) * 1024], qr[b][:], AF.Square)
                wsq = [("dve", (ph, which, "ev", lc * 2 + 1))]
                if lc >= 2:
                    wsq.append(("dve", (ph, which, "grp", lc - 2)))
                self.op("scalar", rmsq, waits=wsq, inc=("act", (ph, which, "sq", lc), 1))
                def rmsg(e, b=b):
                    return e.reduce_sum(grp[:, b * 16:b * 16 + 16],
                                        sqb[:, b * 1024:(b + 1) * 1024]
                                        .rearrange("p (g c) -> p g c", g=16), axis=AX.X)
                self.op("vector", rmsg, waits=[("act", (ph, which, "sq", lc))],
                        inc=("dve", (ph, which, "grp", lc), 1))
                pend_rstd.append(lc)
                if len(pend_rstd) > 1:
                    emit_rstd(pend_rstd.pop(0))
                def rotary(e, b=b):
                    q3 = qr[b].rearrange("p (g d) -> p g d", g=16)
                    r3 = rot.rearrange("p (g d) -> p g d", g=16)
                    t3 = tmp.rearrange("p (g d) -> p g d", g=16)
                    t4 = tmp2.rearrange("p (g d) -> p g d", g=16)
                    c3 = cs[b][:, 0:512].rearrange("p (g d) -> p g d", g=16)
                    s3 = cs[b][:, 512:1024].rearrange("p (g d) -> p g d", g=16)
                    q1, q2 = q3[:, :, 0:32], q3[:, :, 32:64]
                    e.tensor_tensor(r3[:, :, 0:32], q1, c3, ALU.mult)
                    e.tensor_tensor(t3, q2, s3, ALU.mult)
                    e.tensor_tensor(r3[:, :, 0:32], r3[:, :, 0:32], t3, ALU.add)
                    e.tensor_tensor(r3[:, :, 32:64], q2, c3, ALU.mult)
                    e.tensor_tensor(t4, q1, s3, ALU.mult)
                    return e.tensor_tensor(r3[:, :, 32:64], r3[:, :, 32:64], t4, ALU.subtract)
                self.op("gpsimd", rotary,
                        waits=[(f"cs{b}", (ph, which, "cs", lc)),
                               ("dve", (ph, which, "ev", lc * 2 + 1))],
                        inc=("pool", (ph, which, "rot", lc), 1))
                def rscale(e, b=b):
                    r3 = rot.rearrange("p (g d) -> p g d", g=16)
                    o3 = qsb[:, b * 1024:(b + 1) * 1024].rearrange("p (g d) -> p g d", g=16)
                    rb = rstdg[:, b * 16:b * 16 + 16].unsqueeze(2).broadcast_to([128, 16, HD])
                    return e.tensor_tensor(o3, r3, rb, ALU.mult)
                wrs = [("act", (ph, which, "rstd", lc))]
                if lc >= 2:
                    wrs.append((f"sp2{lc % 2}", (ph, which, "sp", lc - 2)))
                elif which == "q" and NLC - 2 + lc >= 0:
                    wrs.append((f"sp2{lc % 2}", (ph, "k", "sp", NLC - 2 + lc)))
                self.op("gpsimd", rscale, waits=wrs, inc=("pool", (ph, which, "qsb", lc), 1))
                self.op("gpsimd", lambda e, lc=lc, b=b, spillname=spillname: e.dma_start(
                    out=dt[spillname][lc * 128:(lc + 1) * 128, :],
                    in_=qsb[:, b * 1024:(b + 1) * 1024]),
                    waits=[("pool", (ph, which, "qsb", lc))],
                    inc=(f"sp2{lc % 2}", (ph, which, "sp", lc), 16))
            while pend_rstd:
                emit_rstd(pend_rstd.pop(0))
            if which == "k":
                self.op("vector", lambda e: e.memset(st[0:1, 26:27], 0.0),
                        inc=("dve", (ph, "kdone"), 1))
        qk_pipeline("k", "kspill", False)
        wprep("bWq", wWb, "wq", waits=[("pe", (ph, "k", "mm", NLC * 2 - 1))])
        qk_pipeline("q", "qspill", True)

        for ik in range(NDK):
            self.op("sync", lambda e, ik=ik: e.dma_start(
                out=qT[:, ik * T:(ik + 1) * T],
                in_=dt["qspill"][:, ik * 128:(ik + 1) * 128], transpose=True),
                waits=[(f"sp2{(NLC - 1) % 2}", (ph, "q", "sp", NLC - 1)),
                       (f"sp2{(NLC - 2) % 2}", (ph, "q", "sp", NLC - 2)),
                       ("pe", (ph, "q", "mm", NLC * 2 - 1))],
                inc=("tlq", (ph, "qT", ik), 16))
            self.op("sync", lambda e, ik=ik: e.dma_start(
                out=kT[:, ik * T:(ik + 1) * T],
                in_=dt["kspill"][:, ik * 128:(ik + 1) * 128], transpose=True),
                waits=[(f"sp2{(NLC - 1) % 2}", (ph, "k", "sp", NLC - 1)),
                       (f"sp2{(NLC - 2) % 2}", (ph, "k", "sp", NLC - 2)),
                       ("pe", (ph, "q", "mm", NLC * 2 - 1))],
                inc=("tlk", (ph, "kT", ik), 16))

        sco_i = 0
        yev_i = 0
        for h in range(NHEAD):
            for m in range(2):
                row = m * 512 + h * HD
                ik, off = row // 128, row % 128
                for qt in range(NQTILE):
                    nkt = (qt + 1) * NQC
                    for kt in range(nkt):
                        ps_s = P[2][:, (sco_i % 2) * 512:(sco_i % 2) * 512 + QT]
                        def mms(e, kt=kt, qt=qt, ik=ik, off=off, ps_s=ps_s):
                            return e.matmul(
                                ps_s,
                                kT[off:off + HD, ik * T + kt * 128: ik * T + kt * 128 + 128],
                                qT[off:off + HD, ik * T + qt * QT: ik * T + qt * QT + QT],
                                start=True, stop=True)
                        w = []
                        if h == 0 and m == 0 and qt == 0 and kt == 0:
                            w = [("tlq", (ph, "qT", NDK - 1)), ("tlk", (ph, "kT", NDK - 1)),
                                 ("dve", (ph, "vones")), ("mks", (ph, "mks")),
                                 ("dve", (ph, "lamcol"))]
                        if sco_i >= 2:
                            w.append(("act", (ph, "sev", sco_i - 2)))
                        self.op("tensor", mms, waits=w, inc=("pe", (ph, "smm", sco_i), 1))
                        we = [("pe", (ph, "smm", sco_i))]
                        prev = (h, m, qt - 1) if qt > 0 else ((h, m - 1, NQTILE - 1) if m > 0 else
                                                             ((h - 1, 1, NQTILE - 1) if h > 0 else None))
                        if prev is not None and kt == 0:
                            we.append(("pe", (ph, "ymm") + prev + (NQC - 1,)))
                        def sev(e, kt=kt, ps_s=ps_s):
                            return e.activation(E_sb[:, kt * 512:kt * 512 + QT], ps_s, AF.Exp)
                        self.op("scalar", sev, waits=we, inc=("act", (ph, "sev", sco_i), 1))
                        sco_i += 1
                    def bmask(e, qt=qt):
                        for j in range(NQC):
                            kt = qt * NQC + j
                            sl = E_sb[:, kt * 512:kt * 512 + QT]
                            inst = e.tensor_tensor(sl, sl, masks[:, j * 512:j * 512 + QT], ALU.mult)
                        return inst
                    self.op("vector", bmask, waits=[("act", (ph, "sev", sco_i - 1))],
                            inc=("dve", (ph, "msk", h, m, qt), 1))
                    for qc in range(NQC):
                        gqc = qt * NQC + qc
                        ps_y = P[qc // 2][:, (qc % 2) * 512:(qc % 2) * 512 + VDE]
                        def mmy(e, qt=qt, qc=qc, h=h, ps_y=ps_y):
                            nk = qt * NQC + qc + 1
                            for kt in range(nk):
                                inst = e.matmul(
                                    ps_y,
                                    E_sb[:, kt * 512 + qc * 128: kt * 512 + qc * 128 + 128],
                                    vx[:, kt * 1032 + h * VDE: kt * 1032 + h * VDE + VDE],
                                    start=(kt == 0), stop=(kt == nk - 1))
                            return inst
                        w = [("dve", (ph, "msk", h, m, qt))]
                        if yev_i >= 4:
                            w.append(("dve", (ph, "yev", yev_i - 4)))
                        self.op("tensor", mmy, waits=w,
                                inc=("pe", (ph, "ymm", h, m, qt, qc), 1))
                        if m == 0:
                            def yev(e, gqc=gqc, ps_y=ps_y):
                                e.reciprocal(rz[:, 0:1], ps_y[:, 128:129])
                                return e.tensor_scalar(
                                    y1[:, gqc * 128:(gqc + 1) * 128], ps_y[:, 0:128],
                                    rz[:, 0:1], None, ALU.mult)
                        else:
                            def yev(e, gqc=gqc, ps_y=ps_y):
                                e.reciprocal(rz[:, 0:1], ps_y[:, 128:129])
                                e.tensor_tensor(rz[:, 1:2], rz[:, 0:1], self.lam_col[:], ALU.mult)
                                e.tensor_scalar(tmp[:, 0:128], ps_y[:, 0:128],
                                                rz[:, 1:2], None, ALU.mult)
                                return e.tensor_tensor(
                                    ycol[:, gqc * 128:(gqc + 1) * 128],
                                    y1[:, gqc * 128:(gqc + 1) * 128], tmp[:, 0:128],
                                    ALU.subtract)
                        wyev = [("pe", (ph, "ymm", h, m, qt, qc))]
                        if m == 1 and qt == 0 and qc == 0 and h >= 1:
                            wyev.append(("ysp", (ph, "ysp", h - 1)))
                        self.op("vector", yev, waits=wyev,
                                inc=("dve", (ph, "yev", yev_i), 1))
                        yev_i += 1
            self.op("gpsimd", lambda e, h=h: e.dma_start(
                out=dt["yspill"].rearrange("(lc p) dd -> p lc dd", p=128)[:, :, h * 128:(h + 1) * 128],
                in_=ycol.rearrange("p (lc c) -> p lc c", lc=NLC)),
                waits=[("dve", (ph, "yev", yev_i - 1))], inc=("ysp", (ph, "ysp", h), 16))

        wprep("bWp", wpT, "wp", waits=[("pe", (ph, "smm", sco_i - 1))])
        for ik in range(NDK):
            self.op("sync", lambda e, ik=ik: e.dma_start(
                out=yT[:, ik * T:(ik + 1) * T],
                in_=dt["yspill"][:, ik * 128:(ik + 1) * 128], transpose=True),
                waits=[("ysp", (ph, "ysp", NHEAD - 1)), ("pe", (ph, "smm", sco_i - 1))],
                inc=("tly", (ph, "yT", ik), 16))
        for lc in range(NLC):
            par = lc % 2
            pb = P[par]
            eb = par
            woxc = [("dve", (ph, "eplg", lc - 2))] if lc >= 2 else []
            if src is self.dt["xspill"]:
                woxc.extend(self.xspill_keys.get(lc, ()))
            self.op("sync", lambda e, lc=lc, eb=eb: e.dma_start(
                out=xc2[eb][:], in_=src[lc * 128:(lc + 1) * 128, :]),
                waits=woxc, inc=(f"oxc{eb}", (ph, "oxc", lc), 16))
            def mmo(e, lc=lc, pb=pb):
                for dt_ in range(2):
                    for ik in range(NDK):
                        inst = e.matmul(pb[:, dt_ * 512:dt_ * 512 + 512],
                                        yT[:, ik * T + lc * 128: ik * T + lc * 128 + 128],
                                        wpT[:, ik * 1024 + dt_ * 512: ik * 1024 + dt_ * 512 + 512],
                                        start=(ik == 0), stop=(ik == NDK - 1))
                return inst
            w = []
            if lc == 0:
                w.append(("tly", (ph, "yT", NDK - 1)))
                w.append(("tlwwp", (ph, "tlwp", NDK - 1)))
                w.append(("dve", (ph, "yev", yev_i - 1)))
            if lc >= 2:
                w.append(("dve", (ph, "eplg_t2", lc - 2)))
            self.op("tensor", mmo, waits=w, inc=("pe", (ph, "omm", lc), 1))
            def ep(e, lc=lc, eb=eb, pb=pb):
                return e.scalar_tensor_tensor(xnew[eb][:], pb[:], 1.0, xc2[eb][:],
                                              ALU.mult, ALU.add)
            wep = [("pe", (ph, "omm", lc)), (f"oxc{eb}", (ph, "oxc", lc))]
            if lc >= 2:
                wep.append((f"xw{eb}", (ph, "xnew", lc - 2)))
                if dst is self.dt["xspill"]:
                    wep.append(("act", (ph, "fssq", lc - 2)))
            self.op("vector", ep, waits=wep, inc=("dve", (ph, "eplg_t2", lc), 1))
            self.op("vector", lambda e: e.memset(st[0:1, 27:28], 0.0),
                    inc=("dve", (ph, "eplg", lc), 1))
            self.op("gpsimd", lambda e, lc=lc, eb=eb, dst=dst: e.dma_start(
                out=dst[lc * 128:(lc + 1) * 128, :], in_=xnew[eb][:]),
                waits=[("dve", (ph, "eplg_t2", lc))], inc=(f"xw{eb}", (ph, "xnew", lc), 16))
            self.ep_ln(ph, lc, xnew[eb], xnb, sq, ("dve", (ph, "eplg_t2", lc)), dst)
            if lc == NLC - 1:
                self.set_xspill_totals(ph, dst)
                self.prev_fused = dst is self.dt["xspill"]


def _pack_inputs(inputs, T, ncores=8):
    f32 = np.float32
    x = np.ascontiguousarray(np.asarray(inputs["L0_attn_a_x"], dtype=f32)[:, :T])
    B = x.shape[0]
    assert B == ncores

    com = {}
    com["aWq"] = np.ascontiguousarray(np.asarray(inputs["L0_attn_a_Wq"], f32))
    com["aWk"] = np.ascontiguousarray(np.asarray(inputs["L0_attn_a_Wk"], f32))
    com["aWv"] = np.ascontiguousarray(np.asarray(inputs["L0_attn_a_Wv"], f32))
    com["aWo"] = np.ascontiguousarray(np.asarray(inputs["L0_attn_a_Wo"], f32))
    com["bo_b"] = np.ascontiguousarray(np.broadcast_to(np.asarray(inputs["L0_attn_a_bo"], f32), (128, D)))
    com["gamma_b"] = np.ascontiguousarray(np.broadcast_to(np.asarray(inputs["L0_attn_a_gamma"], f32), (128, D)))
    com["beta_b"] = np.ascontiguousarray(np.broadcast_to(np.asarray(inputs["L0_attn_a_beta"], f32), (128, D)))
    com["bWq"] = np.ascontiguousarray(np.asarray(inputs["L0_attn_b_Wq"], f32))
    com["bWk"] = np.ascontiguousarray(np.asarray(inputs["L0_attn_b_Wk"], f32))
    com["bWv"] = np.ascontiguousarray(np.asarray(inputs["L0_attn_b_Wv"], f32))
    com["bWp"] = np.ascontiguousarray(np.asarray(inputs["L0_attn_b_Wproj"], f32))
    com["lam"] = np.ascontiguousarray(np.stack([
        np.asarray(inputs["L0_attn_b_lambda_q1"], f32),
        np.asarray(inputs["L0_attn_b_lambda_k1"], f32),
        np.asarray(inputs["L0_attn_b_lambda_q2"], f32),
        np.asarray(inputs["L0_attn_b_lambda_k2"], f32)]))
    com["Wr"] = np.ascontiguousarray(np.asarray(inputs["L0_ffn_Wr"], f32))
    com["br_b"] = np.ascontiguousarray(np.broadcast_to(np.asarray(inputs["L0_ffn_br"], f32), (128, NEXP)))
    pad_h = HPAD - H_RAW
    W1 = np.asarray(inputs["L0_ffn_W1"], f32)
    W2 = np.asarray(inputs["L0_ffn_W2"], f32)
    W3 = np.asarray(inputs["L0_ffn_W3"], f32)
    b1 = np.asarray(inputs["L0_ffn_b1"], f32)
    b2 = np.asarray(inputs["L0_ffn_b2"], f32)
    com["W1"] = np.ascontiguousarray(np.pad(W1, ((0, 0), (0, 0), (0, pad_h))))
    com["W2"] = np.ascontiguousarray(np.pad(W2, ((0, 0), (0, 0), (0, pad_h))))
    com["W3"] = np.ascontiguousarray(np.pad(W3, ((0, 0), (0, pad_h), (0, 0))))
    b1p = np.pad(b1, ((0, 0), (0, pad_h)))
    b2p = np.pad(b2, ((0, 0), (0, pad_h)))
    b12 = np.zeros((128, 2 * NEXP * NHK), f32)
    for e in range(NEXP):
        b12[:, (e * 2 + 0) * NHK:(e * 2 + 0) * NHK + NHK] = b1p[e].reshape(NHK, 128).T
        b12[:, (e * 2 + 1) * NHK:(e * 2 + 1) * NHK + NHK] = b2p[e].reshape(NHK, 128).T
    com["b12"] = b12
    com["b3"] = np.ascontiguousarray(np.asarray(inputs["L0_ffn_b3"], f32))

    inv_freq = (1.0 / (10000.0 ** (np.arange(0, HD, 2, dtype=f32) / f32(HD)))).astype(f32)
    tpos = np.arange(T, dtype=f32)
    freqs = np.outer(tpos, inv_freq).astype(f32)
    cos16 = np.tile(np.cos(freqs, dtype=f32), (1, 16))
    sin16 = np.tile(np.sin(freqs, dtype=f32), (1, 16))
    com["cs_tab"] = np.ascontiguousarray(np.concatenate([cos16, sin16], axis=1).astype(f32))
    masks = np.zeros((4, 128, 512), np.float32)
    p = np.arange(128)[:, None]
    fcol = np.arange(512)[None, :]
    for j in range(4):
        masks[j] = (fcol >= p + 128 * j).astype(np.float32)
    com["masks"] = np.ascontiguousarray(masks.reshape(4 * 128, 512).astype(ml_dtypes.bfloat16))

    in_maps = []
    for c in range(ncores):
        m = dict(com)
        m["x"] = np.ascontiguousarray(x[c])
        in_maps.append(m)
    return in_maps


def kernel(**inputs):
    T = 2048
    mk = MK(T=T, n_sub=4)
    nc = mk.build()
    in_maps = _pack_inputs(inputs, T)
    res = run_bass_kernel_spmd(nc, in_maps, core_ids=list(range(8)))
    return np.stack([r["y"] for r in res.results], axis=0)


# version 09
# speedup vs baseline: 1.0162x; 1.0162x over previous
import numpy as np
import ml_dtypes
import concourse.bass as bass
import concourse.mybir as mybir
from concourse.bass_utils import run_bass_kernel_spmd

F32 = mybir.dt.float32
BF16 = mybir.dt.bfloat16
AF = mybir.ActivationFunctionType
ALU = mybir.AluOpType
AX = mybir.AxisListType

D = 1024
NDK = 8
H_RAW = 2730
HPAD = 2816
NHK = 22
NEXP = 8
LN_EPS = 1e-5
RMS_EPS = float(np.finfo(np.float32).eps)
LAMBDA_INIT = 0.8 - 0.6 * float(np.exp(-0.3 * 1))
NHEAD = 8
HD = 64
VD = 128
VDE = 129
PBF_COLS = 65536
PF_COLS = 18944

ENGINES = ("sync", "tensor", "scalar", "vector", "gpsimd")


class Op:
    __slots__ = ("fn", "waits", "inc", "tag")

    def __init__(self, fn, waits, inc, tag):
        self.fn = fn
        self.waits = waits
        self.inc = inc
        self.tag = tag


class MK:

    def __init__(self, T=2048, n_sub=4):
        assert T % 256 == 0
        self.T = T
        self.NLC = T // 128
        self.LT = min(512, T)
        self.NLT = T // self.LT
        self.n_sub = n_sub
        self.ops = {e: [] for e in ENGINES}
        self.nbar = 0

    def op(self, eng, fn, waits=(), inc=None, tag=""):
        w = list(waits)
        pend = getattr(self, "_pending_first_wait", None)
        if pend and eng in pend and fn is not None:
            w = pend.pop(eng) + w
        self.ops[eng].append(Op(fn, w, inc, tag))

    def barrier(self):
        i = self.nbar
        self.nbar += 1
        target = 5 * (i + 1)
        extra = list(self.xspill_keys.get(0, ())) + list(getattr(self, "nspill_keys", []))
        for e in ENGINES:
            ex = extra if e in ("vector", "scalar", "sync") else []
            self.op(e, None, inc=("bar", f"bar{i}_{e}", 1), tag=f"barrier{i}i")
            self.op(e, None, waits=[("bar", ("BARTARGET", target))] + ex,
                    tag=f"barrier{i}w")

    def build(self):
        T, NLC, LT, NLT = self.T, self.NLC, self.LT, self.NLT
        nc = bass.Bass("TRN2", target_bir_lowering=False)
        self.nc = nc

        dt = {}
        dt["x"] = nc.dram_tensor("x", [T, D], F32, kind="ExternalInput").ap()
        for w in ("aWq", "aWk", "aWv", "aWo"):
            dt[w] = nc.dram_tensor(w, [D, D], F32, kind="ExternalInput").ap()
        for v in ("bo_b", "gamma_b", "beta_b"):
            dt[v] = nc.dram_tensor(v, [128, D], F32, kind="ExternalInput").ap()
        for w in ("bWq", "bWk", "bWv", "bWp"):
            dt[w] = nc.dram_tensor(w, [D, D], F32, kind="ExternalInput").ap()
        dt["lam"] = nc.dram_tensor("lam", [4, 32], F32, kind="ExternalInput").ap()
        dt["Wr"] = nc.dram_tensor("Wr", [D, NEXP], F32, kind="ExternalInput").ap()
        dt["br_b"] = nc.dram_tensor("br_b", [128, NEXP], F32, kind="ExternalInput").ap()
        dt["W1"] = nc.dram_tensor("W1", [NEXP, D, HPAD], F32, kind="ExternalInput").ap()
        dt["W2"] = nc.dram_tensor("W2", [NEXP, D, HPAD], F32, kind="ExternalInput").ap()
        dt["W3"] = nc.dram_tensor("W3", [NEXP, HPAD, D], F32, kind="ExternalInput").ap()
        dt["b12"] = nc.dram_tensor("b12", [128, 2 * NEXP * NHK], F32, kind="ExternalInput").ap()
        dt["b3"] = nc.dram_tensor("b3", [NEXP, D], F32, kind="ExternalInput").ap()
        dt["cs_tab"] = nc.dram_tensor("cs_tab", [T, 1024], F32, kind="ExternalInput").ap()
        dt["masks"] = nc.dram_tensor("masks", [4 * 128, 512], BF16, kind="ExternalInput").ap()
        dt["y"] = nc.dram_tensor("y", [T, D], F32, kind="ExternalOutput").ap()
        dt["xspill"] = nc.dram_tensor("xspill", [T, D], F32, kind="Internal").ap()
        dt["nspill"] = nc.dram_tensor("nspill", [T, D], BF16, kind="Internal").ap()
        dt["qspill"] = nc.dram_tensor("qspill", [T, D], BF16, kind="Internal").ap()
        dt["kspill"] = nc.dram_tensor("kspill", [T, D], BF16, kind="Internal").ap()
        dt["yspill"] = nc.dram_tensor("yspill", [T, D], BF16, kind="Internal").ap()
        dt["gspill"] = nc.dram_tensor("gspill", [T, 128], BF16, kind="Internal").ap()
        for w in ("bWq", "bWk", "bWv", "bWp"):
            dt[w + "s"] = nc.dram_tensor(w + "s", [D, D], BF16, kind="Internal").ap()
        self.dt = dt

        self.pbf = nc.alloc_sbuf_tensor("pbf", [128, PBF_COLS], BF16)
        self.pf = nc.alloc_sbuf_tensor("pf", [128, PF_COLS], F32)
        self.ones_bf = nc.alloc_sbuf_tensor("ones_bf", [128, 8], BF16)
        self.ones_f = nc.alloc_sbuf_tensor("ones_f", [1, 128], F32)
        self.means = nc.alloc_sbuf_tensor("means", [128, self.NLC], F32)
        self.rstds = nc.alloc_sbuf_tensor("rstds", [128, self.NLC], F32)
        self.stat = nc.alloc_sbuf_tensor("stat", [128, 64], F32)
        self.b12_sb = nc.alloc_sbuf_tensor("b12_sb", [128, 2 * NEXP * NHK], F32)
        self.br_sb = nc.alloc_sbuf_tensor("br_sb", [128, NEXP], F32)
        self.lam_sb = [nc.alloc_sbuf_tensor(f"lam{i}", [1, 32], F32) for i in range(4)]
        self.lam_t = nc.alloc_sbuf_tensor("lam_t", [1, 32], F32)
        self.lam_col = nc.alloc_sbuf_tensor("lam_col", [128, 1], F32)
        self.epst = nc.alloc_sbuf_tensor("epst", [128, 4], F32)

        self.P = [nc.alloc_psum_tensor(f"P{i}", [128, 1024], F32) for i in range(4)]

        self.sems = {}

        for w in ("bWq", "bWk", "bWv", "bWp"):
            self.op("gpsimd", lambda e, w=w: e.dma_start(
                out=dt[w + "s"][:], in_=dt[w][:]),
                inc=("wbcpre" + w, ("precast", w), 16))
        self.precast_keys = {w: ("wbcpre" + w, ("precast", w))
                             for w in ("bWq", "bWk", "bWv", "bWp")}

        def init_eps(e):
            e.memset(self.epst[:, 0:1], LN_EPS)
            e.memset(self.epst[:, 1:2], RMS_EPS)
            e.memset(self.epst[:, 2:3], float(np.log(0.125)))
            return e.memset(self.epst[:, 3:4], 0.0)
        self.op("vector", init_eps, inc=("dve", ("init",), 1))
        self._pending_first_wait = {"scalar": [("dve", ("init",))]}

        self.xspill_keys = {}
        self.nspill_keys = []
        self.prev_fused = False
        src = dt["x"]
        for sub in range(self.n_sub):
            kind = ("attn_a", "ffn", "attn_b", "ffn")[sub]
            last = (sub == self.n_sub - 1)
            dst = dt["y"] if last else dt["xspill"]
            if kind == "attn_a":
                self.phase_attn_a(src, dst)
            elif kind == "ffn":
                self.phase_ffn(src, dst)
            else:
                self.phase_attn_b(src, dst)
            src = dt["xspill"]
            self.barrier()

        self.op("sync", None, waits=[("xw0", ("FINALSEM", "xw0")), ("xw1", ("FINALSEM", "xw1"))],
                tag="final_store_wait")

        self.finalize()
        return nc

    def finalize(self):
        counters = {}
        keyval = {}
        for e in ENGINES:
            for o in self.ops[e]:
                if o.inc is not None:
                    s, key, n = o.inc
                    counters[s] = counters.get(s, 0) + n
                    assert key not in keyval, f"dup key {key}"
                    keyval[key] = (s, counters[s])

        def resolve(w):
            s, key = w
            if isinstance(key, tuple) and key and key[0] == "BARTARGET":
                return s, key[1]
            if isinstance(key, tuple) and key and key[0] == "FINALSEM":
                return s, counters.get(s, 0)
            assert key in keyval, f"unresolved wait key {key}"
            ks, v = keyval[key]
            assert ks == s, f"wait {key} on sem {s} but inc was on {ks}"
            return s, v

        nc = self.nc
        CHAINED = ("vector", "scalar", "gpsimd")
        for e in CHAINED:
            counters.setdefault("C" + e, 0)
        for s in counters:
            self.sems[s] = nc.semaphore(s).__enter__()

        _EMIT = {
            "tensor_scalar", "tensor_tensor", "tensor_copy", "tensor_add",
            "tensor_sub", "tensor_mul", "tensor_max", "reduce_sum",
            "reduce_max", "tensor_reduce", "reciprocal", "memset", "memzero",
            "scalar_tensor_tensor", "tensor_single_scalar", "select",
            "copy_predicated", "bn_stats", "bn_aggr", "activation", "copy",
            "mul", "add", "square", "sqrt", "sign", "iota", "affine_select",
            "tensor_scalar_mul", "tensor_scalar_add", "tensor_relu",
        }

        class _Chained:
            def __init__(self, eng, sem, holder):
                self._e = eng
                self._sem = sem
                self._h = holder

            def __getattr__(self, name):
                f = getattr(self._e, name)
                if name == "dma_start":
                    def g(*a, **k):
                        if self._h[0]:
                            self._e.wait_ge(self._sem, self._h[0])
                        return f(*a, **k)
                    return g
                if name in _EMIT:
                    def g(*a, **k):
                        if self._h[0]:
                            self._e.wait_ge(self._sem, self._h[0])
                        inst = f(*a, **k)
                        inst.then_inc(self._sem, 1)
                        self._h[0] += 1
                        return inst
                    return g
                return f

        with nc.Block() as block:
            for e in ENGINES:
                ops = self.ops[e]
                chained = e in CHAINED

                def body(eng, ops=ops, chained=chained, e=e):
                    holder = [0]
                    proxy = _Chained(eng, self.sems["C" + e], holder) if chained else eng
                    for o in ops:
                        for w in o.waits:
                            s, v = resolve(w)
                            eng.wait_ge(self.sems[s], v)
                        h0 = holder[0]
                        inst = o.fn(proxy) if o.fn is not None else None
                        if o.inc is not None:
                            s, key, n = o.inc
                            if chained and inst is not None and holder[0] > h0:
                                eng.wait_ge(self.sems["C" + e], holder[0])
                                eng.sem_inc(self.sems[s], n)
                            else:
                                if inst is None:
                                    inst = eng.nop()
                                inst.then_inc(self.sems[s], n)

                getattr(block, e)(body)

    def snt(self, src, ph, xnt, xc, sq, xnb, xc_views=None, pre_normed=False):
        T, NLC = self.T, self.NLC
        dt, st = self.dt, self.stat
        if pre_normed:
            for dk in range(NDK):
                self.op("sync", lambda e, dk=dk: e.dma_start(
                    out=xnt[:, dk * T:(dk + 1) * T],
                    in_=self.dt["nspill"][:, dk * 128:(dk + 1) * 128], transpose=True),
                    waits=list(self.nspill_keys) if dk == 0 else [],
                    inc=("tl", (ph, "xnt", dk), 16), tag=f"snt tl{dk}")
            return
        if xc_views is not None:
            xcv = xc_views
        else:
            xcv = [xc[lc % 2] for lc in range(NLC)]
        for lc in range(NLC):
            b = lc % 2
            if xc_views is None:
                w = [("dve", (ph, "xnb", lc - 2)), ("act", (ph, "ssq", lc - 2))] if lc >= 2 else []
            else:
                w = [(f"xc{b}", (ph, "xc", lc - 2))] if lc >= 2 else []
            if src is self.dt["xspill"]:
                w.extend(self.xspill_keys.get(lc, ()))
            self.op("sync", lambda e, lc=lc, src=src: e.dma_start(
                out=xcv[lc][:], in_=src[lc * 128:(lc + 1) * 128, :]),
                waits=w, inc=(f"xc{b}", (ph, "xc", lc), 16), tag=f"snt xc{lc}")
            self.op("vector", lambda e, lc=lc: e.reduce_sum(
                st[:, 0:1], xcv[lc][:], axis=AX.X),
                waits=[(f"xc{b}", (ph, "xc", lc))], inc=("dve", (ph, "s", lc), 1))
            self.op("scalar", lambda e, lc=lc: e.activation(
                sq[:], xcv[lc][:], AF.Square, accum_out=st[:, 1:2]),
                waits=[(f"xc{b}", (ph, "xc", lc))] + ([("dve", (ph, "var", lc - 1))] if lc >= 1 else []),
                inc=("act", (ph, "ssq", lc), 1))
            self.op("vector", lambda e, lc=lc: e.tensor_scalar(
                self.means[:, lc:lc + 1], st[:, 0:1], 1.0 / D, None, ALU.mult),
                waits=[("dve", (ph, "s", lc))])
            self.op("vector", lambda e: e.tensor_scalar(
                st[:, 2:3], st[:, 1:2], 1.0 / D, None, ALU.mult),
                waits=[("act", (ph, "ssq", lc))])
            self.op("vector", lambda e, lc=lc: e.tensor_tensor(
                st[:, 3:4], self.means[:, lc:lc + 1], self.means[:, lc:lc + 1], ALU.mult))
            self.op("vector", lambda e: e.tensor_tensor(
                st[:, 4:5], st[:, 2:3], st[:, 3:4], ALU.subtract),
                inc=("dve", (ph, "var", lc), 1))
            self.op("scalar", lambda e: e.activation(
                st[:, 5:6], st[:, 4:5], AF.Ln, bias=self.epst[:, 0:1]),
                waits=[("dve", (ph, "var", lc))])
            self.op("scalar", lambda e, lc=lc: e.activation(
                self.rstds[:, lc:lc + 1], st[:, 5:6], AF.Exp, scale=-0.5),
                inc=("act", (ph, "rstd", lc), 1))
            self.op("vector", lambda e, lc=lc, b=b: e.tensor_scalar(
                xnb[b][:], xcv[lc][:], self.means[:, lc:lc + 1],
                self.rstds[:, lc:lc + 1], ALU.subtract, ALU.mult),
                waits=[("act", (ph, "rstd", lc))] + ([(f"nsp{b}", (ph, "nsp", lc - 2))] if lc >= 2 else []),
                inc=("dve", (ph, "xnb", lc), 1))
            self.op("gpsimd", lambda e, lc=lc, b=b: e.dma_start(
                out=self.dt["nspill"][lc * 128:(lc + 1) * 128, :], in_=xnb[b][:]),
                waits=[("dve", (ph, "xnb", lc))], inc=(f"nsp{b}", (ph, "nsp", lc), 16))
        for dk in range(NDK):
            self.op("sync", lambda e, dk=dk: e.dma_start(
                out=xnt[:, dk * T:(dk + 1) * T],
                in_=self.dt["nspill"][:, dk * 128:(dk + 1) * 128], transpose=True),
                waits=[("nsp0", (ph, "nsp", NLC - 2)), ("nsp1", (ph, "nsp", NLC - 1))],
                inc=("tl", (ph, "xnt", dk), 16), tag=f"snt tl{dk}")

    def w_xnt(self, ph):
        return [("tl", (ph, "xnt", NDK - 1))]

    def ep_ln(self, ph, lc, xv, xnb, sq, ready_key, dst):
        if dst is not self.dt["xspill"]:
            return
        st = self.stat
        b = lc % 2
        self.op("vector", lambda e, xv=xv: e.reduce_sum(st[:, 32:33], xv[:], axis=AX.X),
                waits=[ready_key])
        wsq = [ready_key]
        if lc >= 1:
            wsq.append(("dve", (ph, "fvar", lc - 1)))
        self.op("scalar", lambda e, xv=xv, sq=sq: e.activation(
            sq[:], xv[:], AF.Square, accum_out=st[:, 33:34]),
            waits=wsq, inc=("act", (ph, "fssq", lc), 1))
        def fstats(e, lc=lc):
            e.tensor_scalar(self.means[:, lc:lc + 1], st[:, 32:33], 1.0 / D, None, ALU.mult)
            e.tensor_scalar(st[:, 34:35], st[:, 33:34], 1.0 / D, None, ALU.mult)
            e.tensor_tensor(st[:, 35:36], self.means[:, lc:lc + 1],
                            self.means[:, lc:lc + 1], ALU.mult)
            return e.tensor_tensor(st[:, 36:37], st[:, 34:35], st[:, 35:36], ALU.subtract)
        self.op("vector", fstats, waits=[("act", (ph, "fssq", lc))],
                inc=("dve", (ph, "fvar", lc), 1))
        def frstd(e, lc=lc):
            e.activation(st[:, 37:38], st[:, 36:37], AF.Ln, bias=self.epst[:, 0:1])
            return e.activation(self.rstds[:, lc:lc + 1], st[:, 37:38], AF.Exp, scale=-0.5)
        wr = [("dve", (ph, "fvar", lc))]
        if lc >= 1:
            wr.append(("dve", (ph, "fxn", lc - 1)))
        self.op("scalar", frstd, waits=wr, inc=("act", (ph, "frstd", lc), 1))
        wx = [("act", (ph, "frstd", lc))]
        if lc >= 2:
            wx.append((f"nsp{b}", (ph, "fnsp", lc - 2)))
        self.op("vector", lambda e, lc=lc, b=b, xv=xv: e.tensor_scalar(
            xnb[b][:], xv[:], self.means[:, lc:lc + 1],
            self.rstds[:, lc:lc + 1], ALU.subtract, ALU.mult),
            waits=wx, inc=("dve", (ph, "fxn", lc), 1))
        self.op("gpsimd", lambda e, lc=lc, b=b: e.dma_start(
            out=self.dt["nspill"][lc * 128:(lc + 1) * 128, :], in_=xnb[b][:]),
            waits=[("dve", (ph, "fxn", lc))], inc=(f"nsp{b}", (ph, "fnsp", lc), 16))
        if lc == self.NLC - 1:
            self.nspill_keys = [("nsp0", (ph, "fnsp", self.NLC - 2)),
                                ("nsp1", (ph, "fnsp", self.NLC - 1))]

    def set_xspill_totals(self, ph, dst):
        if dst is self.dt["xspill"]:
            NLC = self.NLC
            tot = [(f"xw{(NLC - 2) % 2}", (ph, "xnew", NLC - 2)),
                   (f"xw{(NLC - 1) % 2}", (ph, "xnew", NLC - 1))]
            self.xspill_keys = {lc: tot for lc in range(NLC)}

    def phase_attn_a(self, src, dst):
        T, NLC, LT, NLT = self.T, self.NLC, self.LT, self.NLT
        ph = f"A{self.nbar}"
        nc, dt, P = self.nc, self.dt, self.P
        st = self.stat

        o = 0
        def carve(n):
            nonlocal o
            v = self.pbf[:, o:o + n]
            o += n
            return v
        xnt = carve(NDK * T)
        v_sb = carve(NDK * T)
        eq_sb = v_sb
        ek_sb = carve(NDK * T)
        at_sb = ek_sb
        wW = carve(NDK * 1024)
        ctx_sb = carve(NDK * 1024)
        xnb = [ctx_sb[:, 0:1024], ctx_sb[:, 1024:2048]]
        assert o <= PBF_COLS, o

        of = 0
        def carvef(n):
            nonlocal of
            v = self.pf[:, of:of + n]
            of += n
            return v
        xc = [carvef(D), carvef(D)]
        sq = carvef(D)
        rzk = carvef(NEXP)
        zql = carvef(NLC)
        t2 = [carvef(D), carvef(D)]
        zbuf = [carvef(D), carvef(D)]
        ubuf = [carvef(D), carvef(D)]
        wbuf = carvef(D)
        v2buf = carvef(D)
        xc2 = [carvef(D), carvef(D)]
        xnew = [carvef(D), carvef(D)]
        bo_b = carvef(D)
        gamma_b = carvef(D)
        beta_b = carvef(D)
        assert of <= PF_COLS, of

        self.op("vector", lambda e: e.memset(self.ones_bf[:], 1.0))
        self.op("vector", lambda e: e.memset(self.ones_f[:], 1.0),
                inc=("dve", (ph, "ones"), 1))
        for name, buf in (("bo_b", bo_b), ("gamma_b", gamma_b), ("beta_b", beta_b)):
            self.op("sync", lambda e, name=name, buf=buf: e.dma_start(
                out=buf[:], in_=dt[name][:]), inc=("vec", (ph, name), 16))

        self.snt(src, ph, xnt, xc, sq, xnb)

        def wcast(wname, key, waits=()):
            self.op("gpsimd", lambda e, wname=wname: e.dma_start(
                out=wW.rearrange("p (c e) -> p c e", c=NDK),
                in_=dt[wname].rearrange("(c p) e -> p c e", p=128)),
                waits=list(waits), inc=("wcast", (ph, key), 16), tag=f"wcast {key}")
        wcast("aWv", "wv")

        def bank4(i):
            t = P[(i % 4) // 2]
            return t[:, (i % 2) * 512:(i % 2) * 512 + 512]
        def bank2(i):
            return P[2][:, (i % 2) * 512:(i % 2) * 512 + 512]
        zk_ps = P[3][:, 512:513]
        zq_ps = P[3][:, 0:1]

        nvt = NLC * 2
        for i in range(nvt):
            lc, et = divmod(i, 2)
            def mmv(e, lc=lc, et=et, i=i):
                ps = bank4(i)
                for dk in range(NDK):
                    inst = e.matmul(ps[:], xnt[:, dk * T + lc * 128: dk * T + lc * 128 + 128],
                                    wW[:, dk * 1024 + et * 512: dk * 1024 + et * 512 + 512],
                                    start=(dk == 0), stop=(dk == NDK - 1))
                return inst
            w = []
            if i == 0:
                w = self.w_xnt(ph) + [("wcast", (ph, "wv"))]
            if i >= 4:
                w.append(("act", (ph, "vev", i - 4)))
            self.op("tensor", mmv, waits=w, inc=("pe", (ph, "vmm", i), 1))
            self.op("scalar", lambda e, lc=lc, et=et, i=i: e.activation(
                v_sb[:, lc * 1024 + et * 512: lc * 1024 + et * 512 + 512],
                bank4(i)[:], AF.Copy),
                waits=[("pe", (ph, "vmm", i))], inc=("act", (ph, "vev", i), 1))
        wcast("aWk", "wk", waits=[("pe", (ph, "vmm", nvt - 1))])

        for i in range(nvt):
            lc, et = divmod(i, 2)
            def mmk(e, lc=lc, et=et, i=i):
                ps = bank4(i)
                for dk in range(NDK):
                    inst = e.matmul(ps[:], xnt[:, dk * T + lc * 128: dk * T + lc * 128 + 128],
                                    wW[:, dk * 1024 + et * 512: dk * 1024 + et * 512 + 512],
                                    start=(dk == 0), stop=(dk == NDK - 1))
                return inst
            w = [("wcast", (ph, "wk"))] if i == 0 else []
            if i >= 4:
                w.append(("act", (ph, "kev", i - 4)))
            elif i < 4:
                w.append(("act", (ph, "vev", nvt - 1)) if i == 0 else None)
            w = [x for x in w if x]
            self.op("tensor", mmk, waits=w, inc=("pe", (ph, "kmm", i), 1))
            self.op("scalar", lambda e, lc=lc, et=et, i=i: e.activation(
                ek_sb[:, lc * 1024 + et * 512: lc * 1024 + et * 512 + 512],
                bank4(i)[:], AF.Exp),
                waits=[("pe", (ph, "kmm", i))], inc=("act", (ph, "kev", i), 1))
        wcast("aWq", "wq", waits=[("pe", (ph, "kmm", nvt - 1))])

        for ek in range(NEXP):
            def mmz(e, ek=ek):
                for lc in range(NLC):
                    inst = e.matmul(zk_ps, ek_sb[:, lc * 1024 + ek * 128: lc * 1024 + ek * 128 + 128],
                                    self.ones_bf[:, 0:1], start=(lc == 0), stop=(lc == NLC - 1))
                return inst
            w = [("act", (ph, "kev", nvt - 1)), ("dve", (ph, "ones"))] if ek == 0 else [("dve", (ph, "zkev", ek - 1))]
            self.op("tensor", mmz, waits=w, inc=("pe", (ph, "zk", ek), 1))
            self.op("vector", lambda e, ek=ek: e.tensor_copy(rzk[:, ek:ek + 1], zk_ps),
                    waits=[("pe", (ph, "zk", ek))], inc=("dve", (ph, "zkev", ek), 1))
        self.op("vector", lambda e: e.reciprocal(rzk[:], rzk[:]),
                waits=[("dve", (ph, "zkev", NEXP - 1))], inc=("dve", (ph, "rzk"), 1))

        nct = NDK * 2
        for i in range(nct):
            dk, evt = divmod(i, 2)
            def mmc(e, dk=dk, evt=evt, i=i):
                ps = bank2(i)
                for lc in range(NLC):
                    inst = e.matmul(ps[:], ek_sb[:, lc * 1024 + dk * 128: lc * 1024 + dk * 128 + 128],
                                    v_sb[:, lc * 1024 + evt * 512: lc * 1024 + evt * 512 + 512],
                                    start=(lc == 0), stop=(lc == NLC - 1))
                return inst
            w = [("dve", (ph, "rzk"))] if i == 0 else []
            if i >= 2:
                w.append(("dve", (ph, "cev", i - 2)))
            self.op("tensor", mmc, waits=w, inc=("pe", (ph, "cmm", i), 1))
            self.op("vector", lambda e, dk=dk, evt=evt, i=i: e.tensor_scalar(
                ctx_sb[:, dk * 1024 + evt * 512: dk * 1024 + evt * 512 + 512],
                bank2(i)[:], rzk[:, dk:dk + 1], None, ALU.mult),
                waits=[("pe", (ph, "cmm", i))], inc=("dve", (ph, "cev", i), 1))

        nqt = NDK * NLT
        for i in range(nqt):
            ek, lt = divmod(i, NLT)
            def mmq(e, ek=ek, lt=lt, i=i):
                ps = bank4(i)
                for dk in range(NDK):
                    inst = e.matmul(ps[:, :LT], wW[:, dk * 1024 + ek * 128: dk * 1024 + ek * 128 + 128],
                                    xnt[:, dk * T + lt * LT: dk * T + lt * LT + LT],
                                    start=(dk == 0), stop=(dk == NDK - 1))
                return inst
            w = []
            if i == 0:
                w = [("wcast", (ph, "wq"))]
            if i >= 4:
                w.append(("act", (ph, "qev", i - 4)))
            self.op("tensor", mmq, waits=w, inc=("pe", (ph, "qmm", i), 1))
            wq_ = [("pe", (ph, "qmm", i))]
            if i == 0:
                wq_.append(("pe", (ph, "cmm", nct - 1)))
            self.op("scalar", lambda e, ek=ek, lt=lt, i=i: e.activation(
                eq_sb[:, ek * T + lt * LT: ek * T + lt * LT + LT],
                bank4(i)[:, :LT], AF.Exp),
                waits=wq_, inc=("act", (ph, "qev", i), 1))
        wcast("aWo", "wo", waits=[("pe", (ph, "qmm", nqt - 1))])

        for j in range(NLC):
            def mmzq(e, j=j):
                for ek in range(NEXP):
                    inst = e.matmul(zq_ps, eq_sb[:, ek * T + j * 128: ek * T + j * 128 + 128],
                                    self.ones_bf[:, 0:1], start=(ek == 0), stop=(ek == NEXP - 1))
                return inst
            w = [("dve", (ph, "zqev", j - 1))] if j > 0 else [("act", (ph, "qev", nqt - 1))]
            self.op("tensor", mmzq, waits=w, inc=("pe", (ph, "zq", j), 1))
            self.op("vector", lambda e, j=j: e.tensor_copy(zql[:, j:j + 1], zq_ps),
                    waits=[("pe", (ph, "zq", j))], inc=("dve", (ph, "zqev", j), 1))
        def zq_fin(e):
            e.reciprocal(zql[:], zql[:])
            return e.tensor_scalar(zql[:], zql[:], 1.0 / 32.0, None, ALU.mult)
        self.op("vector", zq_fin, waits=[("dve", (ph, "zqev", NLC - 1))],
                inc=("dve", (ph, "zql"), 1))

        nat = NDK * NLT
        for i in range(nat):
            evk, lt = divmod(i, NLT)
            def mma(e, evk=evk, lt=lt, i=i):
                ps = bank2(i)
                for dk in range(NDK):
                    inst = e.matmul(ps[:, :LT], ctx_sb[:, dk * 1024 + evk * 128: dk * 1024 + evk * 128 + 128],
                                    eq_sb[:, dk * T + lt * LT: dk * T + lt * LT + LT],
                                    start=(dk == 0), stop=(dk == NDK - 1))
                return inst
            w = []
            if i == 0:
                w = [("act", (ph, "qev", nqt - 1)), ("dve", (ph, "cev", nct - 1))]
            if i >= 2:
                w.append(("act", (ph, "aev", i - 2)))
            self.op("tensor", mma, waits=w, inc=("pe", (ph, "amm", i), 1))
            wa_ = [("pe", (ph, "amm", i))]
            self.op("scalar", lambda e, evk=evk, lt=lt, i=i: e.activation(
                at_sb[:, evk * T + lt * LT: evk * T + lt * LT + LT],
                bank2(i)[:, :LT], AF.Copy),
                waits=wa_, inc=("act", (ph, "aev", i), 1))

        for lc in range(NLC):
            par = lc % 2
            pb = P[par]
            eb = par
            woxc = [("pool", (ph, "eplg", lc - 2))] if lc >= 2 else []
            if src is self.dt["xspill"]:
                woxc.extend(self.xspill_keys.get(lc, ()))
            self.op("sync", lambda e, lc=lc, eb=eb: e.dma_start(
                out=xc2[eb][:], in_=src[lc * 128:(lc + 1) * 128, :]),
                waits=woxc, inc=(f"oxc{eb}", (ph, "oxc", lc), 16))
            def mmo(e, lc=lc, pb=pb):
                for dt_ in range(2):
                    for evk in range(NEXP):
                        inst = e.matmul(pb[:, dt_ * 512:dt_ * 512 + 512],
                                        at_sb[:, evk * T + lc * 128: evk * T + lc * 128 + 128],
                                        wW[:, evk * 1024 + dt_ * 512: evk * 1024 + dt_ * 512 + 512],
                                        start=(evk == 0), stop=(evk == NEXP - 1))
                return inst
            w = []
            if lc == 0:
                w.append(("act", (ph, "aev", nat - 1)))
                w.append(("wcast", (ph, "wo")))
            if lc >= 2:
                w.append(("dve", (ph, "eplg_t2", lc - 2)))
            self.op("tensor", mmo, waits=w, inc=("pe", (ph, "omm", lc), 1))
            def ep_t2(e, lc=lc, eb=eb, pb=pb):
                e.tensor_scalar(t2[eb][:], xc2[eb][:], self.means[:, lc:lc + 1],
                                self.rstds[:, lc:lc + 1], ALU.subtract, ALU.mult)
                return e.scalar_tensor_tensor(t2[eb][:], pb[:], zql[:, lc:lc + 1],
                                              t2[eb][:], ALU.mult, ALU.add)
            self.op("vector", ep_t2,
                    waits=[("pe", (ph, "omm", lc)), (f"oxc{eb}", (ph, "oxc", lc))] +
                          ([("dve", (ph, "zql"))] if lc == 0 else []),
                    inc=("dve", (ph, "eplg_t2", lc), 1))
            def ep_z(e, lc=lc, eb=eb):
                return e.tensor_tensor(zbuf[eb][:], t2[eb][:], bo_b[:], ALU.add)
            wz_ = [("dve", (ph, "eplg_t2", lc))]
            if lc == 0:
                wz_.append(("vec", (ph, "beta_b")))
            if lc >= 2:
                wz_.append(("dve", (ph, "u", lc - 2)))
            self.op("gpsimd", ep_z, waits=wz_, inc=("pool", (ph, "z", lc), 1))
            self.op("vector", lambda e, eb=eb: e.reduce_sum(
                st[:, 8:9], zbuf[eb][:], axis=AX.X),
                waits=[("pool", (ph, "z", lc))])
            wsq_ = [("pool", (ph, "z", lc))]
            if lc >= 1:
                wsq_.append(("dve", (ph, "var2", lc - 1)))
            self.op("scalar", lambda e, eb=eb: e.activation(
                sq[:], zbuf[eb][:], AF.Square, accum_out=st[:, 9:10]),
                waits=wsq_, inc=("act", (ph, "zsq", lc), 1))
            def ep_stats(e):
                e.tensor_scalar(st[:, 10:11], st[:, 8:9], 1.0 / D, None, ALU.mult)
                e.tensor_scalar(st[:, 11:12], st[:, 9:10], 1.0 / D, None, ALU.mult)
                e.tensor_tensor(st[:, 12:13], st[:, 10:11], st[:, 10:11], ALU.mult)
                return e.tensor_tensor(st[:, 13:14], st[:, 11:12], st[:, 12:13], ALU.subtract)
            self.op("vector", ep_stats, waits=[("act", (ph, "zsq", lc))],
                    inc=("dve", (ph, "var2", lc), 1))
            def ep_rstd(e):
                e.activation(st[:, 14:15], st[:, 13:14], AF.Ln, bias=self.epst[:, 0:1])
                return e.activation(st[:, 15:16], st[:, 14:15], AF.Exp, scale=-0.5)
            wr_ = [("dve", (ph, "var2", lc))]
            if lc >= 1:
                wr_.append(("dve", (ph, "u", lc - 1)))
            self.op("scalar", ep_rstd, waits=wr_, inc=("act", (ph, "rstd2", lc), 1))
            def ep_u(e, lc=lc, eb=eb):
                return e.tensor_scalar(ubuf[eb][:], zbuf[eb][:], st[:, 10:11], st[:, 15:16],
                                       ALU.subtract, ALU.mult)
            wu_ = [("act", (ph, "rstd2", lc))]
            if lc >= 2:
                wu_.append(("pool", (ph, "eplg", lc - 2)))
            self.op("vector", ep_u, waits=wu_, inc=("dve", (ph, "u", lc), 1))
            def ep_fin(e, lc=lc, eb=eb):
                e.tensor_tensor(wbuf[:], ubuf[eb][:], gamma_b[:], ALU.mult)
                e.tensor_tensor(v2buf[:], xc2[eb][:], beta_b[:], ALU.add)
                return e.tensor_tensor(xnew[eb][:], wbuf[:], v2buf[:], ALU.add)
            w2_ = [("dve", (ph, "u", lc))]
            if lc >= 2:
                w2_.append((f"xw{eb}", (ph, "xnew", lc - 2)))
                if dst is self.dt["xspill"]:
                    w2_.append(("act", (ph, "fssq", lc - 2)))
                    w2_.append(("dve", (ph, "fxn", lc - 2)))
            self.op("gpsimd", ep_fin, waits=w2_, inc=("pool", (ph, "eplg", lc), 1))
            self.op("gpsimd", lambda e, lc=lc, eb=eb, dst=dst: e.dma_start(
                out=dst[lc * 128:(lc + 1) * 128, :], in_=xnew[eb][:]),
                waits=[], inc=(f"xw{eb}", (ph, "xnew", lc), 16))
            self.ep_ln(ph, lc, xnew[eb], xnb, sq, ("pool", (ph, "eplg", lc)), dst)
        self.set_xspill_totals(ph, dst)
        self.prev_fused = dst is self.dt["xspill"]

    def phase_ffn(self, src, dst):
        T, NLC, LT, NLT = self.T, self.NLC, self.LT, self.NLT
        ph = f"F{self.nbar}"
        nc, dt, P = self.nc, self.dt, self.P
        st = self.stat

        slices = [(0, 4), (4, 8), (8, 12), (12, 16), (16, 20), (20, 22)]
        NS = len(slices)
        NSI = NEXP * NS

        o = 0
        def carve(n):
            nonlocal o
            v = self.pbf[:, o:o + n]
            o += n
            return v
        xnt = carve(NDK * T)
        wf = [carve(3 * 4 * 1024) for _ in range(2)]
        hbuf = [carve(4 * T) for _ in range(2)]
        gT = carve(T)
        b3bf = carve(D)
        wr_bf = carve(NDK * 8)
        gbf = carve(128)
        xnb_cols = carve(2 * 1024)
        xnb = [xnb_cols[:, 0:1024], xnb_cols[:, 1024:2048]]
        assert o <= PBF_COLS, o

        of = 0
        def carvef(n):
            nonlocal of
            v = self.pf[:, of:of + n]
            of += n
            return v
        acc = carvef(NLC * 1024)
        s_sc = [carvef(512), carvef(512)]
        sq = self.pf[:, of - 1024:of]
        logits = carvef(NLC * NEXP)
        gates = carvef(NLC * NEXP)
        mask1 = carvef(NLC * NEXP)
        l2 = carvef(NLC * NEXP)
        gm = carvef(8 * NLC)
        assert of <= PF_COLS, of
        m1 = gm[:, 0 * NLC:1 * NLC]
        m2 = gm[:, 1 * NLC:2 * NLC]
        ddt = gm[:, 2 * NLC:3 * NLC]
        e12 = gm[:, 3 * NLC:4 * NLC]
        w1 = gm[:, 4 * NLC:5 * NLC]
        w2 = gm[:, 5 * NLC:6 * NLC]
        w1d = gm[:, 6 * NLC:7 * NLC]

        self.op("sync", lambda e: e.dma_start(out=self.b12_sb[:], in_=dt["b12"][:]),
                inc=("vec", (ph, "b12"), 16))
        self.op("sync", lambda e: e.dma_start(out=self.br_sb[:], in_=dt["br_b"][:]),
                inc=("vec", (ph, "br"), 16))
        if self.prev_fused:
            self.snt(src, ph, xnt, None, sq, xnb, pre_normed=True)
            for lc in range(NLC):
                b = lc % 2
                wacc = [(f"xc{b}", (ph, "xc", lc - 2))] if lc >= 2 else []
                wacc.extend(self.xspill_keys.get(lc, ()))
                self.op("sync", lambda e, lc=lc: e.dma_start(
                    out=acc[:, lc * 1024:(lc + 1) * 1024],
                    in_=src[lc * 128:(lc + 1) * 128, :]),
                    waits=wacc, inc=(f"xc{b}", (ph, "xc", lc), 16))
        else:
            self.snt(src, ph, xnt, None, sq, xnb,
                     xc_views=[acc[:, lc * 1024:(lc + 1) * 1024] for lc in range(NLC)])

        self.op("gpsimd", lambda e: e.dma_start(
            out=wr_bf.rearrange("p (c e) -> p c e", c=NDK),
            in_=dt["Wr"].rearrange("(c p) e -> p c e", p=128)),
            inc=("wcx", (ph, "wr"), 16))
        self.op("gpsimd", lambda e: e.dma_start(out=b3bf[0:NEXP, :], in_=dt["b3"][:]),
                inc=("wcx", (ph, "b3"), 16))

        for lc in range(NLC):
            zps = P[3][:, (lc % 2) * 512:(lc % 2) * 512 + 8]
            def mml(e, lc=lc, zps=zps):
                for dk in range(NDK):
                    inst = e.matmul(zps, xnt[:, dk * T + lc * 128: dk * T + lc * 128 + 128],
                                    wr_bf[:, dk * 8:dk * 8 + 8],
                                    start=(dk == 0), stop=(dk == NDK - 1))
                return inst
            w = []
            if lc == 0:
                w = self.w_xnt(ph) + [("wcx", (ph, "b3"))]
            if lc >= 2:
                w.append(("dve", (ph, "lgev", lc - 2)))
            self.op("tensor", mml, waits=w, inc=("pe", (ph, "lgmm", lc), 1))
            self.op("vector", lambda e, lc=lc, zps=zps: e.tensor_tensor(
                logits[:, lc * 8:lc * 8 + 8], zps, self.br_sb[:], ALU.add),
                waits=[("pe", (ph, "lgmm", lc))] + ([("vec", (ph, "br"))] if lc == 0 else []),
                inc=("dve", (ph, "lgev", lc), 1))
        lg3 = logits.rearrange("p (lc e) -> p lc e", lc=NLC)
        def g_m1(e):
            return e.reduce_max(m1[:], lg3, axis=AX.X)
        self.op("vector", g_m1, waits=[("dve", (ph, "lgev", NLC - 1))])
        def g_mask1(e):
            m1b = m1[:].unsqueeze(2).broadcast_to([128, NLC, 8])
            return e.tensor_tensor(mask1.rearrange("p (lc e) -> p lc e", lc=NLC),
                                   logits.rearrange("p (lc e) -> p lc e", lc=NLC),
                                   m1b, ALU.is_equal)
        self.op("vector", g_mask1)
        def g_l2(e):
            return e.scalar_tensor_tensor(l2[:], mask1[:], -1e30, logits[:], ALU.mult, ALU.add)
        self.op("vector", g_l2)
        def g_m2(e):
            return e.reduce_max(m2[:], l2.rearrange("p (lc e) -> p lc e", lc=NLC), axis=AX.X)
        self.op("vector", g_m2)
        def g_dd(e):
            return e.tensor_tensor(ddt[:], m1[:], m2[:], ALU.subtract)
        self.op("vector", g_dd, inc=("dve", (ph, "gdd"), 1))
        def g_exp(e):
            return e.activation(e12[:], ddt[:], AF.Exp, scale=-1.0)
        self.op("scalar", g_exp, waits=[("dve", (ph, "gdd"))], inc=("act", (ph, "ge"), 1))
        def g_w(e):
            e.tensor_scalar(w1d[:], e12[:], 1.0, None, ALU.add)
            e.reciprocal(w1[:], w1d[:])
            return e.tensor_tensor(w2[:], e12[:], w1[:], ALU.mult)
        self.op("vector", g_w, waits=[("act", (ph, "ge"))])
        def g_gates(e):
            l23 = l2.rearrange("p (lc e) -> p lc e", lc=NLC)
            m13 = mask1.rearrange("p (lc e) -> p lc e", lc=NLC)
            g3 = gates.rearrange("p (lc e) -> p lc e", lc=NLC)
            gt3 = logits.rearrange("p (lc e) -> p lc e", lc=NLC)
            m2b = m2[:].unsqueeze(2).broadcast_to([128, NLC, 8])
            w1b = w1[:].unsqueeze(2).broadcast_to([128, NLC, 8])
            w2b = w2[:].unsqueeze(2).broadcast_to([128, NLC, 8])
            e.tensor_tensor(gt3, l23, m2b, ALU.is_equal)
            e.tensor_tensor(l23, gt3, w2b, ALU.mult)
            e.tensor_tensor(gt3, m13, w1b, ALU.mult)
            return e.tensor_tensor(g3, l23, gt3, ALU.add)
        self.op("vector", g_gates, inc=("dve", (ph, "gates"), 1))
        self.op("vector", lambda e: e.memset(gbf[:, 8:128], 0.0),
                inc=("dve", (ph, "gbf0"), 1))
        for lc in range(NLC):
            w = [("gsp", (ph, "gsp", lc - 1))] if lc >= 1 else [("dve", (ph, "gbf0"))]
            self.op("vector", lambda e, lc=lc: e.tensor_copy(
                gbf[:, 0:8], gates[:, lc * 8:lc * 8 + 8]),
                waits=w, inc=("dve", (ph, "gbf", lc), 1))
            self.op("gpsimd", lambda e, lc=lc: e.dma_start(
                out=dt["gspill"][lc * 128:(lc + 1) * 128, :], in_=gbf[:]),
                waits=[("dve", (ph, "gbf", lc))], inc=("gsp", (ph, "gsp", lc), 16))
        self.op("sync", lambda e: e.dma_start(
            out=gT[:, 0:T], in_=dt["gspill"][:], transpose=True),
            waits=[("gsp", (ph, "gsp", NLC - 1))], inc=("tl", (ph, "gT"), 16))

        def wslice(si):
            e_, s_ = divmod(si, NS)
            return e_, slices[s_]

        def emit_casts(si):
            ex, (h0, h1) = wslice(si)
            nch = h1 - h0
            par = si % 2
            wfb = wf[par]
            w1v = wfb[:, 0:nch * 1024]
            w2v = wfb[:, 4 * 1024:4 * 1024 + nch * 1024]
            w3v = wfb[:, 8 * 1024:8 * 1024 + nch * 1024]
            prev_ntile = ((slices[(si - 2) % NS][1] - slices[(si - 2) % NS][0]) * NLT
                          if si >= 2 else 0)
            w12_w = [("pe", (ph, "hmm", si - 2, prev_ntile - 1))] if si >= 2 else []
            w3_w = [("pe", (ph, "bmm", si - 2, NLC * 2 - 1))] if si >= 2 else []
            self.op("gpsimd", lambda e, ex=ex, h0=h0, nch=nch, w1v=w1v: e.dma_start(
                out=w1v.rearrange("p (c h) -> p c h", c=NDK),
                in_=dt["W1"][ex, :, h0 * 128:(h0 + nch) * 128]
                .rearrange("(c p) h -> p c h", p=128)),
                waits=w12_w, inc=(f"wa{par}", (ph, "wc1", si), 16))
            self.op("gpsimd", lambda e, ex=ex, h0=h0, nch=nch, w2v=w2v: e.dma_start(
                out=w2v.rearrange("p (c h) -> p c h", c=NDK),
                in_=dt["W2"][ex, :, h0 * 128:(h0 + nch) * 128]
                .rearrange("(c p) h -> p c h", p=128)),
                inc=(f"wa{par}", (ph, "wc12", si), 16))
            self.op("gpsimd", lambda e, ex=ex, h0=h0, nch=nch, w3v=w3v: e.dma_start(
                out=w3v.rearrange("p (c dd) -> p c dd", c=nch),
                in_=dt["W3"][ex, h0 * 128:(h0 + nch) * 128, :]
                .rearrange("(c p) dd -> p c dd", p=128)),
                waits=w3_w, inc=(f"wb{par}", (ph, "wc", si), 16))

        emit_casts(0)
        emit_casts(1)
        for si in range(NSI):
            ex, (h0, h1) = wslice(si)
            nch = h1 - h0
            par = si % 2
            wfb = wf[par]
            hb = hbuf[par]
            w1v = wfb[:, 0:nch * 1024]
            w2v = wfb[:, 4 * 1024:4 * 1024 + nch * 1024]
            w3v = wfb[:, 8 * 1024:8 * 1024 + nch * 1024]
            if si >= 2:
                emit_casts(si)

            ntile = nch * NLT
            for t in range(ntile):
                hk, lt = divmod(t, NLT)
                pa = P[t % 2][:, 0:LT]
                pb_ = P[t % 2][:, 512:512 + LT]
                b1col = self.b12_sb[:, (ex * 2 + 0) * NHK + h0 + hk:(ex * 2 + 0) * NHK + h0 + hk + 1]
                b2col = self.b12_sb[:, (ex * 2 + 1) * NHK + h0 + hk:(ex * 2 + 1) * NHK + h0 + hk + 1]
                def mmh(e, hk=hk, lt=lt, pa=pa, pb_=pb_, w1v=w1v, w2v=w2v, nch=nch):
                    cw = nch * 128
                    for dk in range(NDK):
                        e.matmul(pa, w1v[:, dk * cw + hk * 128: dk * cw + hk * 128 + 128],
                                 xnt[:, dk * T + lt * LT: dk * T + lt * LT + LT],
                                 start=(dk == 0), stop=(dk == NDK - 1))
                    for dk in range(NDK):
                        inst = e.matmul(pb_, w2v[:, dk * cw + hk * 128: dk * cw + hk * 128 + 128],
                                        xnt[:, dk * T + lt * LT: dk * T + lt * LT + LT],
                                        start=(dk == 0), stop=(dk == NDK - 1))
                    return inst
                w = []
                if t == 0:
                    w.append((f"wa{par}", (ph, "wc12", si)))
                    if si == 0:
                        w.append(("vec", (ph, "br")))
                    if si >= 2:
                        pass
                if t >= 2:
                    w.append(("dve", (ph, "hev", si, t - 2)))
                    w.append(("act", (ph, "sev", si, t - 2)))
                elif si >= 1:
                    pt = (si, t)
                    prev_nt = (slices[(si - 1) % NS][1] - slices[(si - 1) % NS][0]) * NLT
                    if prev_nt + t - 2 >= 0 and t < 2:
                        w.append(("dve", (ph, "hev", si - 1, prev_nt + t - 2)))
                        w.append(("act", (ph, "sev", si - 1, prev_nt + t - 2)))
                self.op("tensor", mmh, waits=w, inc=("pe", (ph, "hmm", si, t), 1))
                sb = s_sc[t % 2]
                def silu(e, pb_=pb_, sb=sb, b2col=b2col):
                    return e.activation(sb[:, 0:LT], pb_, AF.Silu, bias=b2col)
                self.op("scalar", silu, waits=[("pe", (ph, "hmm", si, t))],
                        inc=("act", (ph, "sev", si, t), 1))
                def hstt(e, hk=hk, lt=lt, pa=pa, sb=sb, b1col=b1col, hb=hb):
                    return e.scalar_tensor_tensor(
                        hb[:, hk * T + lt * LT: hk * T + lt * LT + LT],
                        pa, b1col, sb[:, 0:LT], ALU.add, ALU.mult)
                wh = [("act", (ph, "sev", si, t))]
                if t == 0 and si >= 2:
                    wh.append(("pe", (ph, "bmm", si - 2, NLC * 2 - 1)))
                self.op("vector", hstt, waits=wh, inc=("dve", (ph, "hev", si, t), 1))

            if si >= 1:
                self.emit_ffn_B(ph, si - 1, wslice, wf, hbuf, gates, acc, dst)
        self._ffn_lnbufs = (xnb, sq)
        self.emit_ffn_B(ph, NSI - 1, wslice, wf, hbuf, gates, acc, dst,
                        last_b3=True, gT=gT, b3bf=b3bf)

        self.set_xspill_totals(ph, dst)
        self.prev_fused = dst is self.dt["xspill"]

    def emit_b3(self, ph, lc, gT, b3bf, acc, dst):
        T, NLC, P = self.T, self.NLC, self.P
        for dtile in range(2):
            j = lc * 2 + dtile
            ps = P[j % 2][:, ((j // 2) % 2) * 512:((j // 2) % 2) * 512 + 512]
            def mmb3(e, lc=lc, dtile=dtile, ps=ps):
                return e.matmul(ps, gT[0:NEXP, lc * 128:(lc + 1) * 128],
                                b3bf[0:NEXP, dtile * 512:dtile * 512 + 512],
                                start=True, stop=True)
            w = []
            if j == 0:
                w = [("tl", (ph, "gT")), ("wcx", (ph, "b3"))]
            if j >= 4:
                w.append(("dve", (ph, "b3ev", j - 4)))
            self.op("tensor", mmb3, waits=w, inc=("pe", (ph, "b3mm", j), 1))
            def b3ev(e, lc=lc, dtile=dtile, ps=ps):
                a = acc[:, lc * 1024 + dtile * 512: lc * 1024 + dtile * 512 + 512]
                return e.tensor_tensor(a, a, ps, ALU.add)
            self.op("vector", b3ev, waits=[("pe", (ph, "b3mm", j))],
                    inc=("dve", (ph, "b3ev", j), 1))
        self.op("gpsimd", lambda e, lc=lc, dst=dst: e.dma_start(
            out=dst[lc * 128:(lc + 1) * 128, :], in_=acc[:, lc * 1024:(lc + 1) * 1024]),
            waits=[("dve", (ph, "b3ev", lc * 2 + 1))],
            inc=(f"xw{lc % 2}", (ph, "xnew", lc), 16))
        xnb, sq = self._ffn_lnbufs
        self.ep_ln(ph, lc, acc[:, lc * 1024:(lc + 1) * 1024], xnb, sq,
                   ("dve", (ph, "b3ev", lc * 2 + 1)), dst)

    def emit_ffn_B(self, ph, si, wslice, wf, hbuf, gates, acc, dst,
                   last_b3=False, gT=None, b3bf=None):
        T, NLC = self.T, self.NLC
        P = self.P
        ex, (h0, h1) = wslice(si)
        nch = h1 - h0
        par = si % 2
        w3v = wf[par][:, 8 * 1024:8 * 1024 + nch * 1024]
        hb = hbuf[par]
        ntile = nch * self.NLT
        for j in range(NLC * 2):
            lc, dtile = divmod(j, 2)
            ps = P[2 + (j % 4) // 2][:, (j % 2) * 512:(j % 2) * 512 + 512]
            def mmb(e, lc=lc, dtile=dtile, ps=ps, hb=hb, w3v=w3v, nch=nch):
                for hk in range(nch):
                    inst = e.matmul(ps, hb[:, hk * T + lc * 128: hk * T + lc * 128 + 128],
                                    w3v[:, hk * 1024 + dtile * 512: hk * 1024 + dtile * 512 + 512],
                                    start=(hk == 0), stop=(hk == nch - 1))
                return inst
            w = []
            if j == 0:
                w.append(("dve", (ph, "hev", si, ntile - 1)))
                w.append((f"wb{par}", (ph, "wc", si)))
                if si == 0:
                    w.append(("dve", (ph, "lgev", NLC - 1)))
            if j >= 4:
                w.append(("dve", (ph, "bev", si, j - 4)))
            elif si >= 1:
                w.append(("dve", (ph, "bev", si - 1, NLC * 2 - 4 + j)))
            self.op("tensor", mmb, waits=w, inc=("pe", (ph, "bmm", si, j), 1))
            gcol = gates[:, lc * 8 + ex: lc * 8 + ex + 1]
            def bev(e, lc=lc, dtile=dtile, ps=ps, gcol=gcol):
                a = acc[:, lc * 1024 + dtile * 512: lc * 1024 + dtile * 512 + 512]
                return e.scalar_tensor_tensor(a, ps, gcol, a, ALU.mult, ALU.add)
            wv = [("pe", (ph, "bmm", si, j))]
            if si == 0 and j == 0:
                wv.append((f"xc{(NLC - 1) % 2}", (ph, "xc", NLC - 1)))
                wv.append((f"xc{(NLC - 2) % 2}", (ph, "xc", NLC - 2)))
                wv.append(("dve", (ph, "gates")))
            self.op("vector", bev, waits=wv, inc=("dve", (ph, "bev", si, j), 1))
            if last_b3 and dtile == 1:
                self.emit_b3(ph, lc, gT, b3bf, acc, dst)

    def phase_attn_b(self, src, dst):
        T, NLC, LT, NLT = self.T, self.NLC, self.LT, self.NLT
        ph = f"B{self.nbar}"
        nc, dt, P = self.nc, self.dt, self.P
        st = self.stat
        QT = LT
        NQTILE = T // QT
        NQC = QT // 128

        o = 0
        def carve(n):
            nonlocal o
            v = self.pbf[:, o:o + n]
            o += n
            return v
        reg0 = carve(max(NDK * T, NDK * 1024))
        reg1 = carve(max(NDK * T, NDK * 1024))
        vx = carve(NLC * 8 * VDE)
        E_sb = carve(16 * 512)
        masks = carve(4 * 512)
        ycol = carve(NLC * 128)
        qsb = carve(2 * 1024)
        xnb = [E_sb[:, 0:1024], E_sb[:, 1024:2048]]
        assert o <= PBF_COLS, o
        xnt = reg0
        qT = reg0
        wWb = reg1[:, 0:NDK * 1024]
        kT = reg1
        yT = reg1
        wpT = reg0[:, 0:NDK * 1024]

        of = 0
        def carvef(n):
            nonlocal of
            v = self.pf[:, of:of + n]
            of += n
            return v
        xc = [carvef(D), carvef(D)]
        sq = carvef(D)
        qr = [carvef(D), carvef(D)]
        rot = carvef(D)
        tmp = carvef(512)
        tmp2 = carvef(512)
        cs = [carvef(D), carvef(D)]
        y1 = carvef(NLC * 128)
        sqb = carvef(2 * D)
        grp = carvef(32)
        rstdg = carvef(32)
        rz = carvef(2)
        xc2 = [carvef(D), carvef(D)]
        xnew = [carvef(D), carvef(D)]
        assert of <= PF_COLS, of

        self.op("sync", lambda e: e.dma_start(
            out=masks.rearrange("p (j f) -> p j f", j=4),
            in_=dt["masks"].rearrange("(j p) f -> p j f", p=128)),
            inc=("mks", (ph, "mks"), 16))
        for i in range(4):
            self.op("sync", lambda e, i=i: e.dma_start(
                out=self.lam_sb[i][:], in_=dt["lam"][i:i + 1, :]),
                inc=("lml", (ph, "lam", i), 16))
        self.op("vector", lambda e: e.memset(self.ones_f[:], 1.0),
                inc=("dve", (ph, "onesf"), 1))
        self.op("vector", lambda e: e.memset(
            vx.rearrange("p (lc h c) -> p lc h c", lc=NLC, h=8)[:, :, :, 128:129], 1.0),
            inc=("dve", (ph, "vones"), 1))
        def lam_comp(e):
            e.tensor_tensor(self.lam_t[:], self.lam_sb[0][:], self.lam_sb[1][:], ALU.mult)
            e.reduce_sum(st[0:1, 20:21], self.lam_t[:], axis=AX.X)
            e.tensor_tensor(self.lam_t[:], self.lam_sb[2][:], self.lam_sb[3][:], ALU.mult)
            return e.reduce_sum(st[0:1, 21:22], self.lam_t[:], axis=AX.X)
        self.op("vector", lam_comp, waits=[("lml", (ph, "lam", 3))],
                inc=("dve", (ph, "lamdot"), 1))
        def lam_exp(e):
            e.activation(st[0:1, 22:23], st[0:1, 20:21], AF.Exp)
            return e.activation(st[0:1, 23:24], st[0:1, 21:22], AF.Exp)
        self.op("scalar", lam_exp, waits=[("dve", (ph, "lamdot"))],
                inc=("act", (ph, "lamexp"), 1))
        def lam_full(e):
            e.tensor_tensor(st[0:1, 24:25], st[0:1, 22:23], st[0:1, 23:24], ALU.subtract)
            return e.tensor_scalar(st[0:1, 25:26], st[0:1, 24:25], LAMBDA_INIT, None, ALU.add)
        self.op("vector", lam_full, waits=[("act", (ph, "lamexp"))],
                inc=("dve", (ph, "lamf"), 1))
        lam_ps = P[3][:, 0:1]
        self.op("tensor", lambda e: e.matmul(lam_ps, self.ones_f[:], st[0:1, 25:26],
                                             start=True, stop=True),
                waits=[("dve", (ph, "lamf")), ("dve", (ph, "onesf"))],
                inc=("pe", (ph, "lammm"), 1))
        self.op("vector", lambda e: e.tensor_copy(self.lam_col[:], lam_ps),
                waits=[("pe", (ph, "lammm"))], inc=("dve", (ph, "lamcol"), 1))

        if self.prev_fused:
            self.snt(src, ph, xnt, xc, sq, xnb, pre_normed=True)
        else:
            self.snt(src, ph, xnt, xc, sq, xnb)

        def wprep(wname, dest, key, waits=()):
            for ik in range(NDK):
                self.op("sync", lambda e, wname=wname, ik=ik, dest=dest: e.dma_start(
                    out=dest[:, ik * 1024:(ik + 1) * 1024],
                    in_=dt[wname + "s"][:, ik * 128:(ik + 1) * 128], transpose=True),
                    waits=([self.precast_keys[wname]] + list(waits)) if ik == 0 else [],
                    inc=("tlw" + key, (ph, "tl" + key, ik), 16))
        wprep("bWv", wWb, "wv")

        def bank4(i):
            t = P[(i % 4) // 2]
            return t[:, (i % 2) * 512:(i % 2) * 512 + 512]

        nvt = NLC * 2
        for i in range(nvt):
            lc, it = divmod(i, 2)
            def mmv(e, lc=lc, it=it, i=i):
                ps = bank4(i)
                for dk in range(NDK):
                    inst = e.matmul(ps, xnt[:, dk * T + lc * 128: dk * T + lc * 128 + 128],
                                    wWb[:, dk * 1024 + it * 512: dk * 1024 + it * 512 + 512],
                                    start=(dk == 0), stop=(dk == NDK - 1))
                return inst
            w = []
            if i == 0:
                w = self.w_xnt(ph) + [("tlwwv", (ph, "tlwv", NDK - 1))]
            if i >= 4:
                w.append(("act", (ph, "vev", i - 4)))
            self.op("tensor", mmv, waits=w, inc=("pe", (ph, "vmm", i), 1))
            def vev(e, lc=lc, it=it, i=i):
                dst_v = vx.rearrange("p (lc h c) -> p lc h c", lc=NLC, h=8)[
                    :, lc, it * 4:(it + 1) * 4, 0:128]
                return e.activation(dst_v, bank4(i).rearrange("p (h c) -> p h c", h=4), AF.Copy)
            self.op("scalar", vev, waits=[("pe", (ph, "vmm", i))],
                    inc=("act", (ph, "vev", i), 1))
        wprep("bWk", wWb, "wk", waits=[("pe", (ph, "vmm", nvt - 1))])

        def qk_pipeline(which, spillname, scale_q):
            wkey = "wk" if which == "k" else "wq"
            pend_rstd = []

            def emit_rstd(lc):
                b = lc % 2
                def rmsr(e, b=b, lc=lc):
                    e.activation(rstdg[:, b * 16:b * 16 + 16], grp[:, b * 16:b * 16 + 16],
                                 AF.Ln, bias=self.epst[:, 1:2], scale=1.0 / HD)
                    bias2 = self.epst[:, 2:3] if scale_q else self.epst[:, 3:4]
                    return e.activation(rstdg[:, b * 16:b * 16 + 16],
                                        rstdg[:, b * 16:b * 16 + 16],
                                        AF.Exp, scale=-0.5, bias=bias2)
                w = [("dve", (ph, which, "grp", lc))]
                if lc >= 2:
                    w.append(("pool", (ph, which, "qsb", lc - 2)))
                self.op("scalar", rmsr, waits=w, inc=("act", (ph, which, "rstd", lc), 1))

            for lc in range(NLC):
                b = lc % 2
                for it in range(2):
                    i = lc * 2 + it
                    def mmk(e, lc=lc, it=it, i=i):
                        ps = bank4(i)
                        for dk in range(NDK):
                            inst = e.matmul(ps, xnt[:, dk * T + lc * 128: dk * T + lc * 128 + 128],
                                            wWb[:, dk * 1024 + it * 512: dk * 1024 + it * 512 + 512],
                                            start=(dk == 0), stop=(dk == NDK - 1))
                        return inst
                    w = []
                    if i == 0:
                        w.append(("tlw" + wkey, (ph, "tl" + wkey, NDK - 1)))
                        if which == "k":
                            w.append(("act", (ph, "vev", nvt - 1)))
                        else:
                            w.append(("dve", (ph, "kdone")))
                    if i >= 4:
                        w.append(("dve", (ph, which, "ev", i - 4)))
                    self.op("tensor", mmk, waits=w, inc=("pe", (ph, which, "mm", i), 1))
                    def kev(e, it=it, i=i, b=b):
                        return e.tensor_copy(qr[b][:, it * 512:it * 512 + 512], bank4(i))
                    wkev = [("pe", (ph, which, "mm", i))]
                    if it == 0:
                        if lc >= 2:
                            wkev.append(("pool", (ph, which, "rot", lc - 2)))
                        elif which == "q" and NLC - 2 + lc >= 0:
                            wkev.append(("pool", (ph, "k", "rot", NLC - 2 + lc)))
                    self.op("vector", kev, waits=wkev, inc=("dve", (ph, which, "ev", i), 1))
                self.op("sync", lambda e, lc=lc, b=b: e.dma_start(
                    out=cs[b][:], in_=dt["cs_tab"][lc * 128:(lc + 1) * 128, :]),
                    waits=[("pool", (ph, which, "rot", lc - 2))] if lc >= 2 else
                          ([("pool", (ph, "k", "rot", NLC - 2)), ("pool", (ph, "k", "rot", NLC - 1))] if which == "q" and NLC >= 2 else []),
                    inc=(f"cs{b}", (ph, which, "cs", lc), 16))
                def rmsq(e, b=b):
                    return e.activation(sqb[:, b * 1024:(b + 1) * 1024], qr[b][:], AF.Square)
                wsq = [("dve", (ph, which, "ev", lc * 2 + 1))]
                if lc >= 2:
                    wsq.append(("dve", (ph, which, "grp", lc - 2)))
                self.op("scalar", rmsq, waits=wsq, inc=("act", (ph, which, "sq", lc), 1))
                def rmsg(e, b=b):
                    return e.reduce_sum(grp[:, b * 16:b * 16 + 16],
                                        sqb[:, b * 1024:(b + 1) * 1024]
                                        .rearrange("p (g c) -> p g c", g=16), axis=AX.X)
                self.op("vector", rmsg, waits=[("act", (ph, which, "sq", lc))],
                        inc=("dve", (ph, which, "grp", lc), 1))
                pend_rstd.append(lc)
                if len(pend_rstd) > 1:
                    emit_rstd(pend_rstd.pop(0))
                def rotary(e, b=b):
                    q3 = qr[b].rearrange("p (g d) -> p g d", g=16)
                    r3 = rot.rearrange("p (g d) -> p g d", g=16)
                    t3 = tmp.rearrange("p (g d) -> p g d", g=16)
                    t4 = tmp2.rearrange("p (g d) -> p g d", g=16)
                    c3 = cs[b][:, 0:512].rearrange("p (g d) -> p g d", g=16)
                    s3 = cs[b][:, 512:1024].rearrange("p (g d) -> p g d", g=16)
                    q1, q2 = q3[:, :, 0:32], q3[:, :, 32:64]
                    e.tensor_tensor(r3[:, :, 0:32], q1, c3, ALU.mult)
                    e.tensor_tensor(t3, q2, s3, ALU.mult)
                    e.tensor_tensor(r3[:, :, 0:32], r3[:, :, 0:32], t3, ALU.add)
                    e.tensor_tensor(r3[:, :, 32:64], q2, c3, ALU.mult)
                    e.tensor_tensor(t4, q1, s3, ALU.mult)
                    return e.tensor_tensor(r3[:, :, 32:64], r3[:, :, 32:64], t4, ALU.subtract)
                self.op("gpsimd", rotary,
                        waits=[(f"cs{b}", (ph, which, "cs", lc)),
                               ("dve", (ph, which, "ev", lc * 2 + 1))],
                        inc=("pool", (ph, which, "rot", lc), 1))
                def rscale(e, b=b):
                    r3 = rot.rearrange("p (g d) -> p g d", g=16)
                    o3 = qsb[:, b * 1024:(b + 1) * 1024].rearrange("p (g d) -> p g d", g=16)
                    rb = rstdg[:, b * 16:b * 16 + 16].unsqueeze(2).broadcast_to([128, 16, HD])
                    return e.tensor_tensor(o3, r3, rb, ALU.mult)
                wrs = [("act", (ph, which, "rstd", lc))]
                if lc >= 2:
                    wrs.append((f"sp2{lc % 2}", (ph, which, "sp", lc - 2)))
                elif which == "q" and NLC - 2 + lc >= 0:
                    wrs.append((f"sp2{lc % 2}", (ph, "k", "sp", NLC - 2 + lc)))
                self.op("gpsimd", rscale, waits=wrs, inc=("pool", (ph, which, "qsb", lc), 1))
                self.op("gpsimd", lambda e, lc=lc, b=b, spillname=spillname: e.dma_start(
                    out=dt[spillname][lc * 128:(lc + 1) * 128, :],
                    in_=qsb[:, b * 1024:(b + 1) * 1024]),
                    waits=[("pool", (ph, which, "qsb", lc))],
                    inc=(f"sp2{lc % 2}", (ph, which, "sp", lc), 16))
            while pend_rstd:
                emit_rstd(pend_rstd.pop(0))
            if which == "k":
                self.op("vector", lambda e: e.memset(st[0:1, 26:27], 0.0),
                        inc=("dve", (ph, "kdone"), 1))
        qk_pipeline("k", "kspill", False)
        wprep("bWq", wWb, "wq", waits=[("pe", (ph, "k", "mm", NLC * 2 - 1))])
        qk_pipeline("q", "qspill", True)

        early_ik = [ik for ik in range(NDK) if ik * T >= NDK * 1024]
        for ik in early_ik:
            self.op("sync", lambda e, ik=ik: e.dma_start(
                out=kT[:, ik * T:(ik + 1) * T],
                in_=dt["kspill"][:, ik * 128:(ik + 1) * 128], transpose=True),
                waits=[(f"sp2{(NLC - 1) % 2}", (ph, "k", "sp", NLC - 1)),
                       (f"sp2{(NLC - 2) % 2}", (ph, "k", "sp", NLC - 2))],
                inc=("tlk", (ph, "kT", ik), 16))
        for ik in range(NDK):
            self.op("sync", lambda e, ik=ik: e.dma_start(
                out=qT[:, ik * T:(ik + 1) * T],
                in_=dt["qspill"][:, ik * 128:(ik + 1) * 128], transpose=True),
                waits=[(f"sp2{(NLC - 1) % 2}", (ph, "q", "sp", NLC - 1)),
                       (f"sp2{(NLC - 2) % 2}", (ph, "q", "sp", NLC - 2)),
                       ("pe", (ph, "q", "mm", NLC * 2 - 1))] if ik == 0 else [],
                inc=("tlq", (ph, "qT", ik), 16))
            if ik not in early_ik:
                self.op("sync", lambda e, ik=ik: e.dma_start(
                    out=kT[:, ik * T:(ik + 1) * T],
                    in_=dt["kspill"][:, ik * 128:(ik + 1) * 128], transpose=True),
                    inc=("tlk", (ph, "kT", ik), 16))

        sco_i = 0
        yev_i = 0
        for h in range(NHEAD):
            for m in range(2):
                row = m * 512 + h * HD
                ik, off = row // 128, row % 128
                for qt in range(NQTILE):
                    nkt = (qt + 1) * NQC
                    for kt in range(nkt):
                        ps_s = P[2][:, (sco_i % 2) * 512:(sco_i % 2) * 512 + QT]
                        def mms(e, kt=kt, qt=qt, ik=ik, off=off, ps_s=ps_s):
                            return e.matmul(
                                ps_s,
                                kT[off:off + HD, ik * T + kt * 128: ik * T + kt * 128 + 128],
                                qT[off:off + HD, ik * T + qt * QT: ik * T + qt * QT + QT],
                                start=True, stop=True)
                        w = []
                        if h == 0 and m == 0 and qt == 0 and kt == 0:
                            w = [("tlq", (ph, "qT", NDK - 1)),
                                 ("tlk", (ph, "kT", ([ik for ik in range(NDK) if ik not in ([i_ for i_ in range(NDK) if i_ * T >= NDK * 1024])] or [NDK - 1])[-1])),
                                 ("dve", (ph, "vones")), ("mks", (ph, "mks")),
                                 ("dve", (ph, "lamcol"))]
                        if sco_i >= 2:
                            w.append(("act", (ph, "sev", sco_i - 2)))
                        self.op("tensor", mms, waits=w, inc=("pe", (ph, "smm", sco_i), 1))
                        we = [("pe", (ph, "smm", sco_i))]
                        prev = (h, m, qt - 1) if qt > 0 else ((h, m - 1, NQTILE - 1) if m > 0 else
                                                             ((h - 1, 1, NQTILE - 1) if h > 0 else None))
                        if prev is not None and kt == 0:
                            we.append(("pe", (ph, "ymm") + prev + (NQC - 1,)))
                        def sev(e, kt=kt, ps_s=ps_s):
                            return e.activation(E_sb[:, kt * 512:kt * 512 + QT], ps_s, AF.Exp)
                        self.op("scalar", sev, waits=we, inc=("act", (ph, "sev", sco_i), 1))
                        sco_i += 1
                    def bmask(e, qt=qt):
                        for j in range(NQC):
                            kt = qt * NQC + j
                            sl = E_sb[:, kt * 512:kt * 512 + QT]
                            inst = e.tensor_tensor(sl, sl, masks[:, j * 512:j * 512 + QT], ALU.mult)
                        return inst
                    self.op("vector", bmask, waits=[("act", (ph, "sev", sco_i - 1))],
                            inc=("dve", (ph, "msk", h, m, qt), 1))
                    for qc in range(NQC):
                        gqc = qt * NQC + qc
                        ps_y = P[qc // 2][:, (qc % 2) * 512:(qc % 2) * 512 + VDE]
                        def mmy(e, qt=qt, qc=qc, h=h, ps_y=ps_y):
                            nk = qt * NQC + qc + 1
                            for kt in range(nk):
                                inst = e.matmul(
                                    ps_y,
                                    E_sb[:, kt * 512 + qc * 128: kt * 512 + qc * 128 + 128],
                                    vx[:, kt * 1032 + h * VDE: kt * 1032 + h * VDE + VDE],
                                    start=(kt == 0), stop=(kt == nk - 1))
                            return inst
                        w = [("dve", (ph, "msk", h, m, qt))]
                        if yev_i >= 4:
                            w.append(("dve", (ph, "yev", yev_i - 4)))
                        self.op("tensor", mmy, waits=w,
                                inc=("pe", (ph, "ymm", h, m, qt, qc), 1))
                        if m == 0:
                            def yev(e, gqc=gqc, ps_y=ps_y):
                                e.reciprocal(rz[:, 0:1], ps_y[:, 128:129])
                                return e.tensor_scalar(
                                    y1[:, gqc * 128:(gqc + 1) * 128], ps_y[:, 0:128],
                                    rz[:, 0:1], None, ALU.mult)
                        else:
                            def yev(e, gqc=gqc, ps_y=ps_y):
                                e.reciprocal(rz[:, 0:1], ps_y[:, 128:129])
                                e.tensor_tensor(rz[:, 1:2], rz[:, 0:1], self.lam_col[:], ALU.mult)
                                e.tensor_scalar(tmp[:, 0:128], ps_y[:, 0:128],
                                                rz[:, 1:2], None, ALU.mult)
                                return e.tensor_tensor(
                                    ycol[:, gqc * 128:(gqc + 1) * 128],
                                    y1[:, gqc * 128:(gqc + 1) * 128], tmp[:, 0:128],
                                    ALU.subtract)
                        wyev = [("pe", (ph, "ymm", h, m, qt, qc))]
                        if m == 1 and qt == 0 and qc == 0 and h >= 1:
                            wyev.append(("ysp", (ph, "ysp", h - 1)))
                        self.op("vector", yev, waits=wyev,
                                inc=("dve", (ph, "yev", yev_i), 1))
                        yev_i += 1
            self.op("gpsimd", lambda e, h=h: e.dma_start(
                out=dt["yspill"].rearrange("(lc p) dd -> p lc dd", p=128)[:, :, h * 128:(h + 1) * 128],
                in_=ycol.rearrange("p (lc c) -> p lc c", lc=NLC)),
                waits=[("dve", (ph, "yev", yev_i - 1))], inc=("ysp", (ph, "ysp", h), 16))

        wprep("bWp", wpT, "wp", waits=[("pe", (ph, "smm", sco_i - 1))])
        for ik in range(NDK):
            self.op("sync", lambda e, ik=ik: e.dma_start(
                out=yT[:, ik * T:(ik + 1) * T],
                in_=dt["yspill"][:, ik * 128:(ik + 1) * 128], transpose=True),
                waits=[("ysp", (ph, "ysp", NHEAD - 1)), ("pe", (ph, "smm", sco_i - 1))],
                inc=("tly", (ph, "yT", ik), 16))
        for lc in range(NLC):
            par = lc % 2
            pb = P[par]
            eb = par
            woxc = [("dve", (ph, "eplg", lc - 2))] if lc >= 2 else []
            if src is self.dt["xspill"]:
                woxc.extend(self.xspill_keys.get(lc, ()))
            self.op("sync", lambda e, lc=lc, eb=eb: e.dma_start(
                out=xc2[eb][:], in_=src[lc * 128:(lc + 1) * 128, :]),
                waits=woxc, inc=(f"oxc{eb}", (ph, "oxc", lc), 16))
            def mmo(e, lc=lc, pb=pb):
                for dt_ in range(2):
                    for ik in range(NDK):
                        inst = e.matmul(pb[:, dt_ * 512:dt_ * 512 + 512],
                                        yT[:, ik * T + lc * 128: ik * T + lc * 128 + 128],
                                        wpT[:, ik * 1024 + dt_ * 512: ik * 1024 + dt_ * 512 + 512],
                                        start=(ik == 0), stop=(ik == NDK - 1))
                return inst
            w = []
            if lc == 0:
                w.append(("tly", (ph, "yT", NDK - 1)))
                w.append(("tlwwp", (ph, "tlwp", NDK - 1)))
                w.append(("dve", (ph, "yev", yev_i - 1)))
            if lc >= 2:
                w.append(("dve", (ph, "eplg_t2", lc - 2)))
            self.op("tensor", mmo, waits=w, inc=("pe", (ph, "omm", lc), 1))
            def ep(e, lc=lc, eb=eb, pb=pb):
                return e.scalar_tensor_tensor(xnew[eb][:], pb[:], 1.0, xc2[eb][:],
                                              ALU.mult, ALU.add)
            wep = [("pe", (ph, "omm", lc)), (f"oxc{eb}", (ph, "oxc", lc))]
            if lc >= 2:
                wep.append((f"xw{eb}", (ph, "xnew", lc - 2)))
                if dst is self.dt["xspill"]:
                    wep.append(("act", (ph, "fssq", lc - 2)))
            self.op("vector", ep, waits=wep, inc=("dve", (ph, "eplg_t2", lc), 1))
            self.op("vector", lambda e: e.memset(st[0:1, 27:28], 0.0),
                    inc=("dve", (ph, "eplg", lc), 1))
            self.op("gpsimd", lambda e, lc=lc, eb=eb, dst=dst: e.dma_start(
                out=dst[lc * 128:(lc + 1) * 128, :], in_=xnew[eb][:]),
                waits=[("dve", (ph, "eplg_t2", lc))], inc=(f"xw{eb}", (ph, "xnew", lc), 16))
            self.ep_ln(ph, lc, xnew[eb], xnb, sq, ("dve", (ph, "eplg_t2", lc)), dst)
            if lc == NLC - 1:
                self.set_xspill_totals(ph, dst)
                self.prev_fused = dst is self.dt["xspill"]


def _pack_inputs(inputs, T, ncores=8):
    f32 = np.float32
    x = np.ascontiguousarray(np.asarray(inputs["L0_attn_a_x"], dtype=f32)[:, :T])
    B = x.shape[0]
    assert B == ncores

    com = {}
    com["aWq"] = np.ascontiguousarray(np.asarray(inputs["L0_attn_a_Wq"], f32))
    com["aWk"] = np.ascontiguousarray(np.asarray(inputs["L0_attn_a_Wk"], f32))
    com["aWv"] = np.ascontiguousarray(np.asarray(inputs["L0_attn_a_Wv"], f32))
    com["aWo"] = np.ascontiguousarray(np.asarray(inputs["L0_attn_a_Wo"], f32))
    com["bo_b"] = np.ascontiguousarray(np.broadcast_to(np.asarray(inputs["L0_attn_a_bo"], f32), (128, D)))
    com["gamma_b"] = np.ascontiguousarray(np.broadcast_to(np.asarray(inputs["L0_attn_a_gamma"], f32), (128, D)))
    com["beta_b"] = np.ascontiguousarray(np.broadcast_to(np.asarray(inputs["L0_attn_a_beta"], f32), (128, D)))
    com["bWq"] = np.ascontiguousarray(np.asarray(inputs["L0_attn_b_Wq"], f32))
    com["bWk"] = np.ascontiguousarray(np.asarray(inputs["L0_attn_b_Wk"], f32))
    com["bWv"] = np.ascontiguousarray(np.asarray(inputs["L0_attn_b_Wv"], f32))
    com["bWp"] = np.ascontiguousarray(np.asarray(inputs["L0_attn_b_Wproj"], f32))
    com["lam"] = np.ascontiguousarray(np.stack([
        np.asarray(inputs["L0_attn_b_lambda_q1"], f32),
        np.asarray(inputs["L0_attn_b_lambda_k1"], f32),
        np.asarray(inputs["L0_attn_b_lambda_q2"], f32),
        np.asarray(inputs["L0_attn_b_lambda_k2"], f32)]))
    com["Wr"] = np.ascontiguousarray(np.asarray(inputs["L0_ffn_Wr"], f32))
    com["br_b"] = np.ascontiguousarray(np.broadcast_to(np.asarray(inputs["L0_ffn_br"], f32), (128, NEXP)))
    pad_h = HPAD - H_RAW
    W1 = np.asarray(inputs["L0_ffn_W1"], f32)
    W2 = np.asarray(inputs["L0_ffn_W2"], f32)
    W3 = np.asarray(inputs["L0_ffn_W3"], f32)
    b1 = np.asarray(inputs["L0_ffn_b1"], f32)
    b2 = np.asarray(inputs["L0_ffn_b2"], f32)
    com["W1"] = np.ascontiguousarray(np.pad(W1, ((0, 0), (0, 0), (0, pad_h))))
    com["W2"] = np.ascontiguousarray(np.pad(W2, ((0, 0), (0, 0), (0, pad_h))))
    com["W3"] = np.ascontiguousarray(np.pad(W3, ((0, 0), (0, pad_h), (0, 0))))
    b1p = np.pad(b1, ((0, 0), (0, pad_h)))
    b2p = np.pad(b2, ((0, 0), (0, pad_h)))
    b12 = np.zeros((128, 2 * NEXP * NHK), f32)
    for e in range(NEXP):
        b12[:, (e * 2 + 0) * NHK:(e * 2 + 0) * NHK + NHK] = b1p[e].reshape(NHK, 128).T
        b12[:, (e * 2 + 1) * NHK:(e * 2 + 1) * NHK + NHK] = b2p[e].reshape(NHK, 128).T
    com["b12"] = b12
    com["b3"] = np.ascontiguousarray(np.asarray(inputs["L0_ffn_b3"], f32))

    inv_freq = (1.0 / (10000.0 ** (np.arange(0, HD, 2, dtype=f32) / f32(HD)))).astype(f32)
    tpos = np.arange(T, dtype=f32)
    freqs = np.outer(tpos, inv_freq).astype(f32)
    cos16 = np.tile(np.cos(freqs, dtype=f32), (1, 16))
    sin16 = np.tile(np.sin(freqs, dtype=f32), (1, 16))
    com["cs_tab"] = np.ascontiguousarray(np.concatenate([cos16, sin16], axis=1).astype(f32))
    masks = np.zeros((4, 128, 512), np.float32)
    p = np.arange(128)[:, None]
    fcol = np.arange(512)[None, :]
    for j in range(4):
        masks[j] = (fcol >= p + 128 * j).astype(np.float32)
    com["masks"] = np.ascontiguousarray(masks.reshape(4 * 128, 512).astype(ml_dtypes.bfloat16))

    in_maps = []
    for c in range(ncores):
        m = dict(com)
        m["x"] = np.ascontiguousarray(x[c])
        in_maps.append(m)
    return in_maps


def kernel(**inputs):
    T = 2048
    mk = MK(T=T, n_sub=4)
    nc = mk.build()
    in_maps = _pack_inputs(inputs, T)
    res = run_bass_kernel_spmd(nc, in_maps, core_ids=list(range(8)))
    return np.stack([r["y"] for r in res.results], axis=0)


# version 10
# speedup vs baseline: 1.0535x; 1.0368x over previous
import numpy as np
import ml_dtypes
import concourse.bass as bass
import concourse.mybir as mybir
from concourse.bass_utils import run_bass_kernel_spmd

F32 = mybir.dt.float32
BF16 = mybir.dt.bfloat16
AF = mybir.ActivationFunctionType
ALU = mybir.AluOpType
AX = mybir.AxisListType

D = 1024
NDK = 8
H_RAW = 2730
HPAD = 2816
NHK = 22
NEXP = 8
LN_EPS = 1e-5
RMS_EPS = float(np.finfo(np.float32).eps)
LAMBDA_INIT = 0.8 - 0.6 * float(np.exp(-0.3 * 1))
NHEAD = 8
HD = 64
VD = 128
VDE = 129
PBF_COLS = 65536
PF_COLS = 18944

ENGINES = ("sync", "tensor", "scalar", "vector", "gpsimd")


class Op:
    __slots__ = ("fn", "waits", "inc", "tag")

    def __init__(self, fn, waits, inc, tag):
        self.fn = fn
        self.waits = waits
        self.inc = inc
        self.tag = tag


class MK:

    def __init__(self, T=2048, n_sub=4):
        assert T % 256 == 0
        self.T = T
        self.NLC = T // 128
        self.LT = min(512, T)
        self.NLT = T // self.LT
        self.n_sub = n_sub
        self.ops = {e: [] for e in ENGINES}
        self.nbar = 0

    def op(self, eng, fn, waits=(), inc=None, tag=""):
        w = list(waits)
        pend = getattr(self, "_pending_first_wait", None)
        if pend and eng in pend and fn is not None:
            w = pend.pop(eng) + w
        self.ops[eng].append(Op(fn, w, inc, tag))

    def barrier(self):
        i = self.nbar
        self.nbar += 1
        target = 5 * (i + 1)
        extra = list(self.xspill_keys.get(0, ())) + list(getattr(self, "nspill_keys", []))
        for e in ENGINES:
            ex = extra if e in ("vector", "scalar", "sync") else []
            self.op(e, None, inc=("bar", f"bar{i}_{e}", 1), tag=f"barrier{i}i")
            self.op(e, None, waits=[("bar", ("BARTARGET", target))] + ex,
                    tag=f"barrier{i}w")

    def build(self):
        T, NLC, LT, NLT = self.T, self.NLC, self.LT, self.NLT
        nc = bass.Bass("TRN2", target_bir_lowering=False)
        self.nc = nc

        dt = {}
        dt["x"] = nc.dram_tensor("x", [T, D], F32, kind="ExternalInput").ap()
        for w in ("aWq", "aWk", "aWv", "aWo"):
            dt[w] = nc.dram_tensor(w, [D, D], F32, kind="ExternalInput").ap()
        for v in ("bo_b", "gamma_b", "beta_b"):
            dt[v] = nc.dram_tensor(v, [128, D], F32, kind="ExternalInput").ap()
        for w in ("bWq", "bWk", "bWv", "bWp"):
            dt[w] = nc.dram_tensor(w, [D, D], F32, kind="ExternalInput").ap()
        dt["lam"] = nc.dram_tensor("lam", [4, 32], F32, kind="ExternalInput").ap()
        dt["Wr"] = nc.dram_tensor("Wr", [D, NEXP], F32, kind="ExternalInput").ap()
        dt["br_b"] = nc.dram_tensor("br_b", [128, NEXP], F32, kind="ExternalInput").ap()
        dt["W1"] = nc.dram_tensor("W1", [NEXP, D, HPAD], F32, kind="ExternalInput").ap()
        dt["W2"] = nc.dram_tensor("W2", [NEXP, D, HPAD], F32, kind="ExternalInput").ap()
        dt["W3"] = nc.dram_tensor("W3", [NEXP, HPAD, D], F32, kind="ExternalInput").ap()
        dt["b12"] = nc.dram_tensor("b12", [128, 2 * NEXP * NHK], F32, kind="ExternalInput").ap()
        dt["b3"] = nc.dram_tensor("b3", [NEXP, D], F32, kind="ExternalInput").ap()
        dt["cs_tab"] = nc.dram_tensor("cs_tab", [T, 1024], F32, kind="ExternalInput").ap()
        dt["masks"] = nc.dram_tensor("masks", [4 * 128, 512], BF16, kind="ExternalInput").ap()
        dt["y"] = nc.dram_tensor("y", [T, D], F32, kind="ExternalOutput").ap()
        dt["xspill"] = nc.dram_tensor("xspill", [T, D], F32, kind="Internal").ap()
        dt["nspill"] = nc.dram_tensor("nspill", [T, D], BF16, kind="Internal").ap()
        dt["qspill"] = nc.dram_tensor("qspill", [T, D], BF16, kind="Internal").ap()
        dt["kspill"] = nc.dram_tensor("kspill", [T, D], BF16, kind="Internal").ap()
        dt["yspill"] = nc.dram_tensor("yspill", [T, D], BF16, kind="Internal").ap()
        dt["gspill"] = nc.dram_tensor("gspill", [T, 128], BF16, kind="Internal").ap()
        for w in ("bWq", "bWk", "bWv", "bWp"):
            dt[w + "s"] = nc.dram_tensor(w + "s", [D, D], BF16, kind="Internal").ap()
        self.dt = dt

        self.pbf = nc.alloc_sbuf_tensor("pbf", [128, PBF_COLS], BF16)
        self.pf = nc.alloc_sbuf_tensor("pf", [128, PF_COLS], F32)
        self.ones_bf = nc.alloc_sbuf_tensor("ones_bf", [128, 8], BF16)
        self.ones_f = nc.alloc_sbuf_tensor("ones_f", [1, 128], F32)
        self.means = nc.alloc_sbuf_tensor("means", [128, self.NLC], F32)
        self.rstds = nc.alloc_sbuf_tensor("rstds", [128, self.NLC], F32)
        self.stat = nc.alloc_sbuf_tensor("stat", [128, 64], F32)
        self.b12_sb = nc.alloc_sbuf_tensor("b12_sb", [128, 2 * NEXP * NHK], F32)
        self.br_sb = nc.alloc_sbuf_tensor("br_sb", [128, NEXP], F32)
        self.lam_sb = [nc.alloc_sbuf_tensor(f"lam{i}", [1, 32], F32) for i in range(4)]
        self.lam_t = nc.alloc_sbuf_tensor("lam_t", [1, 32], F32)
        self.lam_col = nc.alloc_sbuf_tensor("lam_col", [128, 1], F32)
        self.epst = nc.alloc_sbuf_tensor("epst", [128, 4], F32)

        self.P = [nc.alloc_psum_tensor(f"P{i}", [128, 1024], F32) for i in range(4)]

        self.sems = {}

        for w in ("bWq", "bWk", "bWv", "bWp"):
            self.op("gpsimd", lambda e, w=w: e.dma_start(
                out=dt[w + "s"][:], in_=dt[w][:]),
                inc=("wbcpre" + w, ("precast", w), 16))
        self.precast_keys = {w: ("wbcpre" + w, ("precast", w))
                             for w in ("bWq", "bWk", "bWv", "bWp")}

        def init_eps(e):
            e.memset(self.epst[:, 0:1], LN_EPS)
            e.memset(self.epst[:, 1:2], RMS_EPS)
            e.memset(self.epst[:, 2:3], float(np.log(0.125)))
            return e.memset(self.epst[:, 3:4], 0.0)
        self.op("vector", init_eps, inc=("dve", ("init",), 1))
        self._pending_first_wait = {"scalar": [("dve", ("init",))]}

        self.xspill_keys = {}
        self.nspill_keys = []
        self.prev_fused = False
        src = dt["x"]
        for sub in range(self.n_sub):
            kind = ("attn_a", "ffn", "attn_b", "ffn")[sub]
            last = (sub == self.n_sub - 1)
            dst = dt["y"] if last else dt["xspill"]
            if kind == "attn_a":
                self.phase_attn_a(src, dst)
            elif kind == "ffn":
                self.phase_ffn(src, dst)
            else:
                self.phase_attn_b(src, dst)
            src = dt["xspill"]
            self.barrier()

        self.op("sync", None, waits=[("xw0", ("FINALSEM", "xw0")), ("xw1", ("FINALSEM", "xw1"))],
                tag="final_store_wait")

        self.finalize()
        return nc

    def finalize(self):
        counters = {}
        keyval = {}
        for e in ENGINES:
            for o in self.ops[e]:
                if o.inc is not None:
                    s, key, n = o.inc
                    counters[s] = counters.get(s, 0) + n
                    assert key not in keyval, f"dup key {key}"
                    keyval[key] = (s, counters[s])

        def resolve(w):
            s, key = w
            if isinstance(key, tuple) and key and key[0] == "BARTARGET":
                return s, key[1]
            if isinstance(key, tuple) and key and key[0] == "FINALSEM":
                return s, counters.get(s, 0)
            assert key in keyval, f"unresolved wait key {key}"
            ks, v = keyval[key]
            assert ks == s, f"wait {key} on sem {s} but inc was on {ks}"
            return s, v

        nc = self.nc
        CHAINED = ("vector", "scalar", "gpsimd")
        for e in CHAINED:
            counters.setdefault("C" + e, 0)
        for s in counters:
            self.sems[s] = nc.semaphore(s).__enter__()

        _EMIT = {
            "tensor_scalar", "tensor_tensor", "tensor_copy", "tensor_add",
            "tensor_sub", "tensor_mul", "tensor_max", "reduce_sum",
            "reduce_max", "tensor_reduce", "reciprocal", "memset", "memzero",
            "scalar_tensor_tensor", "tensor_single_scalar", "select",
            "copy_predicated", "bn_stats", "bn_aggr", "activation", "copy",
            "mul", "add", "square", "sqrt", "sign", "iota", "affine_select",
            "tensor_scalar_mul", "tensor_scalar_add", "tensor_relu",
        }

        class _Chained:
            def __init__(self, eng, sem, holder):
                self._e = eng
                self._sem = sem
                self._h = holder

            def __getattr__(self, name):
                f = getattr(self._e, name)
                if name == "dma_start":
                    def g(*a, **k):
                        if self._h[0]:
                            self._e.wait_ge(self._sem, self._h[0])
                        return f(*a, **k)
                    return g
                if name in _EMIT:
                    def g(*a, **k):
                        if self._h[0]:
                            self._e.wait_ge(self._sem, self._h[0])
                        inst = f(*a, **k)
                        inst.then_inc(self._sem, 1)
                        self._h[0] += 1
                        return inst
                    return g
                return f

        with nc.Block() as block:
            for e in ENGINES:
                ops = self.ops[e]
                chained = e in CHAINED

                def body(eng, ops=ops, chained=chained, e=e):
                    holder = [0]
                    proxy = _Chained(eng, self.sems["C" + e], holder) if chained else eng
                    for o in ops:
                        for w in o.waits:
                            s, v = resolve(w)
                            eng.wait_ge(self.sems[s], v)
                        h0 = holder[0]
                        inst = o.fn(proxy) if o.fn is not None else None
                        if o.inc is not None:
                            s, key, n = o.inc
                            if chained and inst is not None and holder[0] > h0:
                                eng.wait_ge(self.sems["C" + e], holder[0])
                                eng.sem_inc(self.sems[s], n)
                            else:
                                if inst is None:
                                    inst = eng.nop()
                                inst.then_inc(self.sems[s], n)

                getattr(block, e)(body)

    def snt(self, src, ph, xnt, xc, sq, xnb, xc_views=None, pre_normed=False):
        T, NLC = self.T, self.NLC
        dt, st = self.dt, self.stat
        if pre_normed:
            for dk in range(NDK):
                self.op("sync", lambda e, dk=dk: e.dma_start(
                    out=xnt[:, dk * T:(dk + 1) * T],
                    in_=self.dt["nspill"][:, dk * 128:(dk + 1) * 128], transpose=True),
                    waits=list(self.nspill_keys) if dk == 0 else [],
                    inc=("tl", (ph, "xnt", dk), 16), tag=f"snt tl{dk}")
            return
        if xc_views is not None:
            xcv = xc_views
        else:
            xcv = [xc[lc % 2] for lc in range(NLC)]
        for lc in range(NLC):
            b = lc % 2
            if xc_views is None:
                w = [("dve", (ph, "xnb", lc - 2)), ("act", (ph, "ssq", lc - 2))] if lc >= 2 else []
            else:
                w = [(f"xc{b}", (ph, "xc", lc - 2))] if lc >= 2 else []
            if src is self.dt["xspill"]:
                w.extend(self.xspill_keys.get(lc, ()))
            self.op("sync", lambda e, lc=lc, src=src: e.dma_start(
                out=xcv[lc][:], in_=src[lc * 128:(lc + 1) * 128, :]),
                waits=w, inc=(f"xc{b}", (ph, "xc", lc), 16), tag=f"snt xc{lc}")
            self.op("vector", lambda e, lc=lc: e.reduce_sum(
                st[:, 0:1], xcv[lc][:], axis=AX.X),
                waits=[(f"xc{b}", (ph, "xc", lc))], inc=("dve", (ph, "s", lc), 1))
            self.op("scalar", lambda e, lc=lc: e.activation(
                sq[:], xcv[lc][:], AF.Square, accum_out=st[:, 1:2]),
                waits=[(f"xc{b}", (ph, "xc", lc))] + ([("dve", (ph, "var", lc - 1))] if lc >= 1 else []),
                inc=("act", (ph, "ssq", lc), 1))
            self.op("vector", lambda e, lc=lc: e.tensor_scalar(
                self.means[:, lc:lc + 1], st[:, 0:1], 1.0 / D, None, ALU.mult),
                waits=[("dve", (ph, "s", lc))])
            self.op("vector", lambda e: e.tensor_scalar(
                st[:, 2:3], st[:, 1:2], 1.0 / D, None, ALU.mult),
                waits=[("act", (ph, "ssq", lc))])
            self.op("vector", lambda e, lc=lc: e.tensor_tensor(
                st[:, 3:4], self.means[:, lc:lc + 1], self.means[:, lc:lc + 1], ALU.mult))
            self.op("vector", lambda e: e.tensor_tensor(
                st[:, 4:5], st[:, 2:3], st[:, 3:4], ALU.subtract),
                inc=("dve", (ph, "var", lc), 1))
            self.op("scalar", lambda e: e.activation(
                st[:, 5:6], st[:, 4:5], AF.Ln, bias=self.epst[:, 0:1]),
                waits=[("dve", (ph, "var", lc))])
            self.op("scalar", lambda e, lc=lc: e.activation(
                self.rstds[:, lc:lc + 1], st[:, 5:6], AF.Exp, scale=-0.5),
                inc=("act", (ph, "rstd", lc), 1))
            self.op("vector", lambda e, lc=lc, b=b: e.tensor_scalar(
                xnb[b][:], xcv[lc][:], self.means[:, lc:lc + 1],
                self.rstds[:, lc:lc + 1], ALU.subtract, ALU.mult),
                waits=[("act", (ph, "rstd", lc))] + ([(f"nsp{b}", (ph, "nsp", lc - 2))] if lc >= 2 else []),
                inc=("dve", (ph, "xnb", lc), 1))
            self.op("gpsimd", lambda e, lc=lc, b=b: e.dma_start(
                out=self.dt["nspill"][lc * 128:(lc + 1) * 128, :], in_=xnb[b][:]),
                waits=[("dve", (ph, "xnb", lc))], inc=(f"nsp{b}", (ph, "nsp", lc), 16))
        for dk in range(NDK):
            self.op("sync", lambda e, dk=dk: e.dma_start(
                out=xnt[:, dk * T:(dk + 1) * T],
                in_=self.dt["nspill"][:, dk * 128:(dk + 1) * 128], transpose=True),
                waits=[("nsp0", (ph, "nsp", NLC - 2)), ("nsp1", (ph, "nsp", NLC - 1))],
                inc=("tl", (ph, "xnt", dk), 16), tag=f"snt tl{dk}")

    def w_xnt(self, ph):
        return [("tl", (ph, "xnt", NDK - 1))]

    def ep_ln(self, ph, lc, xv, xnb, sq, ready_key, dst):
        if dst is not self.dt["xspill"]:
            return
        st = self.stat
        b = lc % 2
        self.op("vector", lambda e, xv=xv: e.reduce_sum(st[:, 32:33], xv[:], axis=AX.X),
                waits=[ready_key])
        wsq = [ready_key]
        if lc >= 1:
            wsq.append(("dve", (ph, "fvar", lc - 1)))
        self.op("scalar", lambda e, xv=xv, sq=sq: e.activation(
            sq[:], xv[:], AF.Square, accum_out=st[:, 33:34]),
            waits=wsq, inc=("act", (ph, "fssq", lc), 1))
        def fstats(e, lc=lc):
            e.tensor_scalar(self.means[:, lc:lc + 1], st[:, 32:33], 1.0 / D, None, ALU.mult)
            e.tensor_scalar(st[:, 34:35], st[:, 33:34], 1.0 / D, None, ALU.mult)
            e.tensor_tensor(st[:, 35:36], self.means[:, lc:lc + 1],
                            self.means[:, lc:lc + 1], ALU.mult)
            return e.tensor_tensor(st[:, 36:37], st[:, 34:35], st[:, 35:36], ALU.subtract)
        self.op("vector", fstats, waits=[("act", (ph, "fssq", lc))],
                inc=("dve", (ph, "fvar", lc), 1))
        def frstd(e, lc=lc):
            e.activation(st[:, 37:38], st[:, 36:37], AF.Ln, bias=self.epst[:, 0:1])
            return e.activation(self.rstds[:, lc:lc + 1], st[:, 37:38], AF.Exp, scale=-0.5)
        wr = [("dve", (ph, "fvar", lc))]
        if lc >= 1:
            wr.append(("dve", (ph, "fxn", lc - 1)))
        self.op("scalar", frstd, waits=wr, inc=("act", (ph, "frstd", lc), 1))
        wx = [("act", (ph, "frstd", lc))]
        if lc >= 2:
            wx.append((f"nsp{b}", (ph, "fnsp", lc - 2)))
        self.op("vector", lambda e, lc=lc, b=b, xv=xv: e.tensor_scalar(
            xnb[b][:], xv[:], self.means[:, lc:lc + 1],
            self.rstds[:, lc:lc + 1], ALU.subtract, ALU.mult),
            waits=wx, inc=("dve", (ph, "fxn", lc), 1))
        self.op("gpsimd", lambda e, lc=lc, b=b: e.dma_start(
            out=self.dt["nspill"][lc * 128:(lc + 1) * 128, :], in_=xnb[b][:]),
            waits=[("dve", (ph, "fxn", lc))], inc=(f"nsp{b}", (ph, "fnsp", lc), 16))
        if lc == self.NLC - 1:
            self.nspill_keys = [("nsp0", (ph, "fnsp", self.NLC - 2)),
                                ("nsp1", (ph, "fnsp", self.NLC - 1))]

    def set_xspill_totals(self, ph, dst):
        if dst is self.dt["xspill"]:
            NLC = self.NLC
            tot = [(f"xw{(NLC - 2) % 2}", (ph, "xnew", NLC - 2)),
                   (f"xw{(NLC - 1) % 2}", (ph, "xnew", NLC - 1))]
            self.xspill_keys = {lc: tot for lc in range(NLC)}

    def phase_attn_a(self, src, dst):
        T, NLC, LT, NLT = self.T, self.NLC, self.LT, self.NLT
        ph = f"A{self.nbar}"
        nc, dt, P = self.nc, self.dt, self.P
        st = self.stat

        o = 0
        def carve(n):
            nonlocal o
            v = self.pbf[:, o:o + n]
            o += n
            return v
        xnt = carve(NDK * T)
        v_sb = carve(NDK * T)
        eq_sb = v_sb
        ek_sb = carve(NDK * T)
        at_sb = ek_sb
        wW = carve(NDK * 1024)
        ctx_sb = carve(NDK * 1024)
        xnb = [ctx_sb[:, 0:1024], ctx_sb[:, 1024:2048]]
        assert o <= PBF_COLS, o

        of = 0
        def carvef(n):
            nonlocal of
            v = self.pf[:, of:of + n]
            of += n
            return v
        xc = [carvef(D), carvef(D)]
        sq = carvef(D)
        rzk = carvef(NEXP)
        zql = carvef(NLC)
        t2 = [carvef(D), carvef(D)]
        zbuf = [carvef(D), carvef(D)]
        ubuf = [carvef(D), carvef(D)]
        wbuf = carvef(D)
        v2buf = carvef(D)
        xc2 = [carvef(D), carvef(D)]
        xnew = [carvef(D), carvef(D)]
        bo_b = carvef(D)
        gamma_b = carvef(D)
        beta_b = carvef(D)
        assert of <= PF_COLS, of

        self.op("vector", lambda e: e.memset(self.ones_bf[:], 1.0))
        self.op("vector", lambda e: e.memset(self.ones_f[:], 1.0),
                inc=("dve", (ph, "ones"), 1))
        for name, buf in (("bo_b", bo_b), ("gamma_b", gamma_b), ("beta_b", beta_b)):
            self.op("sync", lambda e, name=name, buf=buf: e.dma_start(
                out=buf[:], in_=dt[name][:]), inc=("vec", (ph, name), 16))

        self.snt(src, ph, xnt, xc, sq, xnb)

        def wcast(wname, key, waits=()):
            self.op("gpsimd", lambda e, wname=wname: e.dma_start(
                out=wW.rearrange("p (c e) -> p c e", c=NDK),
                in_=dt[wname].rearrange("(c p) e -> p c e", p=128)),
                waits=list(waits), inc=("wcast", (ph, key), 16), tag=f"wcast {key}")
        wcast("aWv", "wv")

        def bank4(i):
            t = P[(i % 4) // 2]
            return t[:, (i % 2) * 512:(i % 2) * 512 + 512]
        def bank2(i):
            return P[2][:, (i % 2) * 512:(i % 2) * 512 + 512]
        zk_ps = P[3][:, 512:513]
        zq_ps = P[3][:, 0:1]

        nvt = NLC * 2
        for i in range(nvt):
            lc, et = divmod(i, 2)
            def mmv(e, lc=lc, et=et, i=i):
                ps = bank4(i)
                for dk in range(NDK):
                    inst = e.matmul(ps[:], xnt[:, dk * T + lc * 128: dk * T + lc * 128 + 128],
                                    wW[:, dk * 1024 + et * 512: dk * 1024 + et * 512 + 512],
                                    start=(dk == 0), stop=(dk == NDK - 1))
                return inst
            w = []
            if i == 0:
                w = self.w_xnt(ph) + [("wcast", (ph, "wv"))]
            if i >= 4:
                w.append(("act", (ph, "vev", i - 4)))
            self.op("tensor", mmv, waits=w, inc=("pe", (ph, "vmm", i), 1))
            self.op("scalar", lambda e, lc=lc, et=et, i=i: e.activation(
                v_sb[:, lc * 1024 + et * 512: lc * 1024 + et * 512 + 512],
                bank4(i)[:], AF.Copy),
                waits=[("pe", (ph, "vmm", i))], inc=("act", (ph, "vev", i), 1))
        wcast("aWk", "wk", waits=[("pe", (ph, "vmm", nvt - 1))])

        for i in range(nvt):
            lc, et = divmod(i, 2)
            def mmk(e, lc=lc, et=et, i=i):
                ps = bank4(i)
                for dk in range(NDK):
                    inst = e.matmul(ps[:], xnt[:, dk * T + lc * 128: dk * T + lc * 128 + 128],
                                    wW[:, dk * 1024 + et * 512: dk * 1024 + et * 512 + 512],
                                    start=(dk == 0), stop=(dk == NDK - 1))
                return inst
            w = [("wcast", (ph, "wk"))] if i == 0 else []
            if i >= 4:
                w.append(("act", (ph, "kev", i - 4)))
            elif i < 4:
                w.append(("act", (ph, "vev", nvt - 1)) if i == 0 else None)
            w = [x for x in w if x]
            self.op("tensor", mmk, waits=w, inc=("pe", (ph, "kmm", i), 1))
            self.op("scalar", lambda e, lc=lc, et=et, i=i: e.activation(
                ek_sb[:, lc * 1024 + et * 512: lc * 1024 + et * 512 + 512],
                bank4(i)[:], AF.Exp),
                waits=[("pe", (ph, "kmm", i))], inc=("act", (ph, "kev", i), 1))
        wcast("aWq", "wq", waits=[("pe", (ph, "kmm", nvt - 1))])

        for ek in range(NEXP):
            def mmz(e, ek=ek):
                for lc in range(NLC):
                    inst = e.matmul(zk_ps, ek_sb[:, lc * 1024 + ek * 128: lc * 1024 + ek * 128 + 128],
                                    self.ones_bf[:, 0:1], start=(lc == 0), stop=(lc == NLC - 1))
                return inst
            w = [("act", (ph, "kev", nvt - 1)), ("dve", (ph, "ones"))] if ek == 0 else [("dve", (ph, "zkev", ek - 1))]
            self.op("tensor", mmz, waits=w, inc=("pe", (ph, "zk", ek), 1))
            self.op("vector", lambda e, ek=ek: e.tensor_copy(rzk[:, ek:ek + 1], zk_ps),
                    waits=[("pe", (ph, "zk", ek))], inc=("dve", (ph, "zkev", ek), 1))
        self.op("vector", lambda e: e.reciprocal(rzk[:], rzk[:]),
                waits=[("dve", (ph, "zkev", NEXP - 1))], inc=("dve", (ph, "rzk"), 1))

        nct = NDK * 2
        for i in range(nct):
            dk, evt = divmod(i, 2)
            def mmc(e, dk=dk, evt=evt, i=i):
                ps = bank2(i)
                for lc in range(NLC):
                    inst = e.matmul(ps[:], ek_sb[:, lc * 1024 + dk * 128: lc * 1024 + dk * 128 + 128],
                                    v_sb[:, lc * 1024 + evt * 512: lc * 1024 + evt * 512 + 512],
                                    start=(lc == 0), stop=(lc == NLC - 1))
                return inst
            w = [("dve", (ph, "rzk"))] if i == 0 else []
            if i >= 2:
                w.append(("dve", (ph, "cev", i - 2)))
            self.op("tensor", mmc, waits=w, inc=("pe", (ph, "cmm", i), 1))
            self.op("vector", lambda e, dk=dk, evt=evt, i=i: e.tensor_scalar(
                ctx_sb[:, dk * 1024 + evt * 512: dk * 1024 + evt * 512 + 512],
                bank2(i)[:], rzk[:, dk:dk + 1], None, ALU.mult),
                waits=[("pe", (ph, "cmm", i))], inc=("dve", (ph, "cev", i), 1))

        nqt = NDK * NLT
        for i in range(nqt):
            ek, lt = divmod(i, NLT)
            def mmq(e, ek=ek, lt=lt, i=i):
                ps = bank4(i)
                for dk in range(NDK):
                    inst = e.matmul(ps[:, :LT], wW[:, dk * 1024 + ek * 128: dk * 1024 + ek * 128 + 128],
                                    xnt[:, dk * T + lt * LT: dk * T + lt * LT + LT],
                                    start=(dk == 0), stop=(dk == NDK - 1))
                return inst
            w = []
            if i == 0:
                w = [("wcast", (ph, "wq"))]
            if i >= 4:
                w.append(("act", (ph, "qev", i - 4)))
            self.op("tensor", mmq, waits=w, inc=("pe", (ph, "qmm", i), 1))
            wq_ = [("pe", (ph, "qmm", i))]
            if i == 0:
                wq_.append(("pe", (ph, "cmm", nct - 1)))
            self.op("scalar", lambda e, ek=ek, lt=lt, i=i: e.activation(
                eq_sb[:, ek * T + lt * LT: ek * T + lt * LT + LT],
                bank4(i)[:, :LT], AF.Exp),
                waits=wq_, inc=("act", (ph, "qev", i), 1))
        wcast("aWo", "wo", waits=[("pe", (ph, "qmm", nqt - 1))])

        for j in range(NLC):
            def mmzq(e, j=j):
                for ek in range(NEXP):
                    inst = e.matmul(zq_ps, eq_sb[:, ek * T + j * 128: ek * T + j * 128 + 128],
                                    self.ones_bf[:, 0:1], start=(ek == 0), stop=(ek == NEXP - 1))
                return inst
            w = [("dve", (ph, "zqev", j - 1))] if j > 0 else [("act", (ph, "qev", nqt - 1))]
            self.op("tensor", mmzq, waits=w, inc=("pe", (ph, "zq", j), 1))
            self.op("vector", lambda e, j=j: e.tensor_copy(zql[:, j:j + 1], zq_ps),
                    waits=[("pe", (ph, "zq", j))], inc=("dve", (ph, "zqev", j), 1))
        def zq_fin(e):
            e.reciprocal(zql[:], zql[:])
            return e.tensor_scalar(zql[:], zql[:], 1.0 / 32.0, None, ALU.mult)
        self.op("vector", zq_fin, waits=[("dve", (ph, "zqev", NLC - 1))],
                inc=("dve", (ph, "zql"), 1))

        nat = NDK * NLT
        for i in range(nat):
            evk, lt = divmod(i, NLT)
            def mma(e, evk=evk, lt=lt, i=i):
                ps = bank2(i)
                for dk in range(NDK):
                    inst = e.matmul(ps[:, :LT], ctx_sb[:, dk * 1024 + evk * 128: dk * 1024 + evk * 128 + 128],
                                    eq_sb[:, dk * T + lt * LT: dk * T + lt * LT + LT],
                                    start=(dk == 0), stop=(dk == NDK - 1))
                return inst
            w = []
            if i == 0:
                w = [("act", (ph, "qev", nqt - 1)), ("dve", (ph, "cev", nct - 1))]
            if i >= 2:
                w.append(("act", (ph, "aev", i - 2)))
            self.op("tensor", mma, waits=w, inc=("pe", (ph, "amm", i), 1))
            wa_ = [("pe", (ph, "amm", i))]
            self.op("scalar", lambda e, evk=evk, lt=lt, i=i: e.activation(
                at_sb[:, evk * T + lt * LT: evk * T + lt * LT + LT],
                bank2(i)[:, :LT], AF.Copy),
                waits=wa_, inc=("act", (ph, "aev", i), 1))

        for lc in range(NLC):
            par = lc % 2
            pb = P[par]
            eb = par
            woxc = [("pool", (ph, "eplg", lc - 2))] if lc >= 2 else []
            if src is self.dt["xspill"]:
                woxc.extend(self.xspill_keys.get(lc, ()))
            self.op("sync", lambda e, lc=lc, eb=eb: e.dma_start(
                out=xc2[eb][:], in_=src[lc * 128:(lc + 1) * 128, :]),
                waits=woxc, inc=(f"oxc{eb}", (ph, "oxc", lc), 16))
            def mmo(e, lc=lc, pb=pb):
                for dt_ in range(2):
                    for evk in range(NEXP):
                        inst = e.matmul(pb[:, dt_ * 512:dt_ * 512 + 512],
                                        at_sb[:, evk * T + lc * 128: evk * T + lc * 128 + 128],
                                        wW[:, evk * 1024 + dt_ * 512: evk * 1024 + dt_ * 512 + 512],
                                        start=(evk == 0), stop=(evk == NEXP - 1))
                return inst
            w = []
            if lc == 0:
                w.append(("act", (ph, "aev", nat - 1)))
                w.append(("wcast", (ph, "wo")))
            if lc >= 2:
                w.append(("dve", (ph, "eplg_t2", lc - 2)))
            self.op("tensor", mmo, waits=w, inc=("pe", (ph, "omm", lc), 1))
            def ep_t2(e, lc=lc, eb=eb, pb=pb):
                e.tensor_scalar(t2[eb][:], xc2[eb][:], self.means[:, lc:lc + 1],
                                self.rstds[:, lc:lc + 1], ALU.subtract, ALU.mult)
                return e.scalar_tensor_tensor(t2[eb][:], pb[:], zql[:, lc:lc + 1],
                                              t2[eb][:], ALU.mult, ALU.add)
            self.op("vector", ep_t2,
                    waits=[("pe", (ph, "omm", lc)), (f"oxc{eb}", (ph, "oxc", lc))] +
                          ([("dve", (ph, "zql"))] if lc == 0 else []),
                    inc=("dve", (ph, "eplg_t2", lc), 1))
            def ep_z(e, lc=lc, eb=eb):
                return e.tensor_tensor(zbuf[eb][:], t2[eb][:], bo_b[:], ALU.add)
            wz_ = [("dve", (ph, "eplg_t2", lc))]
            if lc == 0:
                wz_.append(("vec", (ph, "beta_b")))
            if lc >= 2:
                wz_.append(("dve", (ph, "u", lc - 2)))
            self.op("gpsimd", ep_z, waits=wz_, inc=("pool", (ph, "z", lc), 1))
            self.op("vector", lambda e, eb=eb: e.reduce_sum(
                st[:, 8:9], zbuf[eb][:], axis=AX.X),
                waits=[("pool", (ph, "z", lc))])
            wsq_ = [("pool", (ph, "z", lc))]
            if lc >= 1:
                wsq_.append(("dve", (ph, "var2", lc - 1)))
            self.op("scalar", lambda e, eb=eb: e.activation(
                sq[:], zbuf[eb][:], AF.Square, accum_out=st[:, 9:10]),
                waits=wsq_, inc=("act", (ph, "zsq", lc), 1))
            def ep_stats(e):
                e.tensor_scalar(st[:, 10:11], st[:, 8:9], 1.0 / D, None, ALU.mult)
                e.tensor_scalar(st[:, 11:12], st[:, 9:10], 1.0 / D, None, ALU.mult)
                e.tensor_tensor(st[:, 12:13], st[:, 10:11], st[:, 10:11], ALU.mult)
                return e.tensor_tensor(st[:, 13:14], st[:, 11:12], st[:, 12:13], ALU.subtract)
            self.op("vector", ep_stats, waits=[("act", (ph, "zsq", lc))],
                    inc=("dve", (ph, "var2", lc), 1))
            def ep_rstd(e):
                e.activation(st[:, 14:15], st[:, 13:14], AF.Ln, bias=self.epst[:, 0:1])
                return e.activation(st[:, 15:16], st[:, 14:15], AF.Exp, scale=-0.5)
            wr_ = [("dve", (ph, "var2", lc))]
            if lc >= 1:
                wr_.append(("dve", (ph, "u", lc - 1)))
            self.op("scalar", ep_rstd, waits=wr_, inc=("act", (ph, "rstd2", lc), 1))
            def ep_u(e, lc=lc, eb=eb):
                return e.tensor_scalar(ubuf[eb][:], zbuf[eb][:], st[:, 10:11], st[:, 15:16],
                                       ALU.subtract, ALU.mult)
            wu_ = [("act", (ph, "rstd2", lc))]
            if lc >= 2:
                wu_.append(("pool", (ph, "eplg", lc - 2)))
            self.op("vector", ep_u, waits=wu_, inc=("dve", (ph, "u", lc), 1))
            def ep_fin(e, lc=lc, eb=eb):
                e.tensor_tensor(wbuf[:], ubuf[eb][:], gamma_b[:], ALU.mult)
                e.tensor_tensor(v2buf[:], xc2[eb][:], beta_b[:], ALU.add)
                return e.tensor_tensor(xnew[eb][:], wbuf[:], v2buf[:], ALU.add)
            w2_ = [("dve", (ph, "u", lc))]
            if lc >= 2:
                w2_.append((f"xw{eb}", (ph, "xnew", lc - 2)))
                if dst is self.dt["xspill"]:
                    w2_.append(("act", (ph, "fssq", lc - 2)))
                    w2_.append(("dve", (ph, "fxn", lc - 2)))
            self.op("gpsimd", ep_fin, waits=w2_, inc=("pool", (ph, "eplg", lc), 1))
            self.op("gpsimd", lambda e, lc=lc, eb=eb, dst=dst: e.dma_start(
                out=dst[lc * 128:(lc + 1) * 128, :], in_=xnew[eb][:]),
                waits=[], inc=(f"xw{eb}", (ph, "xnew", lc), 16))
            self.ep_ln(ph, lc, xnew[eb], xnb, sq, ("pool", (ph, "eplg", lc)), dst)
        self.set_xspill_totals(ph, dst)
        self.prev_fused = dst is self.dt["xspill"]

    def phase_ffn(self, src, dst):
        T, NLC, LT, NLT = self.T, self.NLC, self.LT, self.NLT
        ph = f"F{self.nbar}"
        nc, dt, P = self.nc, self.dt, self.P
        st = self.stat

        slices = [(0, 4), (4, 8), (8, 12), (12, 16), (16, 20), (20, 22)]
        NS = len(slices)
        NSI = NEXP * NS

        o = 0
        def carve(n):
            nonlocal o
            v = self.pbf[:, o:o + n]
            o += n
            return v
        xnt = carve(NDK * T)
        wf = [carve(3 * 4 * 1024) for _ in range(2)]
        hbuf = [carve(4 * T) for _ in range(2)]
        gT = carve(T)
        b3bf = carve(D)
        wr_bf = carve(NDK * 8)
        gbf = carve(128)
        xnb_cols = carve(2 * 1024)
        xnb = [xnb_cols[:, 0:1024], xnb_cols[:, 1024:2048]]
        assert o <= PBF_COLS, o

        of = 0
        def carvef(n):
            nonlocal of
            v = self.pf[:, of:of + n]
            of += n
            return v
        acc = carvef(NLC * 1024)
        s_sc = [carvef(512), carvef(512)]
        sq = self.pf[:, of - 1024:of]
        logits = carvef(NLC * NEXP)
        gates = carvef(NLC * NEXP)
        mask1 = carvef(NLC * NEXP)
        l2 = carvef(NLC * NEXP)
        gm = carvef(8 * NLC)
        assert of <= PF_COLS, of
        m1 = gm[:, 0 * NLC:1 * NLC]
        m2 = gm[:, 1 * NLC:2 * NLC]
        ddt = gm[:, 2 * NLC:3 * NLC]
        e12 = gm[:, 3 * NLC:4 * NLC]
        w1 = gm[:, 4 * NLC:5 * NLC]
        w2 = gm[:, 5 * NLC:6 * NLC]
        w1d = gm[:, 6 * NLC:7 * NLC]

        self.op("sync", lambda e: e.dma_start(out=self.b12_sb[:], in_=dt["b12"][:]),
                inc=("vec", (ph, "b12"), 16))
        self.op("sync", lambda e: e.dma_start(out=self.br_sb[:], in_=dt["br_b"][:]),
                inc=("vec", (ph, "br"), 16))
        if self.prev_fused:
            self.snt(src, ph, xnt, None, sq, xnb, pre_normed=True)
            for lc in range(NLC):
                b = lc % 2
                wacc = [(f"xc{b}", (ph, "xc", lc - 2))] if lc >= 2 else []
                wacc.extend(self.xspill_keys.get(lc, ()))
                self.op("sync", lambda e, lc=lc: e.dma_start(
                    out=acc[:, lc * 1024:(lc + 1) * 1024],
                    in_=src[lc * 128:(lc + 1) * 128, :]),
                    waits=wacc, inc=(f"xc{b}", (ph, "xc", lc), 16))
        else:
            self.snt(src, ph, xnt, None, sq, xnb,
                     xc_views=[acc[:, lc * 1024:(lc + 1) * 1024] for lc in range(NLC)])

        self.op("gpsimd", lambda e: e.dma_start(
            out=wr_bf.rearrange("p (c e) -> p c e", c=NDK),
            in_=dt["Wr"].rearrange("(c p) e -> p c e", p=128)),
            inc=("wcx", (ph, "wr"), 16))
        self.op("gpsimd", lambda e: e.dma_start(out=b3bf[0:NEXP, :], in_=dt["b3"][:]),
                inc=("wcx", (ph, "b3"), 16))

        for lc in range(NLC):
            zps = P[3][:, (lc % 2) * 512:(lc % 2) * 512 + 8]
            def mml(e, lc=lc, zps=zps):
                for dk in range(NDK):
                    inst = e.matmul(zps, xnt[:, dk * T + lc * 128: dk * T + lc * 128 + 128],
                                    wr_bf[:, dk * 8:dk * 8 + 8],
                                    start=(dk == 0), stop=(dk == NDK - 1))
                return inst
            w = []
            if lc == 0:
                w = self.w_xnt(ph) + [("wcx", (ph, "b3"))]
            if lc >= 2:
                w.append(("dve", (ph, "lgev", lc - 2)))
            self.op("tensor", mml, waits=w, inc=("pe", (ph, "lgmm", lc), 1))
            self.op("vector", lambda e, lc=lc, zps=zps: e.tensor_tensor(
                logits[:, lc * 8:lc * 8 + 8], zps, self.br_sb[:], ALU.add),
                waits=[("pe", (ph, "lgmm", lc))] + ([("vec", (ph, "br"))] if lc == 0 else []),
                inc=("dve", (ph, "lgev", lc), 1))
        lg3 = logits.rearrange("p (lc e) -> p lc e", lc=NLC)
        def g_m1(e):
            return e.reduce_max(m1[:], lg3, axis=AX.X)
        self.op("vector", g_m1, waits=[("dve", (ph, "lgev", NLC - 1))])
        def g_mask1(e):
            m1b = m1[:].unsqueeze(2).broadcast_to([128, NLC, 8])
            return e.tensor_tensor(mask1.rearrange("p (lc e) -> p lc e", lc=NLC),
                                   logits.rearrange("p (lc e) -> p lc e", lc=NLC),
                                   m1b, ALU.is_equal)
        self.op("vector", g_mask1)
        def g_l2(e):
            return e.scalar_tensor_tensor(l2[:], mask1[:], -1e30, logits[:], ALU.mult, ALU.add)
        self.op("vector", g_l2)
        def g_m2(e):
            return e.reduce_max(m2[:], l2.rearrange("p (lc e) -> p lc e", lc=NLC), axis=AX.X)
        self.op("vector", g_m2)
        def g_dd(e):
            return e.tensor_tensor(ddt[:], m1[:], m2[:], ALU.subtract)
        self.op("vector", g_dd, inc=("dve", (ph, "gdd"), 1))
        def g_exp(e):
            return e.activation(e12[:], ddt[:], AF.Exp, scale=-1.0)
        self.op("scalar", g_exp, waits=[("dve", (ph, "gdd"))], inc=("act", (ph, "ge"), 1))
        def g_w(e):
            e.tensor_scalar(w1d[:], e12[:], 1.0, None, ALU.add)
            e.reciprocal(w1[:], w1d[:])
            return e.tensor_tensor(w2[:], e12[:], w1[:], ALU.mult)
        self.op("vector", g_w, waits=[("act", (ph, "ge"))])
        def g_gates(e):
            l23 = l2.rearrange("p (lc e) -> p lc e", lc=NLC)
            m13 = mask1.rearrange("p (lc e) -> p lc e", lc=NLC)
            g3 = gates.rearrange("p (lc e) -> p lc e", lc=NLC)
            gt3 = logits.rearrange("p (lc e) -> p lc e", lc=NLC)
            m2b = m2[:].unsqueeze(2).broadcast_to([128, NLC, 8])
            w1b = w1[:].unsqueeze(2).broadcast_to([128, NLC, 8])
            w2b = w2[:].unsqueeze(2).broadcast_to([128, NLC, 8])
            e.tensor_tensor(gt3, l23, m2b, ALU.is_equal)
            e.tensor_tensor(l23, gt3, w2b, ALU.mult)
            e.tensor_tensor(gt3, m13, w1b, ALU.mult)
            return e.tensor_tensor(g3, l23, gt3, ALU.add)
        self.op("vector", g_gates, inc=("dve", (ph, "gates"), 1))
        self.op("vector", lambda e: e.memset(gbf[:, 8:128], 0.0),
                inc=("dve", (ph, "gbf0"), 1))
        for lc in range(NLC):
            w = [("gsp", (ph, "gsp", lc - 1))] if lc >= 1 else [("dve", (ph, "gbf0"))]
            self.op("vector", lambda e, lc=lc: e.tensor_copy(
                gbf[:, 0:8], gates[:, lc * 8:lc * 8 + 8]),
                waits=w, inc=("dve", (ph, "gbf", lc), 1))
            self.op("gpsimd", lambda e, lc=lc: e.dma_start(
                out=dt["gspill"][lc * 128:(lc + 1) * 128, :], in_=gbf[:]),
                waits=[("dve", (ph, "gbf", lc))], inc=("gsp", (ph, "gsp", lc), 16))
        self.op("sync", lambda e: e.dma_start(
            out=gT[:, 0:T], in_=dt["gspill"][:], transpose=True),
            waits=[("gsp", (ph, "gsp", NLC - 1))], inc=("tl", (ph, "gT"), 16))

        def wslice(si):
            e_, s_ = divmod(si, NS)
            return e_, slices[s_]

        def emit_casts(si):
            ex, (h0, h1) = wslice(si)
            nch = h1 - h0
            par = si % 2
            wfb = wf[par]
            w1v = wfb[:, 0:nch * 1024]
            w2v = wfb[:, 4 * 1024:4 * 1024 + nch * 1024]
            w3v = wfb[:, 8 * 1024:8 * 1024 + nch * 1024]
            prev_ntile = ((slices[(si - 2) % NS][1] - slices[(si - 2) % NS][0]) * NLT
                          if si >= 2 else 0)
            w12_w = [("pe", (ph, "hmm", si - 2, prev_ntile - 1))] if si >= 2 else []
            w3_w = [("pe", (ph, "bmm", si - 2, NLC * 2 - 1))] if si >= 2 else []
            self.op("gpsimd", lambda e, ex=ex, h0=h0, nch=nch, w1v=w1v: e.dma_start(
                out=w1v.rearrange("p (c h) -> p c h", c=NDK),
                in_=dt["W1"][ex, :, h0 * 128:(h0 + nch) * 128]
                .rearrange("(c p) h -> p c h", p=128)),
                waits=w12_w, inc=(f"wa{par}", (ph, "wc1", si), 16))
            self.op("gpsimd", lambda e, ex=ex, h0=h0, nch=nch, w2v=w2v: e.dma_start(
                out=w2v.rearrange("p (c h) -> p c h", c=NDK),
                in_=dt["W2"][ex, :, h0 * 128:(h0 + nch) * 128]
                .rearrange("(c p) h -> p c h", p=128)),
                inc=(f"wa{par}", (ph, "wc12", si), 16))
            self.op("gpsimd", lambda e, ex=ex, h0=h0, nch=nch, w3v=w3v: e.dma_start(
                out=w3v.rearrange("p (c dd) -> p c dd", c=nch),
                in_=dt["W3"][ex, h0 * 128:(h0 + nch) * 128, :]
                .rearrange("(c p) dd -> p c dd", p=128)),
                waits=w3_w, inc=(f"wb{par}", (ph, "wc", si), 16))

        emit_casts(0)
        emit_casts(1)
        for si in range(NSI):
            ex, (h0, h1) = wslice(si)
            nch = h1 - h0
            par = si % 2
            wfb = wf[par]
            hb = hbuf[par]
            w1v = wfb[:, 0:nch * 1024]
            w2v = wfb[:, 4 * 1024:4 * 1024 + nch * 1024]
            w3v = wfb[:, 8 * 1024:8 * 1024 + nch * 1024]
            if si >= 2:
                emit_casts(si)

            ntile = nch * NLT
            for t in range(ntile):
                hk, lt = divmod(t, NLT)
                pa = P[t % 2][:, 0:LT]
                pb_ = P[t % 2][:, 512:512 + LT]
                b1col = self.b12_sb[:, (ex * 2 + 0) * NHK + h0 + hk:(ex * 2 + 0) * NHK + h0 + hk + 1]
                b2col = self.b12_sb[:, (ex * 2 + 1) * NHK + h0 + hk:(ex * 2 + 1) * NHK + h0 + hk + 1]
                def mmh(e, hk=hk, lt=lt, pa=pa, pb_=pb_, w1v=w1v, w2v=w2v, nch=nch):
                    cw = nch * 128
                    for dk in range(NDK):
                        e.matmul(pa, w1v[:, dk * cw + hk * 128: dk * cw + hk * 128 + 128],
                                 xnt[:, dk * T + lt * LT: dk * T + lt * LT + LT],
                                 start=(dk == 0), stop=(dk == NDK - 1))
                    for dk in range(NDK):
                        inst = e.matmul(pb_, w2v[:, dk * cw + hk * 128: dk * cw + hk * 128 + 128],
                                        xnt[:, dk * T + lt * LT: dk * T + lt * LT + LT],
                                        start=(dk == 0), stop=(dk == NDK - 1))
                    return inst
                w = []
                if t == 0:
                    w.append((f"wa{par}", (ph, "wc12", si)))
                    if si == 0:
                        w.append(("vec", (ph, "br")))
                    if si >= 2:
                        pass
                if t >= 2:
                    w.append(("dve", (ph, "hev", si, t - 2)))
                    w.append(("act", (ph, "sev", si, t - 2)))
                elif si >= 1:
                    pt = (si, t)
                    prev_nt = (slices[(si - 1) % NS][1] - slices[(si - 1) % NS][0]) * NLT
                    if prev_nt + t - 2 >= 0 and t < 2:
                        w.append(("dve", (ph, "hev", si - 1, prev_nt + t - 2)))
                        w.append(("act", (ph, "sev", si - 1, prev_nt + t - 2)))
                self.op("tensor", mmh, waits=w, inc=("pe", (ph, "hmm", si, t), 1))
                sb = s_sc[t % 2]
                def silu(e, pb_=pb_, sb=sb, b2col=b2col):
                    return e.activation(sb[:, 0:LT], pb_, AF.Silu, bias=b2col)
                self.op("scalar", silu, waits=[("pe", (ph, "hmm", si, t))],
                        inc=("act", (ph, "sev", si, t), 1))
                def hstt(e, hk=hk, lt=lt, pa=pa, sb=sb, b1col=b1col, hb=hb):
                    return e.scalar_tensor_tensor(
                        hb[:, hk * T + lt * LT: hk * T + lt * LT + LT],
                        pa, b1col, sb[:, 0:LT], ALU.add, ALU.mult)
                wh = [("act", (ph, "sev", si, t))]
                if t == 0 and si >= 2:
                    wh.append(("pe", (ph, "bmm", si - 2, NLC * 2 - 1)))
                self.op("vector", hstt, waits=wh, inc=("dve", (ph, "hev", si, t), 1))

            if si >= 1:
                self.emit_ffn_B(ph, si - 1, wslice, wf, hbuf, gates, acc, dst)
        self._ffn_lnbufs = (xnb, sq)
        self.emit_ffn_B(ph, NSI - 1, wslice, wf, hbuf, gates, acc, dst,
                        last_b3=True, gT=gT, b3bf=b3bf)

        self.set_xspill_totals(ph, dst)
        self.prev_fused = dst is self.dt["xspill"]

    def emit_b3(self, ph, lc, gT, b3bf, acc, dst):
        T, NLC, P = self.T, self.NLC, self.P
        for dtile in range(2):
            j = lc * 2 + dtile
            ps = P[j % 2][:, ((j // 2) % 2) * 512:((j // 2) % 2) * 512 + 512]
            def mmb3(e, lc=lc, dtile=dtile, ps=ps):
                return e.matmul(ps, gT[0:NEXP, lc * 128:(lc + 1) * 128],
                                b3bf[0:NEXP, dtile * 512:dtile * 512 + 512],
                                start=True, stop=True)
            w = []
            if j == 0:
                w = [("tl", (ph, "gT")), ("wcx", (ph, "b3"))]
            if j >= 4:
                w.append(("dve", (ph, "b3ev", j - 4)))
            self.op("tensor", mmb3, waits=w, inc=("pe", (ph, "b3mm", j), 1))
            def b3ev(e, lc=lc, dtile=dtile, ps=ps):
                a = acc[:, lc * 1024 + dtile * 512: lc * 1024 + dtile * 512 + 512]
                return e.tensor_tensor(a, a, ps, ALU.add)
            self.op("vector", b3ev, waits=[("pe", (ph, "b3mm", j))],
                    inc=("dve", (ph, "b3ev", j), 1))
        self.op("gpsimd", lambda e, lc=lc, dst=dst: e.dma_start(
            out=dst[lc * 128:(lc + 1) * 128, :], in_=acc[:, lc * 1024:(lc + 1) * 1024]),
            waits=[("dve", (ph, "b3ev", lc * 2 + 1))],
            inc=(f"xw{lc % 2}", (ph, "xnew", lc), 16))
        xnb, sq = self._ffn_lnbufs
        self.ep_ln(ph, lc, acc[:, lc * 1024:(lc + 1) * 1024], xnb, sq,
                   ("dve", (ph, "b3ev", lc * 2 + 1)), dst)

    def emit_ffn_B(self, ph, si, wslice, wf, hbuf, gates, acc, dst,
                   last_b3=False, gT=None, b3bf=None):
        T, NLC = self.T, self.NLC
        P = self.P
        ex, (h0, h1) = wslice(si)
        nch = h1 - h0
        par = si % 2
        w3v = wf[par][:, 8 * 1024:8 * 1024 + nch * 1024]
        hb = hbuf[par]
        ntile = nch * self.NLT
        for j in range(NLC * 2):
            lc, dtile = divmod(j, 2)
            ps = P[2 + (j % 4) // 2][:, (j % 2) * 512:(j % 2) * 512 + 512]
            def mmb(e, lc=lc, dtile=dtile, ps=ps, hb=hb, w3v=w3v, nch=nch):
                for hk in range(nch):
                    inst = e.matmul(ps, hb[:, hk * T + lc * 128: hk * T + lc * 128 + 128],
                                    w3v[:, hk * 1024 + dtile * 512: hk * 1024 + dtile * 512 + 512],
                                    start=(hk == 0), stop=(hk == nch - 1))
                return inst
            w = []
            if j == 0:
                w.append(("dve", (ph, "hev", si, ntile - 1)))
                w.append((f"wb{par}", (ph, "wc", si)))
                if si == 0:
                    w.append(("dve", (ph, "lgev", NLC - 1)))
            if j >= 4:
                w.append(("dve", (ph, "bev", si, j - 4)))
            elif si >= 1:
                w.append(("dve", (ph, "bev", si - 1, NLC * 2 - 4 + j)))
            self.op("tensor", mmb, waits=w, inc=("pe", (ph, "bmm", si, j), 1))
            gcol = gates[:, lc * 8 + ex: lc * 8 + ex + 1]
            def bev(e, lc=lc, dtile=dtile, ps=ps, gcol=gcol):
                a = acc[:, lc * 1024 + dtile * 512: lc * 1024 + dtile * 512 + 512]
                return e.scalar_tensor_tensor(a, ps, gcol, a, ALU.mult, ALU.add)
            wv = [("pe", (ph, "bmm", si, j))]
            if si == 0 and j == 0:
                wv.append((f"xc{(NLC - 1) % 2}", (ph, "xc", NLC - 1)))
                wv.append((f"xc{(NLC - 2) % 2}", (ph, "xc", NLC - 2)))
                wv.append(("dve", (ph, "gates")))
            self.op("vector", bev, waits=wv, inc=("dve", (ph, "bev", si, j), 1))
            if last_b3 and dtile == 1:
                self.emit_b3(ph, lc, gT, b3bf, acc, dst)

    def phase_attn_b(self, src, dst):
        T, NLC, LT, NLT = self.T, self.NLC, self.LT, self.NLT
        ph = f"B{self.nbar}"
        nc, dt, P = self.nc, self.dt, self.P
        st = self.stat
        QT = LT
        NQTILE = T // QT
        NQC = QT // 128

        o = 0
        def carve(n):
            nonlocal o
            v = self.pbf[:, o:o + n]
            o += n
            return v
        reg0 = carve(max(NDK * T, NDK * 1024))
        reg1 = carve(max(NDK * T, NDK * 1024))
        vx = carve(NLC * 8 * VDE)
        E_sb = carve(16 * 512)
        masks = carve(4 * 512)
        ycol = carve(NLC * 128)
        qsb = carve(2 * 1024)
        xnb = [E_sb[:, 0:1024], E_sb[:, 1024:2048]]
        assert o <= PBF_COLS, o
        xnt = reg0
        qT = reg0
        wWb = reg1[:, 0:NDK * 1024]
        kT = reg1
        yT = reg1
        wpT = reg0[:, 0:NDK * 1024]

        of = 0
        def carvef(n):
            nonlocal of
            v = self.pf[:, of:of + n]
            of += n
            return v
        xc = [carvef(D), carvef(D)]
        sq = carvef(D)
        qr = [carvef(D), carvef(D)]
        rot = carvef(D)
        tmp = carvef(512)
        tmp2 = carvef(512)
        cs = [carvef(D), carvef(D)]
        y1 = carvef(NLC * 128)
        sqb = carvef(2 * D)
        grp = carvef(32)
        rstdg = carvef(32)
        rz = carvef(2)
        xc2 = [carvef(D), carvef(D)]
        xnew = [carvef(D), carvef(D)]
        assert of <= PF_COLS, of

        self.op("sync", lambda e: e.dma_start(
            out=masks.rearrange("p (j f) -> p j f", j=4),
            in_=dt["masks"].rearrange("(j p) f -> p j f", p=128)),
            inc=("mks", (ph, "mks"), 16))
        for i in range(4):
            self.op("sync", lambda e, i=i: e.dma_start(
                out=self.lam_sb[i][:], in_=dt["lam"][i:i + 1, :]),
                inc=("lml", (ph, "lam", i), 16))
        self.op("vector", lambda e: e.memset(self.ones_f[:], 1.0),
                inc=("dve", (ph, "onesf"), 1))
        self.op("vector", lambda e: e.memset(
            vx.rearrange("p (lc h c) -> p lc h c", lc=NLC, h=8)[:, :, :, 128:129], 1.0),
            inc=("dve", (ph, "vones"), 1))
        def lam_comp(e):
            e.tensor_tensor(self.lam_t[:], self.lam_sb[0][:], self.lam_sb[1][:], ALU.mult)
            e.reduce_sum(st[0:1, 20:21], self.lam_t[:], axis=AX.X)
            e.tensor_tensor(self.lam_t[:], self.lam_sb[2][:], self.lam_sb[3][:], ALU.mult)
            return e.reduce_sum(st[0:1, 21:22], self.lam_t[:], axis=AX.X)
        self.op("vector", lam_comp, waits=[("lml", (ph, "lam", 3))],
                inc=("dve", (ph, "lamdot"), 1))
        def lam_exp(e):
            e.activation(st[0:1, 22:23], st[0:1, 20:21], AF.Exp)
            return e.activation(st[0:1, 23:24], st[0:1, 21:22], AF.Exp)
        self.op("scalar", lam_exp, waits=[("dve", (ph, "lamdot"))],
                inc=("act", (ph, "lamexp"), 1))
        def lam_full(e):
            e.tensor_tensor(st[0:1, 24:25], st[0:1, 22:23], st[0:1, 23:24], ALU.subtract)
            return e.tensor_scalar(st[0:1, 25:26], st[0:1, 24:25], LAMBDA_INIT, None, ALU.add)
        self.op("vector", lam_full, waits=[("act", (ph, "lamexp"))],
                inc=("dve", (ph, "lamf"), 1))
        lam_ps = P[3][:, 0:1]
        self.op("tensor", lambda e: e.matmul(lam_ps, self.ones_f[:], st[0:1, 25:26],
                                             start=True, stop=True),
                waits=[("dve", (ph, "lamf")), ("dve", (ph, "onesf"))],
                inc=("pe", (ph, "lammm"), 1))
        self.op("vector", lambda e: e.tensor_copy(self.lam_col[:], lam_ps),
                waits=[("pe", (ph, "lammm"))], inc=("dve", (ph, "lamcol"), 1))

        if self.prev_fused:
            self.snt(src, ph, xnt, xc, sq, xnb, pre_normed=True)
        else:
            self.snt(src, ph, xnt, xc, sq, xnb)

        def wprep(wname, dest, key, waits=()):
            for ik in range(NDK):
                self.op("sync", lambda e, wname=wname, ik=ik, dest=dest: e.dma_start(
                    out=dest[:, ik * 1024:(ik + 1) * 1024],
                    in_=dt[wname + "s"][:, ik * 128:(ik + 1) * 128], transpose=True),
                    waits=([self.precast_keys[wname]] + list(waits)) if ik == 0 else [],
                    inc=("tlw" + key, (ph, "tl" + key, ik), 16))
        wprep("bWv", wWb, "wv")

        def bank4(i):
            t = P[(i % 4) // 2]
            return t[:, (i % 2) * 512:(i % 2) * 512 + 512]

        nvt = NLC * 2
        for i in range(nvt):
            lc, it = divmod(i, 2)
            def mmv(e, lc=lc, it=it, i=i):
                ps = bank4(i)
                for dk in range(NDK):
                    inst = e.matmul(ps, xnt[:, dk * T + lc * 128: dk * T + lc * 128 + 128],
                                    wWb[:, dk * 1024 + it * 512: dk * 1024 + it * 512 + 512],
                                    start=(dk == 0), stop=(dk == NDK - 1))
                return inst
            w = []
            if i == 0:
                w = self.w_xnt(ph) + [("tlwwv", (ph, "tlwv", NDK - 1))]
            if i >= 4:
                w.append(("act", (ph, "vev", i - 4)))
            self.op("tensor", mmv, waits=w, inc=("pe", (ph, "vmm", i), 1))
            def vev(e, lc=lc, it=it, i=i):
                dst_v = vx.rearrange("p (lc h c) -> p lc h c", lc=NLC, h=8)[
                    :, lc, it * 4:(it + 1) * 4, 0:128]
                return e.activation(dst_v, bank4(i).rearrange("p (h c) -> p h c", h=4), AF.Copy)
            self.op("scalar", vev, waits=[("pe", (ph, "vmm", i))],
                    inc=("act", (ph, "vev", i), 1))
        wprep("bWk", wWb, "wk", waits=[("pe", (ph, "vmm", nvt - 1))])

        def qk_pipeline(which, spillname, scale_q):
            wkey = "wk" if which == "k" else "wq"
            pend_rstd = []

            def emit_rstd(lc):
                b = lc % 2
                def rmsr(e, b=b, lc=lc):
                    e.activation(rstdg[:, b * 16:b * 16 + 16], grp[:, b * 16:b * 16 + 16],
                                 AF.Ln, bias=self.epst[:, 1:2], scale=1.0 / HD)
                    bias2 = self.epst[:, 2:3] if scale_q else self.epst[:, 3:4]
                    return e.activation(rstdg[:, b * 16:b * 16 + 16],
                                        rstdg[:, b * 16:b * 16 + 16],
                                        AF.Exp, scale=-0.5, bias=bias2)
                w = [("dve", (ph, which, "grp", lc))]
                if lc >= 2:
                    w.append(("pool", (ph, which, "qsb", lc - 2)))
                self.op("scalar", rmsr, waits=w, inc=("act", (ph, which, "rstd", lc), 1))

            for lc in range(NLC):
                b = lc % 2
                for it in range(2):
                    i = lc * 2 + it
                    def mmk(e, lc=lc, it=it, i=i):
                        ps = bank4(i)
                        for dk in range(NDK):
                            inst = e.matmul(ps, xnt[:, dk * T + lc * 128: dk * T + lc * 128 + 128],
                                            wWb[:, dk * 1024 + it * 512: dk * 1024 + it * 512 + 512],
                                            start=(dk == 0), stop=(dk == NDK - 1))
                        return inst
                    w = []
                    if i == 0:
                        w.append(("tlw" + wkey, (ph, "tl" + wkey, NDK - 1)))
                        if which == "k":
                            w.append(("act", (ph, "vev", nvt - 1)))
                        else:
                            w.append(("dve", (ph, "kdone")))
                    if i >= 4:
                        w.append(("dve", (ph, which, "ev", i - 4)))
                    self.op("tensor", mmk, waits=w, inc=("pe", (ph, which, "mm", i), 1))
                    def kev(e, it=it, i=i, b=b):
                        return e.tensor_copy(qr[b][:, it * 512:it * 512 + 512], bank4(i))
                    wkev = [("pe", (ph, which, "mm", i))]
                    if it == 0:
                        if lc >= 2:
                            wkev.append(("pool", (ph, which, "rot", lc - 2)))
                        elif which == "q" and NLC - 2 + lc >= 0:
                            wkev.append(("pool", (ph, "k", "rot", NLC - 2 + lc)))
                    self.op("vector", kev, waits=wkev, inc=("dve", (ph, which, "ev", i), 1))
                self.op("sync", lambda e, lc=lc, b=b: e.dma_start(
                    out=cs[b][:], in_=dt["cs_tab"][lc * 128:(lc + 1) * 128, :]),
                    waits=[("pool", (ph, which, "rot", lc - 2))] if lc >= 2 else
                          ([("pool", (ph, "k", "rot", NLC - 2)), ("pool", (ph, "k", "rot", NLC - 1))] if which == "q" and NLC >= 2 else []),
                    inc=(f"cs{b}", (ph, which, "cs", lc), 16))
                def rmsq(e, b=b):
                    return e.activation(sqb[:, b * 1024:(b + 1) * 1024], qr[b][:], AF.Square)
                wsq = [("dve", (ph, which, "ev", lc * 2 + 1))]
                if lc >= 2:
                    wsq.append(("dve", (ph, which, "grp", lc - 2)))
                self.op("scalar", rmsq, waits=wsq, inc=("act", (ph, which, "sq", lc), 1))
                def rmsg(e, b=b):
                    return e.reduce_sum(grp[:, b * 16:b * 16 + 16],
                                        sqb[:, b * 1024:(b + 1) * 1024]
                                        .rearrange("p (g c) -> p g c", g=16), axis=AX.X)
                self.op("vector", rmsg, waits=[("act", (ph, which, "sq", lc))],
                        inc=("dve", (ph, which, "grp", lc), 1))
                pend_rstd.append(lc)
                if len(pend_rstd) > 1:
                    emit_rstd(pend_rstd.pop(0))
                def rotary(e, b=b):
                    q3 = qr[b].rearrange("p (g d) -> p g d", g=16)
                    r3 = rot.rearrange("p (g d) -> p g d", g=16)
                    t3 = tmp.rearrange("p (g d) -> p g d", g=16)
                    t4 = tmp2.rearrange("p (g d) -> p g d", g=16)
                    c3 = cs[b][:, 0:512].rearrange("p (g d) -> p g d", g=16)
                    s3 = cs[b][:, 512:1024].rearrange("p (g d) -> p g d", g=16)
                    q1, q2 = q3[:, :, 0:32], q3[:, :, 32:64]
                    e.tensor_tensor(r3[:, :, 0:32], q1, c3, ALU.mult)
                    e.tensor_tensor(t3, q2, s3, ALU.mult)
                    e.tensor_tensor(r3[:, :, 0:32], r3[:, :, 0:32], t3, ALU.add)
                    e.tensor_tensor(r3[:, :, 32:64], q2, c3, ALU.mult)
                    e.tensor_tensor(t4, q1, s3, ALU.mult)
                    return e.tensor_tensor(r3[:, :, 32:64], r3[:, :, 32:64], t4, ALU.subtract)
                self.op("gpsimd", rotary,
                        waits=[(f"cs{b}", (ph, which, "cs", lc)),
                               ("dve", (ph, which, "ev", lc * 2 + 1))],
                        inc=("pool", (ph, which, "rot", lc), 1))
                def rscale(e, b=b):
                    r3 = rot.rearrange("p (g d) -> p g d", g=16)
                    o3 = qsb[:, b * 1024:(b + 1) * 1024].rearrange("p (g d) -> p g d", g=16)
                    rb = rstdg[:, b * 16:b * 16 + 16].unsqueeze(2).broadcast_to([128, 16, HD])
                    return e.tensor_tensor(o3, r3, rb, ALU.mult)
                wrs = [("act", (ph, which, "rstd", lc))]
                if lc >= 2:
                    wrs.append((f"sp2{lc % 2}", (ph, which, "sp", lc - 2)))
                elif which == "q" and NLC - 2 + lc >= 0:
                    wrs.append((f"sp2{lc % 2}", (ph, "k", "sp", NLC - 2 + lc)))
                self.op("gpsimd", rscale, waits=wrs, inc=("pool", (ph, which, "qsb", lc), 1))
                self.op("gpsimd", lambda e, lc=lc, b=b, spillname=spillname: e.dma_start(
                    out=dt[spillname][lc * 128:(lc + 1) * 128, :],
                    in_=qsb[:, b * 1024:(b + 1) * 1024]),
                    waits=[("pool", (ph, which, "qsb", lc))],
                    inc=(f"sp2{lc % 2}", (ph, which, "sp", lc), 16))
            while pend_rstd:
                emit_rstd(pend_rstd.pop(0))
            if which == "k":
                self.op("vector", lambda e: e.memset(st[0:1, 26:27], 0.0),
                        inc=("dve", (ph, "kdone"), 1))
        qk_pipeline("k", "kspill", False)
        wprep("bWq", wWb, "wq", waits=[("pe", (ph, "k", "mm", NLC * 2 - 1))])
        qk_pipeline("q", "qspill", True)

        early_ik = [ik for ik in range(NDK) if ik * T >= NDK * 1024]
        for ik in early_ik:
            self.op("sync", lambda e, ik=ik: e.dma_start(
                out=kT[:, ik * T:(ik + 1) * T],
                in_=dt["kspill"][:, ik * 128:(ik + 1) * 128], transpose=True),
                waits=[(f"sp2{(NLC - 1) % 2}", (ph, "k", "sp", NLC - 1)),
                       (f"sp2{(NLC - 2) % 2}", (ph, "k", "sp", NLC - 2))],
                inc=("tlk", (ph, "kT", ik), 16))
        for ik in range(NDK):
            self.op("sync", lambda e, ik=ik: e.dma_start(
                out=qT[:, ik * T:(ik + 1) * T],
                in_=dt["qspill"][:, ik * 128:(ik + 1) * 128], transpose=True),
                waits=[(f"sp2{(NLC - 1) % 2}", (ph, "q", "sp", NLC - 1)),
                       (f"sp2{(NLC - 2) % 2}", (ph, "q", "sp", NLC - 2)),
                       ("pe", (ph, "q", "mm", NLC * 2 - 1))] if ik == 0 else [],
                inc=("tlq", (ph, "qT", ik), 16))
            if ik not in early_ik:
                self.op("sync", lambda e, ik=ik: e.dma_start(
                    out=kT[:, ik * T:(ik + 1) * T],
                    in_=dt["kspill"][:, ik * 128:(ik + 1) * 128], transpose=True),
                    inc=("tlk", (ph, "kT", ik), 16))

        sco_i = 0
        yev_i = 0
        for h in range(NHEAD):
            for m in range(2):
                row = m * 512 + h * HD
                ik, off = row // 128, row % 128
                for qt in range(NQTILE):
                    nkt = (qt + 1) * NQC
                    for kt0 in range(0, nkt, 2):
                        npair = min(2, nkt - kt0)
                        pp = P[2 + (sco_i % 2)]
                        def mms(e, kt0=kt0, npair=npair, qt=qt, ik=ik, off=off, pp=pp):
                            for kk in range(npair):
                                inst = e.matmul(
                                    pp[:, kk * 512:kk * 512 + QT],
                                    kT[off:off + HD, ik * T + (kt0 + kk) * 128: ik * T + (kt0 + kk) * 128 + 128],
                                    qT[off:off + HD, ik * T + qt * QT: ik * T + qt * QT + QT],
                                    start=True, stop=True)
                            return inst
                        w = []
                        if h == 0 and m == 0 and qt == 0 and kt0 == 0:
                            w = [("tlq", (ph, "qT", NDK - 1)),
                                 ("tlk", (ph, "kT", ([ik for ik in range(NDK) if ik not in ([i_ for i_ in range(NDK) if i_ * T >= NDK * 1024])] or [NDK - 1])[-1])),
                                 ("dve", (ph, "vones")), ("mks", (ph, "mks")),
                                 ("dve", (ph, "lamcol"))]
                        if sco_i >= 2:
                            w.append(("act", (ph, "sev", sco_i - 2)))
                        self.op("tensor", mms, waits=w, inc=("pe", (ph, "smm", sco_i), 1))
                        we = [("pe", (ph, "smm", sco_i))]
                        prev = (h, m, qt - 1) if qt > 0 else ((h, m - 1, NQTILE - 1) if m > 0 else
                                                             ((h - 1, 1, NQTILE - 1) if h > 0 else None))
                        if prev is not None and kt0 == 0:
                            we.append(("pe", (ph, "ymm") + prev + (NQC - 1,)))
                        def sev(e, kt0=kt0, npair=npair, pp=pp):
                            if npair == 2 and QT == 512:
                                return e.activation(E_sb[:, kt0 * 512:kt0 * 512 + 1024],
                                                    pp[:, 0:1024], AF.Exp)
                            inst = None
                            for kk in range(npair):
                                inst = e.activation(E_sb[:, (kt0 + kk) * 512:(kt0 + kk) * 512 + QT],
                                                    pp[:, kk * 512:kk * 512 + QT], AF.Exp)
                            return inst
                        self.op("scalar", sev, waits=we, inc=("act", (ph, "sev", sco_i), 1))
                        sco_i += 1
                    def bmask(e, qt=qt):
                        for j in range(NQC):
                            kt = qt * NQC + j
                            sl = E_sb[:, kt * 512:kt * 512 + QT]
                            inst = e.tensor_tensor(sl, sl, masks[:, j * 512:j * 512 + QT], ALU.mult)
                        return inst
                    self.op("vector", bmask, waits=[("act", (ph, "sev", sco_i - 1))],
                            inc=("dve", (ph, "msk", h, m, qt), 1))
                    for qc in range(NQC):
                        gqc = qt * NQC + qc
                        ps_y = P[qc // 2][:, (qc % 2) * 512:(qc % 2) * 512 + VDE]
                        def mmy(e, qt=qt, qc=qc, h=h, ps_y=ps_y):
                            nk = qt * NQC + qc + 1
                            for kt in range(nk):
                                inst = e.matmul(
                                    ps_y,
                                    E_sb[:, kt * 512 + qc * 128: kt * 512 + qc * 128 + 128],
                                    vx[:, kt * 1032 + h * VDE: kt * 1032 + h * VDE + VDE],
                                    start=(kt == 0), stop=(kt == nk - 1))
                            return inst
                        w = [("dve", (ph, "msk", h, m, qt))]
                        if yev_i >= 4:
                            w.append(("dve", (ph, "yev", yev_i - 4)))
                        self.op("tensor", mmy, waits=w,
                                inc=("pe", (ph, "ymm", h, m, qt, qc), 1))
                        if m == 0:
                            def yev(e, gqc=gqc, ps_y=ps_y):
                                e.reciprocal(rz[:, 0:1], ps_y[:, 128:129])
                                return e.tensor_scalar(
                                    y1[:, gqc * 128:(gqc + 1) * 128], ps_y[:, 0:128],
                                    rz[:, 0:1], None, ALU.mult)
                        else:
                            def yev(e, gqc=gqc, ps_y=ps_y):
                                e.reciprocal(rz[:, 0:1], ps_y[:, 128:129])
                                e.tensor_tensor(rz[:, 1:2], rz[:, 0:1], self.lam_col[:], ALU.mult)
                                e.tensor_scalar(tmp[:, 0:128], ps_y[:, 0:128],
                                                rz[:, 1:2], None, ALU.mult)
                                return e.tensor_tensor(
                                    ycol[:, gqc * 128:(gqc + 1) * 128],
                                    y1[:, gqc * 128:(gqc + 1) * 128], tmp[:, 0:128],
                                    ALU.subtract)
                        wyev = [("pe", (ph, "ymm", h, m, qt, qc))]
                        if m == 1 and qt == 0 and qc == 0 and h >= 1:
                            wyev.append(("ysp", (ph, "ysp", h - 1)))
                        self.op("vector", yev, waits=wyev,
                                inc=("dve", (ph, "yev", yev_i), 1))
                        yev_i += 1
            self.op("gpsimd", lambda e, h=h: e.dma_start(
                out=dt["yspill"].rearrange("(lc p) dd -> p lc dd", p=128)[:, :, h * 128:(h + 1) * 128],
                in_=ycol.rearrange("p (lc c) -> p lc c", lc=NLC)),
                waits=[("dve", (ph, "yev", yev_i - 1))], inc=("ysp", (ph, "ysp", h), 16))

        wprep("bWp", wpT, "wp", waits=[("pe", (ph, "smm", sco_i - 1))])
        for ik in range(NDK):
            self.op("sync", lambda e, ik=ik: e.dma_start(
                out=yT[:, ik * T:(ik + 1) * T],
                in_=dt["yspill"][:, ik * 128:(ik + 1) * 128], transpose=True),
                waits=[("ysp", (ph, "ysp", NHEAD - 1)), ("pe", (ph, "smm", sco_i - 1))],
                inc=("tly", (ph, "yT", ik), 16))
        for lc in range(NLC):
            par = lc % 2
            pb = P[par]
            eb = par
            woxc = [("dve", (ph, "eplg", lc - 2))] if lc >= 2 else []
            if src is self.dt["xspill"]:
                woxc.extend(self.xspill_keys.get(lc, ()))
            self.op("sync", lambda e, lc=lc, eb=eb: e.dma_start(
                out=xc2[eb][:], in_=src[lc * 128:(lc + 1) * 128, :]),
                waits=woxc, inc=(f"oxc{eb}", (ph, "oxc", lc), 16))
            def mmo(e, lc=lc, pb=pb):
                for dt_ in range(2):
                    for ik in range(NDK):
                        inst = e.matmul(pb[:, dt_ * 512:dt_ * 512 + 512],
                                        yT[:, ik * T + lc * 128: ik * T + lc * 128 + 128],
                                        wpT[:, ik * 1024 + dt_ * 512: ik * 1024 + dt_ * 512 + 512],
                                        start=(ik == 0), stop=(ik == NDK - 1))
                return inst
            w = []
            if lc == 0:
                w.append(("tly", (ph, "yT", NDK - 1)))
                w.append(("tlwwp", (ph, "tlwp", NDK - 1)))
                w.append(("dve", (ph, "yev", yev_i - 1)))
            if lc >= 2:
                w.append(("dve", (ph, "eplg_t2", lc - 2)))
            self.op("tensor", mmo, waits=w, inc=("pe", (ph, "omm", lc), 1))
            def ep(e, lc=lc, eb=eb, pb=pb):
                return e.scalar_tensor_tensor(xnew[eb][:], pb[:], 1.0, xc2[eb][:],
                                              ALU.mult, ALU.add)
            wep = [("pe", (ph, "omm", lc)), (f"oxc{eb}", (ph, "oxc", lc))]
            if lc >= 2:
                wep.append((f"xw{eb}", (ph, "xnew", lc - 2)))
                if dst is self.dt["xspill"]:
                    wep.append(("act", (ph, "fssq", lc - 2)))
            self.op("vector", ep, waits=wep, inc=("dve", (ph, "eplg_t2", lc), 1))
            self.op("vector", lambda e: e.memset(st[0:1, 27:28], 0.0),
                    inc=("dve", (ph, "eplg", lc), 1))
            self.op("gpsimd", lambda e, lc=lc, eb=eb, dst=dst: e.dma_start(
                out=dst[lc * 128:(lc + 1) * 128, :], in_=xnew[eb][:]),
                waits=[("dve", (ph, "eplg_t2", lc))], inc=(f"xw{eb}", (ph, "xnew", lc), 16))
            self.ep_ln(ph, lc, xnew[eb], xnb, sq, ("dve", (ph, "eplg_t2", lc)), dst)
            if lc == NLC - 1:
                self.set_xspill_totals(ph, dst)
                self.prev_fused = dst is self.dt["xspill"]


def _pack_inputs(inputs, T, ncores=8):
    f32 = np.float32
    x = np.ascontiguousarray(np.asarray(inputs["L0_attn_a_x"], dtype=f32)[:, :T])
    B = x.shape[0]
    assert B == ncores

    com = {}
    com["aWq"] = np.ascontiguousarray(np.asarray(inputs["L0_attn_a_Wq"], f32))
    com["aWk"] = np.ascontiguousarray(np.asarray(inputs["L0_attn_a_Wk"], f32))
    com["aWv"] = np.ascontiguousarray(np.asarray(inputs["L0_attn_a_Wv"], f32))
    com["aWo"] = np.ascontiguousarray(np.asarray(inputs["L0_attn_a_Wo"], f32))
    com["bo_b"] = np.ascontiguousarray(np.broadcast_to(np.asarray(inputs["L0_attn_a_bo"], f32), (128, D)))
    com["gamma_b"] = np.ascontiguousarray(np.broadcast_to(np.asarray(inputs["L0_attn_a_gamma"], f32), (128, D)))
    com["beta_b"] = np.ascontiguousarray(np.broadcast_to(np.asarray(inputs["L0_attn_a_beta"], f32), (128, D)))
    com["bWq"] = np.ascontiguousarray(np.asarray(inputs["L0_attn_b_Wq"], f32))
    com["bWk"] = np.ascontiguousarray(np.asarray(inputs["L0_attn_b_Wk"], f32))
    com["bWv"] = np.ascontiguousarray(np.asarray(inputs["L0_attn_b_Wv"], f32))
    com["bWp"] = np.ascontiguousarray(np.asarray(inputs["L0_attn_b_Wproj"], f32))
    com["lam"] = np.ascontiguousarray(np.stack([
        np.asarray(inputs["L0_attn_b_lambda_q1"], f32),
        np.asarray(inputs["L0_attn_b_lambda_k1"], f32),
        np.asarray(inputs["L0_attn_b_lambda_q2"], f32),
        np.asarray(inputs["L0_attn_b_lambda_k2"], f32)]))
    com["Wr"] = np.ascontiguousarray(np.asarray(inputs["L0_ffn_Wr"], f32))
    com["br_b"] = np.ascontiguousarray(np.broadcast_to(np.asarray(inputs["L0_ffn_br"], f32), (128, NEXP)))
    pad_h = HPAD - H_RAW
    W1 = np.asarray(inputs["L0_ffn_W1"], f32)
    W2 = np.asarray(inputs["L0_ffn_W2"], f32)
    W3 = np.asarray(inputs["L0_ffn_W3"], f32)
    b1 = np.asarray(inputs["L0_ffn_b1"], f32)
    b2 = np.asarray(inputs["L0_ffn_b2"], f32)
    com["W1"] = np.ascontiguousarray(np.pad(W1, ((0, 0), (0, 0), (0, pad_h))))
    com["W2"] = np.ascontiguousarray(np.pad(W2, ((0, 0), (0, 0), (0, pad_h))))
    com["W3"] = np.ascontiguousarray(np.pad(W3, ((0, 0), (0, pad_h), (0, 0))))
    b1p = np.pad(b1, ((0, 0), (0, pad_h)))
    b2p = np.pad(b2, ((0, 0), (0, pad_h)))
    b12 = np.zeros((128, 2 * NEXP * NHK), f32)
    for e in range(NEXP):
        b12[:, (e * 2 + 0) * NHK:(e * 2 + 0) * NHK + NHK] = b1p[e].reshape(NHK, 128).T
        b12[:, (e * 2 + 1) * NHK:(e * 2 + 1) * NHK + NHK] = b2p[e].reshape(NHK, 128).T
    com["b12"] = b12
    com["b3"] = np.ascontiguousarray(np.asarray(inputs["L0_ffn_b3"], f32))

    inv_freq = (1.0 / (10000.0 ** (np.arange(0, HD, 2, dtype=f32) / f32(HD)))).astype(f32)
    tpos = np.arange(T, dtype=f32)
    freqs = np.outer(tpos, inv_freq).astype(f32)
    cos16 = np.tile(np.cos(freqs, dtype=f32), (1, 16))
    sin16 = np.tile(np.sin(freqs, dtype=f32), (1, 16))
    com["cs_tab"] = np.ascontiguousarray(np.concatenate([cos16, sin16], axis=1).astype(f32))
    masks = np.zeros((4, 128, 512), np.float32)
    p = np.arange(128)[:, None]
    fcol = np.arange(512)[None, :]
    for j in range(4):
        masks[j] = (fcol >= p + 128 * j).astype(np.float32)
    com["masks"] = np.ascontiguousarray(masks.reshape(4 * 128, 512).astype(ml_dtypes.bfloat16))

    in_maps = []
    for c in range(ncores):
        m = dict(com)
        m["x"] = np.ascontiguousarray(x[c])
        in_maps.append(m)
    return in_maps


def kernel(**inputs):
    T = 2048
    mk = MK(T=T, n_sub=4)
    nc = mk.build()
    in_maps = _pack_inputs(inputs, T)
    res = run_bass_kernel_spmd(nc, in_maps, core_ids=list(range(8)))
    return np.stack([r["y"] for r in res.results], axis=0)
